# version 07
# baseline (speedup 1.0000x reference)
import math
import numpy as np
import ml_dtypes

import concourse.bass as bass
import concourse.mybir as mybir
from concourse.bass_utils import run_bass_kernel_spmd

F32 = mybir.dt.float32
BF16 = mybir.dt.bfloat16
I32 = mybir.dt.int32
AF = mybir.ActivationFunctionType
ALU = mybir.AluOpType
AX = mybir.AxisListType

B = 2
T = 2048
C = 1024
H = 16
D = 64
NCC = 8
NTT = 16
SEG = 512
ISQRT8 = float(2.0 ** -1.5)
EPS = 1e-5
NEG = -1.0e30
MAGIC = 0x5F3759DF

def _cut_table():
    cuts = []
    for h in range(H):
        sl = 2.0 ** (-(h + 1) / 2.0)
        c = 1 + math.ceil(110.0 / (128.0 * sl))
        cuts.append(min(16, c))
    return cuts

CUT = _cut_table()


def _slopes_f32():
    start = 2.0 ** (-(2.0 ** (-(math.log2(H) - 3))))
    return [np.float32(start * (start ** i)) for i in range(H)]


def host_consts():
    bf = ml_dtypes.bfloat16
    slopes = _slopes_f32()
    pos = np.arange(T)
    qf = (pos % 128).astype(np.float32)
    qh = ((pos // 128) * 128).astype(np.float32)
    kp = qf
    qaug = np.zeros((H, 7, T), dtype=bf)
    kaug = np.zeros((H, 7, T), dtype=bf)
    for h in range(H):
        s = slopes[h]
        s_hi = np.float32(bf(s))
        s_lo = np.float32(bf(np.float32(s - s_hi)))
        qaug[h, 0, :] = bf(0.0)
        qaug[h, 1, :] = bf(s_hi)
        qaug[h, 2, :] = bf(s_lo)
        qaug[h, 3, :] = qf.astype(bf)
        qaug[h, 4, :] = qf.astype(bf)
        qaug[h, 5, :] = qh.astype(bf)
        qaug[h, 6, :] = qh.astype(bf)
        kaug[h, 0, :] = bf(1.0)
        kaug[h, 1, :] = kp.astype(bf)
        kaug[h, 2, :] = kp.astype(bf)
        kaug[h, 3, :] = bf(-s_hi)
        kaug[h, 4, :] = bf(-s_lo)
        kaug[h, 5, :] = bf(-s_hi)
        kaug[h, 6, :] = bf(-s_lo)
    kk = np.arange(128)[:, None]
    qq = np.arange(128)[None, :]
    mask = np.where(kk > qq, np.float32(NEG), np.float32(0.0)).astype(np.float32)
    ident = np.eye(128, dtype=np.float32).astype(bf)
    biasc = np.zeros((128, 272), dtype=np.float32)
    for h in range(H):
        se = eff_slope(h)
        for j in range(16):
            biasc[:, h * 16 + j] = np.float32(se * 128.0 * j)
    biasc[:, 256] = np.float32(0.5)
    biasc[:, 257] = np.float32(np.float32(0.5) ** np.float32(0.5))
    return qaug, kaug, mask, ident, biasc, slopes


def eff_slope(h):
    bf = ml_dtypes.bfloat16
    s = _slopes_f32()[h]
    s_hi = np.float32(bf(s))
    s_lo = np.float32(bf(np.float32(s - s_hi)))
    return float(np.float32(s_hi + s_lo))


class Chan:
    def __init__(self, sem, step):
        self.sem = sem
        self.n = 0
        self.step = step

    def bump(self):
        self.n += self.step
        return self.n


class Plan:

    ENGS = ("t", "v", "s", "y", "g")

    def __init__(self, nc, sems):
        self.nc = nc
        self.q = {e: [] for e in self.ENGS}
        self.ch = {}
        for nm in sems:
            self.ch[nm] = Chan(sems[nm], 1 if nm.startswith("P") else 16)
        self._waited = {e: {} for e in self.ENGS}

    def wait(self, e, cname, tok):
        if tok is None or tok <= 0:
            return
        w = self._waited[e]
        if w.get(cname, 0) >= tok:
            return
        w[cname] = tok
        sem = self.ch[cname].sem
        self.q[e].append(lambda eng: eng.wait_ge(sem, tok))

    def emit(self, e, fn, inc=None, dr=None):
        if dr is None:
            dr = e in ("v", "s") and inc is None
        if inc is None:
            if dr:
                def thunk0(eng):
                    fn(eng)
                    eng.drain()
                self.q[e].append(thunk0)
            else:
                self.q[e].append(lambda eng: fn(eng))
            return None
        ch = self.ch[inc]
        tok = ch.bump()
        sem, step = ch.sem, ch.step

        def thunk(eng):
            fn(eng).then_inc(sem, step)
            if dr:
                eng.drain()

        self.q[e].append(thunk)
        return tok

    def run(self, block):
        block.tensor(lambda eng: [f(eng) for f in self.q["t"]])
        block.vector(lambda eng: [f(eng) for f in self.q["v"]])
        block.scalar(lambda eng: [f(eng) for f in self.q["s"]])
        block.sync(lambda eng: [f(eng) for f in self.q["y"]])
        block.gpsimd(lambda eng: [f(eng) for f in self.q["g"]])


def build_nc(phases=3):
    import os as _os
    KDUMP = int(_os.environ.get("KDUMP", "0"))
    KB = int(_os.environ.get("KB", str(B)))
    nc = bass.Bass("TRN2", target_bir_lowering=False,
                   dynamic_dma_scratch_size=12288)

    x_in = nc.dram_tensor("x_in", [B, T, C], F32, kind="ExternalInput")
    w_attn = nc.dram_tensor("w_attn", [3 * C, C], F32, kind="ExternalInput")
    w_proj = nc.dram_tensor("w_proj", [C, C], F32, kind="ExternalInput")
    w_r = nc.dram_tensor("w_r", [C, C], F32, kind="ExternalInput")
    w_k = nc.dram_tensor("w_k", [C, C], F32, kind="ExternalInput")
    w_v = nc.dram_tensor("w_v", [C, C], F32, kind="ExternalInput")
    mu_r = nc.dram_tensor("mu_r", [128, 8], F32, kind="ExternalInput")
    mu_k = nc.dram_tensor("mu_k", [128, 8], F32, kind="ExternalInput")
    qaug = nc.dram_tensor("qaug", [H, 7, T], BF16, kind="ExternalInput")
    kaug = nc.dram_tensor("kaug", [H, 7, T], BF16, kind="ExternalInput")
    maskc = nc.dram_tensor("maskc", [128, 128], F32, kind="ExternalInput")
    identc = nc.dram_tensor("identc", [128, 128], BF16, kind="ExternalInput")
    biascc = nc.dram_tensor("biascc", [128, 272], F32, kind="ExternalInput")
    y_out = nc.dram_tensor("y_out", [B, T, C], F32, kind="ExternalOutput")

    wattn_b = nc.dram_tensor("wattn_b", [3 * C, C], BF16, kind="Internal")
    wproj_b = nc.dram_tensor("wproj_b", [C, C], BF16, kind="Internal")
    wr_b = nc.dram_tensor("wr_b", [C, C], BF16, kind="Internal")
    wk_b = nc.dram_tensor("wk_b", [C, C], BF16, kind="Internal")
    wv_b = nc.dram_tensor("wv_b", [C, C], BF16, kind="Internal")
    xb16 = nc.dram_tensor("xb16", [B, T, C], BF16, kind="Internal")
    Qs = nc.dram_tensor("Qs", [B, H, 71, T], BF16, kind="Internal")
    Ks = nc.dram_tensor("Ks", [B, H, 71, T], BF16, kind="Internal")
    Ysh = nc.dram_tensor("Ysh", [8, 64, T], BF16, kind="Internal")
    ElRaw = nc.dram_tensor("ElRaw", [H, T], BF16, kind="Internal")
    ElRec = nc.dram_tensor("ElRec", [H, T], BF16, kind="Internal")

    AW = nc.alloc_sbuf_tensor("AW", [128, 16384], BF16)
    BW = nc.alloc_sbuf_tensor("BW", [128, 32768], BF16)
    XY = nc.alloc_sbuf_tensor("XY", [128, 16384], BF16)
    VS = nc.alloc_sbuf_tensor("VS", [128, 16 * 1040], BF16)
    qt = [nc.alloc_sbuf_tensor(f"qt{i}", [128, T], BF16) for i in range(2)]
    kt = [nc.alloc_sbuf_tensor(f"kt{i}", [128, T], BF16) for i in range(2)]
    qkev = [nc.alloc_sbuf_tensor(f"qkev{i}", [128, T], BF16) for i in range(2)]
    expb = [nc.alloc_sbuf_tensor(f"expb{i}", [128, SEG], BF16) for i in range(3)]
    yun = nc.alloc_sbuf_tensor("yun", [128, T], BF16)
    sqb = nc.alloc_sbuf_tensor("sqb", [128, T], BF16)
    x1b = nc.alloc_sbuf_tensor("x1b", [128, C], BF16)
    x1T = nc.alloc_sbuf_tensor("x1T", [128, C], BF16)
    xpT = nc.alloc_sbuf_tensor("xpT", [128, C], BF16)
    xrT = nc.alloc_sbuf_tensor("xrT", [128, C], BF16)
    xkT = nc.alloc_sbuf_tensor("xkT", [128, C], BF16)
    kaT = nc.alloc_sbuf_tensor("kaT", [128, C], BF16)
    mask_sb = nc.alloc_sbuf_tensor("mask_sb", [128, 128], F32)
    biasc_sb = nc.alloc_sbuf_tensor("biasc_sb", [128, 272], F32)
    id_sb = nc.alloc_sbuf_tensor("id_sb", [128, 128], BF16)
    ones_sb = nc.alloc_sbuf_tensor("ones_sb", [128, 64], BF16)
    mur_sb = nc.alloc_sbuf_tensor("mur_sb", [128, 8], F32)
    muk_sb = nc.alloc_sbuf_tensor("muk_sb", [128, 8], F32)
    SM2 = nc.alloc_sbuf_tensor("SM2", [128, 32], F32)
    SMI = nc.alloc_sbuf_tensor("SMI", [128, 4], I32)
    lastc = nc.alloc_sbuf_tensor("lastc", [128, 8], BF16)
    ellp = nc.alloc_sbuf_tensor("ellp", [128, 16], BF16)
    ellr = nc.alloc_sbuf_tensor("ellr", [128, 16], F32)

    PS = nc.alloc_psum_tensor("PS", [128, 4096], F32)

    def awf(lo, hi):
        return AW.ap()[:, lo:hi].bitcast(F32)

    x3f = [awf(0, 2048), awf(2048, 4096)]
    sx1 = awf(4096, 6144)
    rtf = [awf(6144, 8192), awf(8192, 10240)]
    outb = [awf(10240, 12288), awf(12288, 14336)]
    ktmp = awf(14336, 16384)

    ssum = SM2.ap()[:, 0:1]
    ssq = SM2.ap()[:, 1:2]
    negm = SM2.ap()[:, 2:3]
    msq = SM2.ap()[:, 3:4]
    var = SM2.ap()[:, 4:5]
    rstd = SM2.ap()[:, 5:6]
    hv = SM2.ap()[:, 6:7]
    nt1 = SM2.ap()[:, 7:8]
    nt2 = SM2.ap()[:, 8:9]
    mkseg = lambda i: SM2.ap()[64:65, 16 + i:17 + i]
    mk2 = SM2.ap()[64:65, 20:21]
    ngmk = SM2.ap()[64:65, 21:22]

    sems = {}
    import contextlib
    es = contextlib.ExitStack()
    for nm in ("PT", "PV", "PS", "DYW", "DYQ", "DYE", "DYR", "DYS", "DY3",
               "DSQ", "DSE", "DSY", "DSO", "DG", "DGX", "DGR"):
        sems[nm] = es.enter_context(nc.semaphore(nm))

    P = Plan(nc, sems)
    ch = P.ch

    P.emit("g", lambda g: g.dma_start(wattn_b.ap(), w_attn.ap()), inc="DG")
    P.emit("g", lambda g: g.dma_start(wproj_b.ap(), w_proj.ap()), inc="DG")
    P.emit("g", lambda g: g.dma_start(wr_b.ap(), w_r.ap()), inc="DG")
    P.emit("g", lambda g: g.dma_start(wk_b.ap(), w_k.ap()), inc="DG")
    P.emit("g", lambda g: g.dma_start(wv_b.ap(), w_v.ap()), inc="DG")
    tok_wcast = ch["DG"].n
    for b in range(B):
        for h in range(H):
            P.emit("g", lambda g, b=b, h=h: g.dma_start(
                Qs.ap()[b, h, 64:71, :], qaug.ap()[h]), inc="DG")
            P.emit("g", lambda g, b=b, h=h: g.dma_start(
                Ks.ap()[b, h, 64:71, :], kaug.ap()[h]), inc="DG")
    tok_aug = ch["DG"].n
    P.emit("g", lambda g: g.dma_start(mask_sb.ap(), maskc.ap()), inc="DG")
    P.emit("g", lambda g: g.dma_start(id_sb.ap(), identc.ap()), inc="DG")
    P.emit("g", lambda g: g.dma_start(mur_sb.ap(), mu_r.ap()), inc="DG")
    P.emit("g", lambda g: g.dma_start(muk_sb.ap(), mu_k.ap()), inc="DG")
    P.emit("g", lambda g: g.dma_start(biasc_sb.ap(), biascc.ap()), inc="DG")
    tok_consts = ch["DG"].n
    _reg = set()

    def _regconst(val, col):
        if val in _reg or val == 0.0:
            return
        _reg.add(val)
        nc.const_aps.register(F32, val, biasc_sb.ap()[:, col:col + 1])

    for _h in range(H):
        _se = eff_slope(_h)
        for _j in range(16):
            _regconst(float(np.float32(_se * 128.0 * _j)), _h * 16 + _j)
    _regconst(0.5, 256)
    _regconst(float(np.float32(0.5 ** 0.5)), 257)
    tok_xb = [None] * B
    for b in range(B):
        tok_xb[b] = P.emit("g", lambda g, b=b: g.dma_start(
            xb16.ap()[b], x_in.ap()[b]), inc="DGX")

    P.emit("v", lambda v: v.memset(ones_sb.ap(), 1.0))
    P.emit("v", lambda v: v.memset(SMI.ap()[:, 0:1], MAGIC))
    vs_ones = VS.ap().rearrange("p (g x) -> p g x", x=65)[:, :, 64:65]
    tok_vinit = P.emit("v", lambda v: v.memset(vs_ones, 1.0), inc="PV")

    def wslab(dst_ap, src, r0, c0):
        return lambda y: y.dma_start(dst_ap, src.ap()[r0:r0 + 1024, c0:c0 + 128],
                                     transpose=True)

    def load_wq():
        P.wait("y", "DG", tok_wcast)
        for c in range(NCC):
            P.emit("y", wslab(AW.ap()[:, c * 1024:(c + 1) * 1024],
                              wattn_b, 0, c * 128), inc="DYW")
        return ch["DYW"].n

    def load_wk_slot1():
        for c in range(NCC):
            P.emit("y", wslab(AW.ap()[:, 8192 + c * 1024:8192 + (c + 1) * 1024],
                              wattn_b, 1024, c * 128), inc="DYW")
        return ch["DYW"].n

    def load_wv_slot0():
        for c in range(NCC):
            P.emit("y", wslab(AW.ap()[:, c * 1024:(c + 1) * 1024],
                              wattn_b, 2048, c * 128), inc="DYW")
        return ch["DYW"].n

    def load_bw():
        for i, wsrc in enumerate((wproj_b, wr_b, wk_b, wv_b)):
            for c in range(NCC):
                P.emit("y", wslab(
                    BW.ap()[:, i * 8192 + c * 1024:i * 8192 + (c + 1) * 1024],
                    wsrc, 0, c * 128), inc="DYW")
        return ch["DYW"].n

    def load_xT(b):
        P.wait("y", "DGX", tok_xb[b])
        for c in range(NCC):
            P.emit("y", lambda y, b=b, c=c: y.dma_start(
                XY.ap()[:, c * T:(c + 1) * T],
                xb16.ap()[b][:, c * 128:(c + 1) * 128], transpose=True),
                inc="DYW")
        return ch["DYW"].n

    def aw_q(c):
        return AW.ap()[:, c * 1024:(c + 1) * 1024]

    def aw_k(c):
        return AW.ap()[:, 8192 + c * 1024:8192 + (c + 1) * 1024]

    def bw(i, c):
        return BW.ap()[:, i * 8192 + c * 1024:i * 8192 + (c + 1) * 1024]

    def xT(c):
        return XY.ap()[:, c * T:(c + 1) * T]

    yT = xT

    tok_wq = load_wq()
    tok_wk = load_wk_slot1()
    tok_xT = [None] * B
    tok_xT[0] = load_xT(0)
    tok_bw = load_bw()

    tok_p3_done_t = 0
    tok_p3_done_v = 0
    tok_p3_done_s = 0
    tok_p3_out_d = 0
    slopes_eff = [eff_slope(h) for h in range(H)]
    tok_s_tile = [0] * NTT
    tok_tp_evac = 0
    tok_ffn = {0: 0, 1: 0}
    tok_out = {0: 0, 1: 0}
    tok_out_dma = {0: 0, 1: 0}
    tok_sq2 = 0
    tok_bl_prev = 0
    tok_mm2_prev = 0
    tok_mm3_prev = 0
    st2 = {
        "exp_by_bank": {0: 0, 1: 0},
        "sbank": 0,
        "expn": 0,
        "av_by_expb": {0: 0, 1: 0, 2: 0},
        "ye": 0,
        "rell": 0,
        "norm": 0,
        "yun_free_ds": 0,
    }

    for b in range(KB):
        tok_qs_w = {}
        grp_evac = {0: 0, 1: 0}
        if b > 0:
            P.wait("y", "PT", tok_p3_done_t)
            P.wait("y", "PV", tok_p3_done_v)
            P.wait("y", "PS", tok_p3_done_s)
            P.wait("y", "DSO", tok_p3_out_d)
            tok_wq = load_wq()
            tok_wk = load_wk_slot1()
            tok_xT[b] = load_xT(b)

        last_qk_mm = 0
        tok_p1_evac_last = 0
        _wl = []
        if phases >= 0.5:
            _wl.append(("q", aw_q, Qs))
        if phases >= 0.7:
            _wl.append(("k", aw_k, Ks))
        for which, wsel, scratch in _wl:
            P.wait("t", "DYW", tok_wq if which == "q" else tok_wk)
            P.wait("t", "DYW", tok_xT[b])
            if b > 0:
                P.wait("t", "PV", tok_p3_done_v)
                P.wait("t", "PS", tok_p3_done_s)
            for dt in range(8):
                gi = dt % 2
                grp = gi * 2048
                P.wait("t", "PV", grp_evac[gi])
                tok_mm = None
                for cc in range(NCC):
                    for sgi in range(4):
                        tok_mm = P.emit("t", lambda t, w=wsel(cc), cc=cc, dt=dt,
                                        sgi=sgi, grp=grp: t.matmul(
                            PS.ap()[:, grp + sgi * 512:grp + (sgi + 1) * 512],
                            w[:, dt * 128:(dt + 1) * 128],
                            xT(cc)[:, sgi * 512:(sgi + 1) * 512],
                            start=(cc == 0), stop=(cc == NCC - 1)),
                            inc="PT" if (cc == NCC - 1 and sgi == 3) else None)
                last_qk_mm = tok_mm
                ev = qkev[gi]
                prev = tok_qs_w.get((which, dt - 2)) or (
                    tok_qs_w.get(("q", dt + 6)) if which == "k" else None)
                P.wait("v", "DSQ", prev)
                P.wait("v", "PT", tok_mm)
                tok_ev = None
                for sgi in range(4):
                    tok_ev = P.emit("v", lambda v, ev=ev, grp=grp, sgi=sgi:
                                    v.tensor_scalar(
                        ev.ap()[:, sgi * 512:(sgi + 1) * 512],
                        PS.ap()[:, grp + sgi * 512:grp + (sgi + 1) * 512],
                        ISQRT8, None, ALU.mult),
                        inc="PV" if sgi == 3 else None,
                        dr=False if sgi < 3 else None)
                grp_evac[gi] = tok_ev
                tok_p1_evac_last = tok_ev
                P.wait("s", "PV", tok_ev)
                P.wait("s", "DG", tok_aug)
                tk = None
                for half in range(2):
                    tk = P.emit("s", lambda s, ev=ev, b=b, dt=dt, half=half,
                                scratch=scratch: s.dma_start(
                        scratch.ap()[b, 2 * dt + half, 0:64, :],
                        ev.ap()[64 * half:64 * half + 64, :]), inc="DSQ",
                        dr=False)
                tok_qs_w[(which, dt)] = tk

        if phases < 1:
            continue
        P.wait("y", "PT", last_qk_mm)
        tok_wv = load_wv_slot0()
        P.wait("t", "DYW", tok_wv)
        P.wait("t", "PV", grp_evac[0])
        vgrp_evac = {0: 0, 1: 0}
        tok_v_evac = 0
        for tt in range(NTT):
            gi = tt % 2
            grp = gi * 1024
            P.wait("t", "PV", vgrp_evac[gi])
            tok_mm = None
            for cc in range(NCC):
                for sgi in range(2):
                    tok_mm = P.emit("t", lambda t, cc=cc, tt=tt, sgi=sgi, grp=grp:
                                    t.matmul(
                        PS.ap()[:, grp + sgi * 512:grp + (sgi + 1) * 512],
                        xT(cc)[:, tt * 128:(tt + 1) * 128],
                        aw_q(cc)[:, sgi * 512:(sgi + 1) * 512],
                        start=(cc == 0), stop=(cc == NCC - 1)),
                        inc="PT" if (cc == NCC - 1 and sgi == 1) else None)
            P.wait("v", "PT", tok_mm)
            P.wait("v", "PV", tok_vinit)
            src = PS.ap()[:, grp:grp + 1024].rearrange("p (h e) -> p h e", e=64)
            dst = VS.ap()[:, tt * 1040:(tt + 1) * 1040].rearrange(
                "p (h e) -> p h e", e=65)[:, :, 0:64]
            tok_v_evac = P.emit("v", lambda v, src=src, dst=dst:
                                v.tensor_copy(dst, src), inc="PV")
            vgrp_evac[gi] = tok_v_evac

        if phases < 2:
            tok_p3_done_t = ch["PT"].n
            tok_p3_done_v = ch["PV"].n
            tok_p3_done_s = ch["PS"].n
            tok_p3_out_d = ch["DSO"].n
            continue
        Yps = PS.ap()[0:65, 0:2048]
        Yl = PS.ap()[64:65, 0:2048]
        Sb = [PS.ap()[:, 2048 + 512 * i:2048 + 512 * (i + 1)] for i in range(2)]
        nrm = PS.ap()[64:65, 3072:3584]
        bcb = PS.ap()[0:64, 3584:4096]

        tok_qk_load = {}
        tok_ub = {}
        tok_scores_last = {}

        def p2_load(h):
            buf = h % 2
            P.wait("y", "DSQ", ch["DSQ"].n if h == 0 else
                   max(tok_qs_w[("q", h // 2)], tok_qs_w[("k", h // 2)]))
            P.wait("y", "PT", tok_scores_last.get(h - 2))
            P.wait("y", "PV", tok_ub.get(h - 2))
            P.emit("y", lambda y, b=b, h=h, buf=buf: y.dma_start(
                kt[buf].ap()[0:71, :], Ks.ap()[b, h]), inc="DYQ")
            tok_qk_load[h] = P.emit("y", lambda y, b=b, h=h, buf=buf: y.dma_start(
                qt[buf].ap()[0:71, :], Qs.ap()[b, h]), inc="DYQ")

        def p2_prep(h):
            buf = h % 2
            P.wait("v", "DYQ", tok_qk_load[h])
            tk = P.emit("v", lambda v, buf=buf: v.tensor_mul(
                sqb.ap()[0:64, :], kt[buf].ap()[0:64, :], kt[buf].ap()[0:64, :]),
                inc="PV")
            P.wait("t", "PV", tok_vinit)
            if h == 0:
                P.wait("t", "PV", tok_p1_evac_last)
            last_red = 0
            for sgi in range(4):
                P.wait("t", "PV", tk)
                P.wait("t", "PV", last_red)
                tm = P.emit("t", lambda t, sgi=sgi: t.matmul(
                    nrm, ones_sb.ap()[0:64, 0:1],
                    sqb.ap()[0:64, sgi * 512:(sgi + 1) * 512],
                    start=True, stop=True), inc="PT")
                yield
                P.wait("v", "PT", tm)
                last_red = P.emit("v", lambda v, sgi=sgi: v.reduce_max(
                    mkseg(sgi), nrm, AX.X), inc="PV", dr=True)
            P.wait("v", "PT", ch["PT"].n)
            tkq = P.emit("v", lambda v, buf=buf: v.tensor_mul(
                sqb.ap()[0:64, :], qt[buf].ap()[0:64, :], qt[buf].ap()[0:64, :]),
                inc="PV")
            P.emit("v", lambda v: v.reduce_max(mk2, SM2.ap()[64:65, 16:20], AX.X))
            tmk = P.emit("v", lambda v: v.tensor_scalar(
                ngmk, mk2, -0.5, None, ALU.mult), inc="PV")
            lastu = 0
            for sgi in range(4):
                P.wait("t", "PV", tkq)
                P.wait("t", "PV", lastu)
                tm = P.emit("t", lambda t, sgi=sgi: t.matmul(
                    nrm, ones_sb.ap()[0:64, 0:1],
                    sqb.ap()[0:64, sgi * 512:(sgi + 1) * 512],
                    start=True, stop=True), inc="PT")
                yield
                P.wait("v", "PT", tm)
                P.wait("v", "PV", tmk)
                lastu = P.emit("v", lambda v, buf=buf, sgi=sgi: v.tensor_scalar(
                    qt[buf].ap()[64:65, sgi * 512:(sgi + 1) * 512],
                    nrm, -0.5, ngmk, ALU.mult, ALU.add), inc="PV")
            tok_ub[h] = lastu

        def p2_dump_qt(h):
            if not (KDUMP == 2 and b == 0 and h == int(_os.environ.get("KH", "1"))):
                return
            buf = h % 2
            d_qt = nc.dram_tensor("d_qt", [128, T], BF16, kind="ExternalOutput")
            d_kt = nc.dram_tensor("d_kt", [128, T], BF16, kind="ExternalOutput")
            P.wait("s", "PV", tok_ub[h])
            P.emit("s", lambda s, buf=buf: s.dma_start(d_qt.ap(), qt[buf].ap()),
                   inc="DSO", dr=False)
            P.emit("s", lambda s, buf=buf: s.dma_start(d_kt.ap(), kt[buf].ap()),
                   inc="DSO", dr=False)

        def p2_head(h, first_of_batch, prep_gen=None, do_av=True, do_fin=True):
            buf = h % 2
            cut = CUT[h]
            sl = slopes_eff[h]
            firstj = {}
            lastj = {}
            for j in range(NTT):
                qmax = min(T, 128 * (j + cut))
                for bank in range((128 * j) // 512, (qmax + 511) // 512):
                    firstj.setdefault(bank, j)
                    lastj[bank] = j
            tok_av_last = 0
            for j in range(NTT):
                if prep_gen is not None:
                    next(prep_gen, None)
                qmax = min(T, 128 * (j + cut))
                q0 = 128 * j
                bounds = [q0]
                nb = (q0 // 512 + 1) * 512
                while nb < qmax:
                    bounds.append(nb)
                    nb += 512
                bounds.append(qmax)
                for si in range(len(bounds) - 1):
                    qa, qb = bounds[si], bounds[si + 1]
                    sbk = st2["sbank"]
                    sb = Sb[sbk]
                    P.wait("t", "PV", tok_ub[h])
                    P.wait("t", "PS", st2["exp_by_bank"][sbk])
                    if first_of_batch and j == 0 and si == 0:
                        P.wait("t", "PV", tok_p1_evac_last)
                    tok_sc = P.emit("t", lambda t, buf=buf, j=j, qa=qa, qb=qb,
                                    sb=sb: t.matmul(
                        sb[:, 0:qb - qa], kt[buf].ap()[0:71, j * 128:(j + 1) * 128],
                        qt[buf].ap()[0:71, qa:qb], start=True, stop=True),
                        inc="PT")
                    tok_scores_last[h] = tok_sc
                    tok_mask = None
                    if si == 0:
                        P.wait("v", "DG", tok_consts)
                        P.wait("v", "PT", tok_sc)
                        tok_mask = P.emit("v", lambda v, sb=sb: v.tensor_add(
                            sb[:, 0:128], sb[:, 0:128], mask_sb.ap()), inc="PV")
                    ei = st2["expn"] % 3
                    eb = expb[ei]
                    st2["expn"] += 1
                    if tok_mask is not None:
                        P.wait("s", "PV", tok_mask)
                    P.wait("s", "DG", tok_consts)
                    P.wait("s", "PT", tok_sc)
                    P.wait("s", "PT", st2["av_by_expb"][ei])
                    bias = float(np.float32(sl * 128.0 * j))
                    tok_exp = P.emit("s", lambda s, sb=sb, eb=eb, qa=qa, qb=qb,
                                     bias=bias: s.activation(
                        eb.ap()[:, 0:qb - qa], sb[:, 0:qb - qa], AF.Exp,
                        bias=bias), inc="PS")
                    st2["exp_by_bank"][sbk] = tok_exp
                    st2["sbank"] ^= 1
                    if not do_av:
                        continue
                    P.wait("t", "PS", tok_exp)
                    if first_of_batch and j == 0 and si == 0:
                        P.wait("t", "PV", tok_v_evac)
                    if j == 0 and si == 0:
                        P.wait("t", "PS", st2["ye"])
                        P.wait("t", "PV", st2["rell"])
                    bk0 = qa // 512
                    bk1 = (qb - 1) // 512
                    for bank in range(bk0, bk1 + 1):
                        ca = max(qa, bank * 512)
                        cb = min(qb, (bank + 1) * 512)
                        stf = (firstj[bank] == j)
                        spf = (lastj[bank] == j)
                        tok_av_last = P.emit("t", lambda t, h=h, j=j, ca=ca,
                                             cb=cb, qa=qa, eb=eb, stf=stf,
                                             spf=spf: t.matmul(
                            Yps[:, ca:cb],
                            VS.ap()[:, j * 1040 + h * 65:j * 1040 + h * 65 + 65],
                            eb.ap()[:, ca - qa:cb - qa], start=stf, stop=spf),
                            inc="PT")
                    st2["av_by_expb"][ei] = tok_av_last
            if not do_fin:
                return
            import os as _os
            KFIN = int(_os.environ.get("KFIN", "31"))
            P.wait("v", "PT", tok_av_last)
            tok_rell = None
            tok_rell_done = None
            if KFIN & 1:
                tok_rell = P.emit("v", lambda v: v.tensor_copy(
                    sqb.ap()[64:65, 0:2048], Yl), inc="PV")
                P.wait("s", "PV", tok_rell)
                t1 = P.emit("s", lambda s, h=h: s.dma_start(
                    ElRaw.ap()[h:h + 1, :], sqb.ap()[64:65, 0:2048]),
                    inc="DSE", dr=False)
                P.wait("y", "DSE", t1)
                P.wait("y", "PV", st2.get("ellp_free", 0))
                t2 = P.emit("y", lambda y, h=h: y.dma_start(
                    ellp.ap(), ElRaw.ap()[h].rearrange("(p c) -> p c", p=128)),
                    inc="DYE")
                P.wait("v", "DYE", t2)
                tok_rec = P.emit("v", lambda v: v.reciprocal(
                    ellr.ap(), ellp.ap()), inc="PV", dr=True)
                st2["ellp_free"] = tok_rec
                P.wait("g", "PV", tok_rec)
                t3 = P.emit("g", lambda g, h=h: g.dma_start(
                    ElRec.ap()[h].rearrange("(p c) -> p c", p=128), ellr.ap()),
                    inc="DGR", dr=False)
                P.wait("y", "DGR", t3)
                tok_rell_done = P.emit("y", lambda y, h=h: y.dma_start(
                    sqb.ap()[64:65, 0:2048], ElRec.ap()[h:h + 1, :]), inc="DYR")
            tok_ye = None
            if KFIN & 2:
                P.wait("s", "PT", tok_av_last)
                P.wait("s", "PV", st2["norm"])
                P.wait("s", "DSY", st2.get("yun_free_ds", 0))
                tok_ye = P.emit("s", lambda s: s.activation(
                    yun.ap()[0:64, :], PS.ap()[0:64, 0:2048], AF.Copy), inc="PS")
            if KDUMP == 2 and b == 0 and h == int(_os.environ.get("KH", "1")):
                d_yun = nc.dram_tensor("d_yun", [64, T], BF16, kind="ExternalOutput")
                P.wait("s", "PS", tok_ye)
                P.emit("s", lambda s: s.dma_start(d_yun.ap(), yun.ap()[0:64, :]),
                       inc="DSO", dr=False)
                d_rell = nc.dram_tensor("d_rell", [1, T], BF16, kind="ExternalOutput")
                P.wait("s", "PV", tok_rell)
                P.emit("s", lambda s: s.dma_start(d_rell.ap(), sqb.ap()[64:65, :]),
                       inc="DSO", dr=False)
            tok_norm = 0
            pair = h // 2
            odd = h % 2
            for sgi in range(4):
                if not (KFIN & 4):
                    break
                P.wait("t", "DYR", tok_rell_done)
                if sgi == 0:
                    P.wait("t", "PV", st2["norm"])
                    if first_of_batch:
                        P.wait("t", "PV", tok_p1_evac_last)
                else:
                    P.wait("t", "PV", tok_norm)
                tbc = P.emit("t", lambda t, sgi=sgi: t.matmul(
                    bcb, ones_sb.ap()[64:65, 0:64],
                    sqb.ap()[64:65, sgi * 512:(sgi + 1) * 512],
                    start=True, stop=True), inc="PT")
                if not (KFIN & 8):
                    continue
                P.wait("v", "PT", tbc)
                P.wait("v", "PS", tok_ye)
                if odd:
                    tok_norm = P.emit("v", lambda v, sgi=sgi: v.tensor_mul(
                        yun.ap()[0:64, sgi * 512:(sgi + 1) * 512],
                        yun.ap()[0:64, sgi * 512:(sgi + 1) * 512], bcb),
                        inc="PV")
                else:
                    tok_norm = P.emit("v", lambda v, pair=pair, sgi=sgi:
                                      v.tensor_mul(
                        yT(pair)[0:64, sgi * 512:(sgi + 1) * 512],
                        yun.ap()[0:64, sgi * 512:(sgi + 1) * 512], bcb),
                        inc="PV")
            if (KFIN & 16) and odd:
                slot = pair
                P.wait("s", "PV", tok_norm)
                tw = P.emit("s", lambda s, slot=slot: s.dma_start(
                    Ysh.ap()[slot], yun.ap()[0:64, :]), inc="DSY", dr=False)
                P.wait("y", "DSY", tw)
                tr = P.emit("y", lambda y, pair=pair, slot=slot: y.dma_start(
                    XY.ap()[64:128, pair * T:(pair + 1) * T], Ysh.ap()[slot]),
                    inc="DYS")
                st2["ydma"] = tr
                tok_norm = None
                st2["yun_free_ds"] = tw
            if tok_ye:
                st2["ye"] = tok_ye
            if tok_rell:
                st2["rell"] = tok_rell
            if tok_norm:
                st2["norm"] = tok_norm

        p2_load(0)
        if phases >= 1.5:
            for _ in p2_prep(0):
                pass
        gen_next = None
        for h in range(H):
            if h + 1 < H:
                p2_load(h + 1)
            if phases < 1.5:
                continue
            gen_next = p2_prep(h + 1) if h + 1 < H else None
            p2_dump_qt(h)
            if phases < 1.7:
                continue
            p2_head(h, first_of_batch=(h == 0), prep_gen=gen_next,
                    do_av=(phases >= 1.8), do_fin=(phases >= 1.9))
            if gen_next is not None:
                for _ in gen_next:
                    pass
        tok_norm_last = st2["norm"]
        tok_ydma_last = st2.get("ydma", 0)
        tok_ye_last = st2["ye"]

        if phases < 3:
            tok_p3_done_t = ch["PT"].n
            tok_p3_done_v = ch["PV"].n
            tok_p3_done_s = ch["PS"].n
            tok_p3_out_d = ch["DSO"].n
            continue
        P1p = PS.ap()[:, 0:1024]
        R1 = PS.ap()[:, 1024:2048]
        K2 = PS.ap()[:, 2048:3072]
        P3p = PS.ap()[:, 3072:4096]
        TpA = PS.ap()[:, 0:512].bitcast(BF16)

        tok_x3 = [None] * NTT
        P.wait("y", "PT", ch["PT"].n)

        def load_x3(tt):
            if tt >= 2:
                P.wait("y", "PV", tok_s_tile[tt - 2])
            tok_x3[tt] = P.emit("y", lambda y, b=b, tt=tt: y.dma_start(
                x3f[tt % 2], x_in.ap()[b, tt * 128:(tt + 1) * 128, :]),
                inc="DY3")

        load_x3(0)
        tok_lc = P.emit("v", lambda v: v.memset(lastc.ap(), 0.0), inc="PV")

        for tt in range(NTT):
            par = tt % 2
            if tt + 1 < NTT:
                load_x3(tt + 1)
            P.wait("t", "DYW", tok_bw)
            P.wait("t", "PV", tok_norm_last)
            P.wait("t", "DYS", tok_ydma_last)
            if tt == 0:
                P.wait("t", "PS", ch["PS"].n)
            else:
                P.wait("t", "PV", tok_tp_evac)
            tok_proj = None
            for dc in range(NCC):
                for sgi in range(2):
                    tok_proj = P.emit("t", lambda t, dc=dc, tt=tt, sgi=sgi:
                                      t.matmul(
                        P1p[:, sgi * 512:(sgi + 1) * 512],
                        yT(dc)[:, tt * 128:(tt + 1) * 128],
                        bw(0, dc)[:, sgi * 512:(sgi + 1) * 512],
                        start=(dc == 0), stop=(dc == NCC - 1)),
                        inc="PT" if (dc == NCC - 1 and sgi == 1) else None)
            P.wait("v", "PT", tok_proj)
            P.wait("v", "DY3", tok_x3[tt])
            tok_s = P.emit("v", lambda v, tt=tt: v.scalar_tensor_tensor(
                sx1, P1p, 1.0, x3f[tt % 2], ALU.mult, ALU.add, accum_out=ssum),
                inc="PV")
            tok_s_tile[tt] = tok_s
            P.wait("s", "PV", tok_s)
            P.wait("s", "DSO", tok_out_dma[par])
            if KDUMP == 3 and b == 0 and tt == 0:
                d_s = nc.dram_tensor("d_s", [128, C], F32, kind="ExternalOutput")
                tds = P.emit("s", lambda s: s.dma_start(d_s.ap(), sx1),
                             inc="DSO", dr=False)
                P.wait("s", "DSO", tds)
            tok_ssq = P.emit("s", lambda s, tt=tt: s.activation(
                outb[tt % 2], sx1, AF.Square, accum_out=ssq), inc="PS")

            def newton(tok_sq):
                P.wait("v", "PS", tok_sq)
                P.emit("v", lambda v: v.tensor_scalar(
                    negm, ssum, -1.0 / C, None, ALU.mult))
                P.emit("v", lambda v: v.tensor_mul(msq, negm, negm))
                P.emit("v", lambda v: v.tensor_scalar(
                    nt1, ssq, 1.0 / C, EPS, ALU.mult, ALU.add))
                P.emit("v", lambda v: v.tensor_sub(var, nt1, msq))
                P.emit("v", lambda v: v.tensor_scalar(
                    SMI.ap()[:, 1:2], var.bitcast(I32), 1, None,
                    ALU.arith_shift_right))
                P.emit("v", lambda v: v.scalar_tensor_tensor(
                    rstd.bitcast(I32), SMI.ap()[:, 0:1], 0, SMI.ap()[:, 1:2],
                    ALU.add, ALU.subtract))
                P.emit("v", lambda v: v.tensor_scalar(
                    hv, var, -0.5, None, ALU.mult))
                for _ in range(3):
                    P.emit("v", lambda v: v.tensor_mul(nt1, rstd, rstd))
                    P.emit("v", lambda v: v.tensor_scalar(
                        nt2, nt1, hv, 1.5, ALU.mult, ALU.add))
                    P.emit("v", lambda v: v.tensor_mul(rstd, rstd, nt2))

            newton(tok_ssq)
            tok_nmr = P.emit("v", lambda v: v.tensor_mul(nt1, negm, rstd),
                             inc="PV", dr=True)
            P.wait("s", "PV", tok_nmr)
            tok_x1 = P.emit("s", lambda s: s.activation(
                sx1, sx1, AF.Identity, bias=nt1, scale=rstd), inc="PS", dr=True)
            if KDUMP == 3 and b == 0 and tt == 0:
                d_x1 = nc.dram_tensor("d_x1", [128, C], F32, kind="ExternalOutput")
                P.wait("s", "PV", tok_x1)
                P.emit("s", lambda s: s.dma_start(d_x1.ap(), sx1), inc="DSO", dr=False)
            P.wait("s", "PV", tok_bl_prev)
            tok_x1b = P.emit("s", lambda s: s.activation(
                x1b.ap(), sx1, AF.Copy), inc="PS")
            P.wait("t", "PS", tok_x1b)
            P.wait("t", "PV", tok_s)
            P.wait("t", "DG", tok_consts)
            tok_tp = None
            for cc in range(NCC):
                tok_tp = P.emit("t", lambda t, cc=cc: t.transpose(
                    TpA[:, cc * 128:(cc + 1) * 128],
                    x1b.ap()[:, cc * 128:(cc + 1) * 128], id_sb.ap()),
                    inc="PT" if cc == NCC - 1 else None)
            P.wait("v", "PT", tok_tp)
            tok_tp_evac = P.emit("v", lambda v: v.tensor_copy(
                x1T.ap(), TpA), inc="PV")
            P.wait("s", "PV", tok_tp_evac)
            P.wait("s", "PV", tok_lc)
            src_sh = x1T.ap().rearrange("p (c x) -> p c x", x=128)[:, :, 0:127]
            dst_sh = xpT.ap().rearrange("p (c x) -> p c x", x=128)[:, :, 1:128]
            P.emit("s", lambda s: s.copy(dst_sh, src_sh))
            tok_xp = P.emit("s", lambda s: s.copy(
                xpT.ap().rearrange("p (c x) -> p c x", x=128)[:, :, 0:1],
                lastc.ap().rearrange("p (c o) -> p c o", o=1)), inc="PS")
            P.wait("v", "PS", tok_xp)
            tok_lc = P.emit("v", lambda v: v.tensor_copy(
                lastc.ap(),
                x1T.ap().rearrange("p (c x) -> p c x", x=128)[:, :, 127:128]
                .rearrange("p c x -> p (c x)")), inc="PV")
            tok_d = P.emit("v", lambda v: v.tensor_sub(
                x1b.ap(), x1T.ap(), xpT.ap()), inc="PV", dr=True)
            P.wait("v", "PT", tok_mm2_prev)
            P.wait("v", "DG", tok_consts)
            tok_bl = None
            for cc in range(NCC):
                P.emit("v", lambda v, cc=cc: v.scalar_tensor_tensor(
                    xrT.ap()[:, cc * 128:(cc + 1) * 128],
                    x1b.ap()[:, cc * 128:(cc + 1) * 128],
                    mur_sb.ap()[:, cc:cc + 1],
                    xpT.ap()[:, cc * 128:(cc + 1) * 128], ALU.mult, ALU.add),
                    dr=False)
                tok_bl = P.emit("v", lambda v, cc=cc: v.scalar_tensor_tensor(
                    xkT.ap()[:, cc * 128:(cc + 1) * 128],
                    x1b.ap()[:, cc * 128:(cc + 1) * 128],
                    muk_sb.ap()[:, cc:cc + 1],
                    xpT.ap()[:, cc * 128:(cc + 1) * 128], ALU.mult, ALU.add),
                    inc="PV" if cc == NCC - 1 else None,
                    dr=False if cc < NCC - 1 else None)
            tok_bl_prev = tok_bl
            if KDUMP == 3 and b == 0 and tt == 0:
                d_xr = nc.dram_tensor("d_xr", [128, C], BF16, kind="ExternalOutput")
                d_xk = nc.dram_tensor("d_xk", [128, C], BF16, kind="ExternalOutput")
                P.wait("s", "PV", tok_bl)
                P.emit("s", lambda s: s.dma_start(d_xr.ap(), xrT.ap()), inc="DSO", dr=False)
                P.emit("s", lambda s: s.dma_start(d_xk.ap(), xkT.ap()), inc="DSO", dr=False)
            P.wait("t", "PV", tok_bl)
            if tt > 0:
                P.wait("t", "PS", tok_sq2)
                P.wait("t", "PV", tok_ffn[1 - par])
            tok_mm1 = None
            for cc in range(NCC):
                for sgi in range(2):
                    tok_mm1 = P.emit("t", lambda t, cc=cc, sgi=sgi: t.matmul(
                        R1[:, sgi * 512:(sgi + 1) * 512],
                        xrT.ap()[:, cc * 128:(cc + 1) * 128],
                        bw(1, cc)[:, sgi * 512:(sgi + 1) * 512],
                        start=(cc == 0), stop=(cc == NCC - 1)),
                        inc="PT" if (cc == NCC - 1 and sgi == 1) else None)
            tok_mm2 = None
            for dc in range(NCC):
                for cc in range(NCC):
                    tok_mm2 = P.emit("t", lambda t, dc=dc, cc=cc: t.matmul(
                        K2[:, dc * 128:(dc + 1) * 128],
                        bw(2, cc)[:, dc * 128:(dc + 1) * 128],
                        xkT.ap()[:, cc * 128:(cc + 1) * 128],
                        start=(cc == 0), stop=(cc == NCC - 1)),
                        inc="PT" if (dc == NCC - 1 and cc == NCC - 1) else None)
            tok_mm2_prev = tok_mm2
            P.wait("s", "PT", tok_mm1)
            P.wait("s", "PV", tok_ffn[par])
            tok_th = P.emit("s", lambda s, tt=tt: s.activation(
                rtf[tt % 2], R1, AF.Tanh, scale=0.5), inc="PS")
            P.wait("s", "PT", tok_mm2)
            tok_rel = P.emit("s", lambda s: s.activation(
                ktmp, K2, AF.Relu), inc="PS", dr=True)
            P.wait("s", "PT", tok_mm3_prev)
            tok_ka = P.emit("s", lambda s: s.activation(
                kaT.ap(), ktmp, AF.Square, scale=float(np.float32(0.5 ** 0.5))),
                inc="PS")
            if KDUMP == 3 and b == 0 and tt == 0:
                d_ka = nc.dram_tensor("d_ka", [128, C], BF16, kind="ExternalOutput")
                P.wait("s", "PV", tok_ka)
                P.emit("s", lambda s: s.dma_start(d_ka.ap(), kaT.ap()), inc="DSO", dr=False)
            P.wait("t", "PS", tok_ka)
            tok_mm3 = None
            for dc in range(NCC):
                for sgi in range(2):
                    tok_mm3 = P.emit("t", lambda t, dc=dc, sgi=sgi: t.matmul(
                        P3p[:, sgi * 512:(sgi + 1) * 512],
                        kaT.ap()[:, dc * 128:(dc + 1) * 128],
                        bw(3, dc)[:, sgi * 512:(sgi + 1) * 512],
                        start=(dc == 0), stop=(dc == NCC - 1)),
                        inc="PT" if (dc == NCC - 1 and sgi == 1) else None)
            tok_mm3_prev = tok_mm3
            P.wait("v", "PT", tok_mm3)
            P.wait("v", "PS", tok_th)
            tok_ffn[par] = P.emit("v", lambda v, tt=tt: v.scalar_tensor_tensor(
                rtf[tt % 2], rtf[tt % 2], 1.0, P3p, ALU.add, ALU.mult),
                inc="PV", dr=True)
            tok_s2 = P.emit("v", lambda v, tt=tt: v.scalar_tensor_tensor(
                ktmp, rtf[tt % 2], 1.0, sx1, ALU.mult, ALU.add, accum_out=ssum),
                inc="PV")
            P.wait("s", "PV", tok_s2)
            tok_sq2 = P.emit("s", lambda s, tt=tt: s.activation(
                outb[tt % 2], ktmp, AF.Square, accum_out=ssq), inc="PS")
            newton(tok_sq2)
            tok_out[par] = P.emit("v", lambda v, tt=tt: v.tensor_scalar(
                outb[tt % 2], ktmp, negm, rstd, ALU.add, ALU.mult), inc="PV")
            if KDUMP == 3 and b == 0 and tt == 0:
                d_rt = nc.dram_tensor("d_rt", [128, C], F32, kind="ExternalOutput")
                P.wait("s", "PV", tok_ffn[par])
                P.emit("s", lambda s, tt=tt: s.dma_start(d_rt.ap(), rtf[tt % 2]),
                       inc="DSO", dr=False)
            P.wait("s", "PV", tok_out[par])
            tok_out_dma[par] = P.emit("s", lambda s, b=b, tt=tt: s.dma_start(
                y_out.ap()[b, tt * 128:(tt + 1) * 128, :], outb[tt % 2]),
                inc="DSO", dr=False)

        tok_p3_done_t = ch["PT"].n
        tok_p3_done_v = ch["PV"].n
        tok_p3_done_s = ch["PS"].n
        tok_p3_out_d = ch["DSO"].n

    if KDUMP:
        d_q = nc.dram_tensor("d_q", [H, 71, T], BF16, kind="ExternalOutput")
        d_k = nc.dram_tensor("d_k", [H, 71, T], BF16, kind="ExternalOutput")
        d_xb = nc.dram_tensor("d_xb", [T, C], BF16, kind="ExternalOutput")
        d_xy = nc.dram_tensor("d_xy", [128, 16384], BF16, kind="ExternalOutput")
        d_vs = nc.dram_tensor("d_vs", [128, 16 * 1040], BF16, kind="ExternalOutput")
        d_ysh = nc.dram_tensor("d_ysh", [8, 64, T], BF16, kind="ExternalOutput")
        for _cn in ("DSQ", "DSE", "DSY", "DSO", "DYW", "DYQ", "DYE", "DYR",
                    "DYS", "DY3", "PV", "PS", "PT"):
            P.wait("g", _cn, ch[_cn].n)
        P.emit("g", lambda g: g.dma_start(d_q.ap(), Qs.ap()[0]), inc="DG")
        P.emit("g", lambda g: g.dma_start(d_k.ap(), Ks.ap()[0]), inc="DG")
        P.emit("g", lambda g: g.dma_start(d_xb.ap(), xb16.ap()[0]), inc="DG")
        P.emit("g", lambda g: g.dma_start(d_xy.ap(), XY.ap()), inc="DG")
        P.emit("g", lambda g: g.dma_start(d_vs.ap(), VS.ap()), inc="DG")
        P.emit("g", lambda g: g.dma_start(d_ysh.ap(), Ysh.ap()), inc="DG")
        P.q["y"].append(lambda eng, v=ch["DG"].n: eng.wait_ge(sems["DG"], v))

    final_ds = ch["DSO"].n
    P.q["y"].append(lambda eng: eng.wait_ge(sems["DSO"], final_ds))

    with nc.Block() as blk:
        P.run(blk)
    es.close()
    return nc


_cache = {}


def kernel(L0_attn_x, L0_attn_w_attn, L0_attn_w_proj, L0_ffn_W_r, L0_ffn_W_k,
           L0_ffn_W_v, L0_ffn_mu_r, L0_ffn_mu_k):
    x = np.ascontiguousarray(np.asarray(L0_attn_x, dtype=np.float32))
    wa = np.ascontiguousarray(np.asarray(L0_attn_w_attn, dtype=np.float32))
    wp = np.ascontiguousarray(np.asarray(L0_attn_w_proj, dtype=np.float32))
    wr = np.ascontiguousarray(np.asarray(L0_ffn_W_r, dtype=np.float32))
    wk = np.ascontiguousarray(np.asarray(L0_ffn_W_k, dtype=np.float32))
    wv = np.ascontiguousarray(np.asarray(L0_ffn_W_v, dtype=np.float32))
    mur = np.ascontiguousarray(
        np.asarray(L0_ffn_mu_r, dtype=np.float32).reshape(8, 128).T)
    muk = np.ascontiguousarray(
        np.asarray(L0_ffn_mu_k, dtype=np.float32).reshape(8, 128).T)
    qaug, kaug, maskc, identc, biasc, _ = host_consts()

    import os
    _ph = float(os.environ.get("KPHASES", "3"))
    if _ph not in _cache:
        _cache[_ph] = build_nc(_ph)
    nc = _cache[_ph]

    in_maps = []
    for core in range(8):
        in_maps.append({
            "x_in": np.ascontiguousarray(x[2 * core:2 * core + 2]),
            "w_attn": wa, "w_proj": wp, "w_r": wr, "w_k": wk, "w_v": wv,
            "mu_r": mur, "mu_k": muk,
            "qaug": qaug, "kaug": kaug, "maskc": maskc, "identc": identc,
            "biascc": biasc,
        })
    res = run_bass_kernel_spmd(nc, in_maps, core_ids=list(range(8)))
    return np.concatenate([r["y_out"] for r in res.results], axis=0)


# version 08
# speedup vs baseline: 1.0246x; 1.0246x over previous
import math
import numpy as np
import ml_dtypes

import concourse.bass as bass
import concourse.mybir as mybir
from concourse.bass_utils import run_bass_kernel_spmd

F32 = mybir.dt.float32
BF16 = mybir.dt.bfloat16
I32 = mybir.dt.int32
AF = mybir.ActivationFunctionType
ALU = mybir.AluOpType
AX = mybir.AxisListType

B = 2
T = 2048
C = 1024
H = 16
D = 64
NCC = 8
NTT = 16
SEG = 512
ISQRT8 = float(2.0 ** -1.5)
EPS = 1e-5
NEG = -1.0e30
MAGIC = 0x5F3759DF

def _cut_table():
    cuts = []
    for h in range(H):
        sl = 2.0 ** (-(h + 1) / 2.0)
        c = 1 + math.ceil(110.0 / (128.0 * sl))
        cuts.append(min(16, c))
    return cuts

CUT = _cut_table()


def _slopes_f32():
    start = 2.0 ** (-(2.0 ** (-(math.log2(H) - 3))))
    return [np.float32(start * (start ** i)) for i in range(H)]


def host_consts():
    bf = ml_dtypes.bfloat16
    slopes = _slopes_f32()
    pos = np.arange(T)
    qf = (pos % 128).astype(np.float32)
    qh = ((pos // 128) * 128).astype(np.float32)
    kp = qf
    qaug = np.zeros((H, 7, T), dtype=bf)
    kaug = np.zeros((H, 7, T), dtype=bf)
    for h in range(H):
        s = slopes[h]
        s_hi = np.float32(bf(s))
        s_lo = np.float32(bf(np.float32(s - s_hi)))
        qaug[h, 0, :] = bf(0.0)
        qaug[h, 1, :] = bf(s_hi)
        qaug[h, 2, :] = bf(s_lo)
        qaug[h, 3, :] = qf.astype(bf)
        qaug[h, 4, :] = qf.astype(bf)
        qaug[h, 5, :] = qh.astype(bf)
        qaug[h, 6, :] = qh.astype(bf)
        kaug[h, 0, :] = bf(1.0)
        kaug[h, 1, :] = kp.astype(bf)
        kaug[h, 2, :] = kp.astype(bf)
        kaug[h, 3, :] = bf(-s_hi)
        kaug[h, 4, :] = bf(-s_lo)
        kaug[h, 5, :] = bf(-s_hi)
        kaug[h, 6, :] = bf(-s_lo)
    kk = np.arange(128)[:, None]
    qq = np.arange(128)[None, :]
    mask = np.where(kk > qq, np.float32(NEG), np.float32(0.0)).astype(np.float32)
    ident = np.eye(128, dtype=np.float32).astype(bf)
    biasc = np.zeros((128, 272), dtype=np.float32)
    for h in range(H):
        se = eff_slope(h)
        for j in range(16):
            biasc[:, h * 16 + j] = np.float32(se * 128.0 * j)
    biasc[:, 256] = np.float32(0.5)
    biasc[:, 257] = np.float32(np.float32(0.5) ** np.float32(0.5))
    return qaug, kaug, mask, ident, biasc, slopes


def eff_slope(h):
    bf = ml_dtypes.bfloat16
    s = _slopes_f32()[h]
    s_hi = np.float32(bf(s))
    s_lo = np.float32(bf(np.float32(s - s_hi)))
    return float(np.float32(s_hi + s_lo))


class Chan:
    def __init__(self, sem, step):
        self.sem = sem
        self.n = 0
        self.step = step

    def bump(self):
        self.n += self.step
        return self.n


class Plan:

    ENGS = ("t", "v", "s", "y", "g")

    def __init__(self, nc, sems):
        self.nc = nc
        self.q = {e: [] for e in self.ENGS}
        self.ch = {}
        for nm in sems:
            self.ch[nm] = Chan(sems[nm], 1 if nm.startswith("P") else 16)
        self._waited = {e: {} for e in self.ENGS}

    def wait(self, e, cname, tok):
        if tok is None or tok <= 0:
            return
        w = self._waited[e]
        if w.get(cname, 0) >= tok:
            return
        w[cname] = tok
        sem = self.ch[cname].sem
        self.q[e].append(lambda eng: eng.wait_ge(sem, tok))

    def emit(self, e, fn, inc=None, dr=None):
        if dr is None:
            dr = e in ("v", "s") and inc is None
        if inc is None:
            if dr:
                def thunk0(eng):
                    fn(eng)
                    eng.drain()
                self.q[e].append(thunk0)
            else:
                self.q[e].append(lambda eng: fn(eng))
            return None
        ch = self.ch[inc]
        tok = ch.bump()
        sem, step = ch.sem, ch.step

        def thunk(eng):
            fn(eng).then_inc(sem, step)
            if dr:
                eng.drain()

        self.q[e].append(thunk)
        return tok

    def run(self, block):
        block.tensor(lambda eng: [f(eng) for f in self.q["t"]])
        block.vector(lambda eng: [f(eng) for f in self.q["v"]])
        block.scalar(lambda eng: [f(eng) for f in self.q["s"]])
        block.sync(lambda eng: [f(eng) for f in self.q["y"]])
        block.gpsimd(lambda eng: [f(eng) for f in self.q["g"]])


def build_nc(phases=3):
    import os as _os
    KDUMP = int(_os.environ.get("KDUMP", "0"))
    KB = int(_os.environ.get("KB", str(B)))
    nc = bass.Bass("TRN2", target_bir_lowering=False,
                   dynamic_dma_scratch_size=12288)

    x_in = nc.dram_tensor("x_in", [B, T, C], F32, kind="ExternalInput")
    w_attn = nc.dram_tensor("w_attn", [3 * C, C], F32, kind="ExternalInput")
    w_proj = nc.dram_tensor("w_proj", [C, C], F32, kind="ExternalInput")
    w_r = nc.dram_tensor("w_r", [C, C], F32, kind="ExternalInput")
    w_k = nc.dram_tensor("w_k", [C, C], F32, kind="ExternalInput")
    w_v = nc.dram_tensor("w_v", [C, C], F32, kind="ExternalInput")
    mu_r = nc.dram_tensor("mu_r", [128, 8], F32, kind="ExternalInput")
    mu_k = nc.dram_tensor("mu_k", [128, 8], F32, kind="ExternalInput")
    qaug = nc.dram_tensor("qaug", [H, 7, T], BF16, kind="ExternalInput")
    kaug = nc.dram_tensor("kaug", [H, 7, T], BF16, kind="ExternalInput")
    maskc = nc.dram_tensor("maskc", [128, 128], F32, kind="ExternalInput")
    identc = nc.dram_tensor("identc", [128, 128], BF16, kind="ExternalInput")
    biascc = nc.dram_tensor("biascc", [128, 272], F32, kind="ExternalInput")
    y_out = nc.dram_tensor("y_out", [B, T, C], F32, kind="ExternalOutput")

    wattn_b = nc.dram_tensor("wattn_b", [3 * C, C], BF16, kind="Internal")
    wproj_b = nc.dram_tensor("wproj_b", [C, C], BF16, kind="Internal")
    wr_b = nc.dram_tensor("wr_b", [C, C], BF16, kind="Internal")
    wk_b = nc.dram_tensor("wk_b", [C, C], BF16, kind="Internal")
    wv_b = nc.dram_tensor("wv_b", [C, C], BF16, kind="Internal")
    xb16 = nc.dram_tensor("xb16", [B, T, C], BF16, kind="Internal")
    Qs = nc.dram_tensor("Qs", [B, H, 71, T], BF16, kind="Internal")
    Ks = nc.dram_tensor("Ks", [B, H, 71, T], BF16, kind="Internal")
    Ysh = nc.dram_tensor("Ysh", [8, 64, T], BF16, kind="Internal")
    ElRaw = nc.dram_tensor("ElRaw", [H, T], BF16, kind="Internal")
    ElRec = nc.dram_tensor("ElRec", [H, T], BF16, kind="Internal")

    AW = nc.alloc_sbuf_tensor("AW", [128, 16384], BF16)
    BW = nc.alloc_sbuf_tensor("BW", [128, 32768], BF16)
    XY = nc.alloc_sbuf_tensor("XY", [128, 16384], BF16)
    VS = nc.alloc_sbuf_tensor("VS", [128, 16 * 1040], BF16)
    qt = [nc.alloc_sbuf_tensor(f"qt{i}", [128, T], BF16) for i in range(2)]
    kt = [nc.alloc_sbuf_tensor(f"kt{i}", [128, T], BF16) for i in range(2)]
    qkev = [nc.alloc_sbuf_tensor(f"qkev{i}", [128, T], BF16) for i in range(2)]
    expb = [nc.alloc_sbuf_tensor(f"expb{i}", [128, SEG], BF16) for i in range(3)]
    yun = nc.alloc_sbuf_tensor("yun", [128, T], BF16)
    sqb = nc.alloc_sbuf_tensor("sqb", [128, T], BF16)
    x1b = nc.alloc_sbuf_tensor("x1b", [128, C], BF16)
    x1T = nc.alloc_sbuf_tensor("x1T", [128, C], BF16)
    xpT = nc.alloc_sbuf_tensor("xpT", [128, C], BF16)
    xrT = nc.alloc_sbuf_tensor("xrT", [128, C], BF16)
    xkT = nc.alloc_sbuf_tensor("xkT", [128, C], BF16)
    kaT = nc.alloc_sbuf_tensor("kaT", [128, C], BF16)
    mask_sb = nc.alloc_sbuf_tensor("mask_sb", [128, 128], F32)
    biasc_sb = nc.alloc_sbuf_tensor("biasc_sb", [128, 272], F32)
    id_sb = nc.alloc_sbuf_tensor("id_sb", [128, 128], BF16)
    ones_sb = nc.alloc_sbuf_tensor("ones_sb", [128, 64], BF16)
    mur_sb = nc.alloc_sbuf_tensor("mur_sb", [128, 8], F32)
    muk_sb = nc.alloc_sbuf_tensor("muk_sb", [128, 8], F32)
    SM2 = nc.alloc_sbuf_tensor("SM2", [128, 32], F32)
    SMI = nc.alloc_sbuf_tensor("SMI", [128, 4], I32)
    lastc = nc.alloc_sbuf_tensor("lastc", [128, 8], BF16)
    ellp = nc.alloc_sbuf_tensor("ellp", [128, 16], BF16)
    ellr = nc.alloc_sbuf_tensor("ellr", [128, 16], F32)

    PS = nc.alloc_psum_tensor("PS", [128, 4096], F32)

    def awf(lo, hi):
        return AW.ap()[:, lo:hi].bitcast(F32)

    x3f = [awf(0, 2048), awf(2048, 4096)]
    sx1 = awf(4096, 6144)
    rtf = [awf(6144, 8192), awf(8192, 10240)]
    outb = [awf(10240, 12288), awf(12288, 14336)]
    ktmp = awf(14336, 16384)

    ssum = SM2.ap()[:, 0:1]
    ssq = SM2.ap()[:, 1:2]
    negm = SM2.ap()[:, 2:3]
    msq = SM2.ap()[:, 3:4]
    var = SM2.ap()[:, 4:5]
    rstd = SM2.ap()[:, 5:6]
    hv = SM2.ap()[:, 6:7]
    nt1 = SM2.ap()[:, 7:8]
    nt2 = SM2.ap()[:, 8:9]
    mkseg = lambda i: SM2.ap()[64:65, 16 + i:17 + i]
    mk2 = SM2.ap()[64:65, 20:21]
    ngmk = SM2.ap()[64:65, 21:22]

    sems = {}
    import contextlib
    es = contextlib.ExitStack()
    for nm in ("PT", "PV", "PS", "DYW", "DYQ", "DYE", "DYR", "DYS", "DY3",
               "DSQ", "DSE", "DSY", "DSO", "DG", "DGX", "DGR"):
        sems[nm] = es.enter_context(nc.semaphore(nm))

    P = Plan(nc, sems)
    ch = P.ch

    P.emit("g", lambda g: g.dma_start(wattn_b.ap(), w_attn.ap()), inc="DG")
    P.emit("g", lambda g: g.dma_start(wproj_b.ap(), w_proj.ap()), inc="DG")
    P.emit("g", lambda g: g.dma_start(wr_b.ap(), w_r.ap()), inc="DG")
    P.emit("g", lambda g: g.dma_start(wk_b.ap(), w_k.ap()), inc="DG")
    P.emit("g", lambda g: g.dma_start(wv_b.ap(), w_v.ap()), inc="DG")
    tok_wcast = ch["DG"].n
    for b in range(B):
        for h in range(H):
            P.emit("g", lambda g, b=b, h=h: g.dma_start(
                Qs.ap()[b, h, 64:71, :], qaug.ap()[h]), inc="DG")
            P.emit("g", lambda g, b=b, h=h: g.dma_start(
                Ks.ap()[b, h, 64:71, :], kaug.ap()[h]), inc="DG")
    tok_aug = ch["DG"].n
    P.emit("g", lambda g: g.dma_start(mask_sb.ap(), maskc.ap()), inc="DG")
    P.emit("g", lambda g: g.dma_start(id_sb.ap(), identc.ap()), inc="DG")
    P.emit("g", lambda g: g.dma_start(mur_sb.ap(), mu_r.ap()), inc="DG")
    P.emit("g", lambda g: g.dma_start(muk_sb.ap(), mu_k.ap()), inc="DG")
    P.emit("g", lambda g: g.dma_start(biasc_sb.ap(), biascc.ap()), inc="DG")
    tok_consts = ch["DG"].n
    _reg = set()

    def _regconst(val, col):
        if val in _reg or val == 0.0:
            return
        _reg.add(val)
        nc.const_aps.register(F32, val, biasc_sb.ap()[:, col:col + 1])

    for _h in range(H):
        _se = eff_slope(_h)
        for _j in range(16):
            _regconst(float(np.float32(_se * 128.0 * _j)), _h * 16 + _j)
    _regconst(0.5, 256)
    _regconst(float(np.float32(0.5 ** 0.5)), 257)
    tok_xb = [None] * B
    for b in range(B):
        tok_xb[b] = P.emit("g", lambda g, b=b: g.dma_start(
            xb16.ap()[b], x_in.ap()[b]), inc="DGX")

    P.emit("v", lambda v: v.memset(ones_sb.ap(), 1.0))
    P.emit("v", lambda v: v.memset(SMI.ap()[:, 0:1], MAGIC))
    vs_ones = VS.ap().rearrange("p (g x) -> p g x", x=65)[:, :, 64:65]
    tok_vinit = P.emit("v", lambda v: v.memset(vs_ones, 1.0), inc="PV")

    def wslab(dst_ap, src, r0, c0):
        return lambda y: y.dma_start(dst_ap, src.ap()[r0:r0 + 1024, c0:c0 + 128],
                                     transpose=True)

    def load_wq():
        P.wait("y", "DG", tok_wcast)
        for c in range(NCC):
            P.emit("y", wslab(AW.ap()[:, c * 1024:(c + 1) * 1024],
                              wattn_b, 0, c * 128), inc="DYW")
        return ch["DYW"].n

    def load_wk_slot1():
        for c in range(NCC):
            P.emit("y", wslab(AW.ap()[:, 8192 + c * 1024:8192 + (c + 1) * 1024],
                              wattn_b, 1024, c * 128), inc="DYW")
        return ch["DYW"].n

    def load_wv_slot0():
        for c in range(NCC):
            P.emit("y", wslab(AW.ap()[:, c * 1024:(c + 1) * 1024],
                              wattn_b, 2048, c * 128), inc="DYW")
        return ch["DYW"].n

    def load_bw():
        for i, wsrc in enumerate((wproj_b, wr_b, wk_b, wv_b)):
            for c in range(NCC):
                P.emit("y", wslab(
                    BW.ap()[:, i * 8192 + c * 1024:i * 8192 + (c + 1) * 1024],
                    wsrc, 0, c * 128), inc="DYW")
        return ch["DYW"].n

    def load_xT(b):
        P.wait("y", "DGX", tok_xb[b])
        for c in range(NCC):
            P.emit("y", lambda y, b=b, c=c: y.dma_start(
                XY.ap()[:, c * T:(c + 1) * T],
                xb16.ap()[b][:, c * 128:(c + 1) * 128], transpose=True),
                inc="DYW")
        return ch["DYW"].n

    def aw_q(c):
        return AW.ap()[:, c * 1024:(c + 1) * 1024]

    def aw_k(c):
        return AW.ap()[:, 8192 + c * 1024:8192 + (c + 1) * 1024]

    def bw(i, c):
        return BW.ap()[:, i * 8192 + c * 1024:i * 8192 + (c + 1) * 1024]

    def xT(c):
        return XY.ap()[:, c * T:(c + 1) * T]

    yT = xT

    tok_wq = load_wq()
    tok_wk = load_wk_slot1()
    tok_xT = [None] * B
    tok_xT[0] = load_xT(0)
    tok_bw = load_bw()

    tok_p3_done_t = 0
    tok_p3_done_v = 0
    tok_p3_done_s = 0
    tok_p3_out_d = 0
    slopes_eff = [eff_slope(h) for h in range(H)]
    tok_s_tile = [0] * NTT
    tok_tp_evac = 0
    tok_ffn = {0: 0, 1: 0}
    tok_out = {0: 0, 1: 0}
    tok_out_dma = {0: 0, 1: 0}
    tok_sq2 = 0
    tok_bl_prev = 0
    tok_mm2_prev = 0
    tok_mm3_prev = 0
    st2 = {
        "exp_by_bank": {0: 0, 1: 0},
        "sbank": 0,
        "expn": 0,
        "av_by_expb": {0: 0, 1: 0, 2: 0},
        "ye": 0,
        "rell": 0,
        "norm": 0,
        "yun_free_ds": 0,
    }

    for b in range(KB):
        tok_qs_w = {}
        grp_evac = {0: 0, 1: 0}
        if b > 0:
            P.wait("y", "PT", tok_p3_done_t)
            P.wait("y", "PV", tok_p3_done_v)
            P.wait("y", "PS", tok_p3_done_s)
            P.wait("y", "DSO", tok_p3_out_d)
            tok_wq = load_wq()
            tok_wk = load_wk_slot1()
            tok_xT[b] = load_xT(b)

        last_qk_mm = 0
        tok_p1_evac_last = 0
        _wl = []
        if phases >= 0.5:
            _wl.append(("q", aw_q, Qs))
        if phases >= 0.7:
            _wl.append(("k", aw_k, Ks))
        for which, wsel, scratch in _wl:
            P.wait("t", "DYW", tok_wq if which == "q" else tok_wk)
            P.wait("t", "DYW", tok_xT[b])
            if b > 0:
                P.wait("t", "PV", tok_p3_done_v)
                P.wait("t", "PS", tok_p3_done_s)
            for dt in range(8):
                gi = dt % 2
                grp = gi * 2048
                P.wait("t", "PV", grp_evac[gi])
                tok_mm = None
                for cc in range(NCC):
                    for sgi in range(4):
                        tok_mm = P.emit("t", lambda t, w=wsel(cc), cc=cc, dt=dt,
                                        sgi=sgi, grp=grp: t.matmul(
                            PS.ap()[:, grp + sgi * 512:grp + (sgi + 1) * 512],
                            w[:, dt * 128:(dt + 1) * 128],
                            xT(cc)[:, sgi * 512:(sgi + 1) * 512],
                            start=(cc == 0), stop=(cc == NCC - 1)),
                            inc="PT" if (cc == NCC - 1 and sgi == 3) else None)
                last_qk_mm = tok_mm
                ev = qkev[gi]
                prev = tok_qs_w.get((which, dt - 2)) or (
                    tok_qs_w.get(("q", dt + 6)) if which == "k" else None)
                P.wait("v", "DSQ", prev)
                P.wait("v", "PT", tok_mm)
                tok_ev = None
                for sgi in range(4):
                    tok_ev = P.emit("v", lambda v, ev=ev, grp=grp, sgi=sgi:
                                    v.tensor_scalar(
                        ev.ap()[:, sgi * 512:(sgi + 1) * 512],
                        PS.ap()[:, grp + sgi * 512:grp + (sgi + 1) * 512],
                        ISQRT8, None, ALU.mult),
                        inc="PV" if sgi == 3 else None,
                        dr=False if sgi < 3 else None)
                grp_evac[gi] = tok_ev
                tok_p1_evac_last = tok_ev
                P.wait("s", "PV", tok_ev)
                P.wait("s", "DG", tok_aug)
                tk = None
                for half in range(2):
                    tk = P.emit("s", lambda s, ev=ev, b=b, dt=dt, half=half,
                                scratch=scratch: s.dma_start(
                        scratch.ap()[b, 2 * dt + half, 0:64, :],
                        ev.ap()[64 * half:64 * half + 64, :]), inc="DSQ",
                        dr=False)
                tok_qs_w[(which, dt)] = tk

        if phases < 1:
            continue
        P.wait("y", "PT", last_qk_mm)
        tok_wv = load_wv_slot0()
        P.wait("t", "DYW", tok_wv)
        P.wait("t", "PV", grp_evac[0])
        vgrp_evac = {0: 0, 1: 0}
        tok_v_evac = 0
        for tt in range(NTT):
            gi = tt % 2
            grp = gi * 1024
            P.wait("t", "PV", vgrp_evac[gi])
            tok_mm = None
            for cc in range(NCC):
                for sgi in range(2):
                    tok_mm = P.emit("t", lambda t, cc=cc, tt=tt, sgi=sgi, grp=grp:
                                    t.matmul(
                        PS.ap()[:, grp + sgi * 512:grp + (sgi + 1) * 512],
                        xT(cc)[:, tt * 128:(tt + 1) * 128],
                        aw_q(cc)[:, sgi * 512:(sgi + 1) * 512],
                        start=(cc == 0), stop=(cc == NCC - 1)),
                        inc="PT" if (cc == NCC - 1 and sgi == 1) else None)
            P.wait("v", "PT", tok_mm)
            P.wait("v", "PV", tok_vinit)
            src = PS.ap()[:, grp:grp + 1024].rearrange("p (h e) -> p h e", e=64)
            dst = VS.ap()[:, tt * 1040:(tt + 1) * 1040].rearrange(
                "p (h e) -> p h e", e=65)[:, :, 0:64]
            tok_v_evac = P.emit("v", lambda v, src=src, dst=dst:
                                v.tensor_copy(dst, src), inc="PV")
            vgrp_evac[gi] = tok_v_evac

        if phases < 2:
            tok_p3_done_t = ch["PT"].n
            tok_p3_done_v = ch["PV"].n
            tok_p3_done_s = ch["PS"].n
            tok_p3_out_d = ch["DSO"].n
            continue
        Yps = PS.ap()[0:65, 0:2048]
        Yl = PS.ap()[64:65, 0:2048]
        Sb = [PS.ap()[:, 2048 + 512 * i:2048 + 512 * (i + 1)] for i in range(2)]
        nrm = PS.ap()[64:65, 3072:3584]
        bcb = PS.ap()[0:64, 3584:4096]

        tok_qk_load = {}
        tok_ub = {}
        tok_scores_last = {}

        def p2_load(h):
            buf = h % 2
            P.wait("y", "DSQ", ch["DSQ"].n if h == 0 else
                   max(tok_qs_w[("q", h // 2)], tok_qs_w[("k", h // 2)]))
            P.wait("y", "PT", tok_scores_last.get(h - 2))
            P.wait("y", "PV", tok_ub.get(h - 2))
            P.emit("y", lambda y, b=b, h=h, buf=buf: y.dma_start(
                kt[buf].ap()[0:71, :], Ks.ap()[b, h]), inc="DYQ")
            tok_qk_load[h] = P.emit("y", lambda y, b=b, h=h, buf=buf: y.dma_start(
                qt[buf].ap()[0:71, :], Qs.ap()[b, h]), inc="DYQ")

        def p2_prep(h):
            buf = h % 2
            P.wait("v", "DYQ", tok_qk_load[h])
            tk = P.emit("v", lambda v, buf=buf: v.tensor_mul(
                sqb.ap()[0:64, :], kt[buf].ap()[0:64, :], kt[buf].ap()[0:64, :]),
                inc="PV")
            P.wait("t", "PV", tok_vinit)
            if h == 0:
                P.wait("t", "PV", tok_p1_evac_last)
            last_red = 0
            for sgi in range(4):
                P.wait("t", "PV", tk)
                P.wait("t", "PV", last_red)
                tm = P.emit("t", lambda t, sgi=sgi: t.matmul(
                    nrm, ones_sb.ap()[0:64, 0:1],
                    sqb.ap()[0:64, sgi * 512:(sgi + 1) * 512],
                    start=True, stop=True), inc="PT")
                yield
                P.wait("v", "PT", tm)
                last_red = P.emit("v", lambda v, sgi=sgi: v.reduce_max(
                    mkseg(sgi), nrm, AX.X), inc="PV", dr=True)
            P.wait("v", "PT", ch["PT"].n)
            tkq = P.emit("v", lambda v, buf=buf: v.tensor_mul(
                sqb.ap()[0:64, :], qt[buf].ap()[0:64, :], qt[buf].ap()[0:64, :]),
                inc="PV")
            P.emit("v", lambda v: v.reduce_max(mk2, SM2.ap()[64:65, 16:20], AX.X))
            tmk = P.emit("v", lambda v: v.tensor_scalar(
                ngmk, mk2, -0.5, None, ALU.mult), inc="PV")
            lastu = 0
            for sgi in range(4):
                P.wait("t", "PV", tkq)
                P.wait("t", "PV", lastu)
                tm = P.emit("t", lambda t, sgi=sgi: t.matmul(
                    nrm, ones_sb.ap()[0:64, 0:1],
                    sqb.ap()[0:64, sgi * 512:(sgi + 1) * 512],
                    start=True, stop=True), inc="PT")
                yield
                P.wait("v", "PT", tm)
                P.wait("v", "PV", tmk)
                lastu = P.emit("v", lambda v, buf=buf, sgi=sgi: v.tensor_scalar(
                    qt[buf].ap()[64:65, sgi * 512:(sgi + 1) * 512],
                    nrm, -0.5, ngmk, ALU.mult, ALU.add), inc="PV")
            tok_ub[h] = lastu

        def p2_dump_qt(h):
            if not (KDUMP == 2 and b == 0 and h == int(_os.environ.get("KH", "1"))):
                return
            buf = h % 2
            d_qt = nc.dram_tensor("d_qt", [128, T], BF16, kind="ExternalOutput")
            d_kt = nc.dram_tensor("d_kt", [128, T], BF16, kind="ExternalOutput")
            P.wait("s", "PV", tok_ub[h])
            P.emit("s", lambda s, buf=buf: s.dma_start(d_qt.ap(), qt[buf].ap()),
                   inc="DSO", dr=False)
            P.emit("s", lambda s, buf=buf: s.dma_start(d_kt.ap(), kt[buf].ap()),
                   inc="DSO", dr=False)

        def p2_head(h, first_of_batch, prep_gen=None, fin_gen=None,
                    do_av=True, do_fin=True):
            buf = h % 2
            cut = CUT[h]
            sl = slopes_eff[h]
            firstj = {}
            lastj = {}
            for j in range(NTT):
                qmax = min(T, 128 * (j + cut))
                for bank in range((128 * j) // 512, (qmax + 511) // 512):
                    firstj.setdefault(bank, j)
                    lastj[bank] = j
            tok_av_last = 0
            for j in range(NTT):
                if prep_gen is not None:
                    next(prep_gen, None)
                if fin_gen is not None:
                    next(fin_gen, None)
                qmax = min(T, 128 * (j + cut))
                q0 = 128 * j
                bounds = [q0]
                nb = (q0 // 512 + 1) * 512
                while nb < qmax:
                    bounds.append(nb)
                    nb += 512
                bounds.append(qmax)
                for si in range(len(bounds) - 1):
                    qa, qb = bounds[si], bounds[si + 1]
                    sbk = st2["sbank"]
                    sb = Sb[sbk]
                    P.wait("t", "PV", tok_ub[h])
                    P.wait("t", "PS", st2["exp_by_bank"][sbk])
                    if first_of_batch and j == 0 and si == 0:
                        P.wait("t", "PV", tok_p1_evac_last)
                    tok_sc = P.emit("t", lambda t, buf=buf, j=j, qa=qa, qb=qb,
                                    sb=sb: t.matmul(
                        sb[:, 0:qb - qa], kt[buf].ap()[0:71, j * 128:(j + 1) * 128],
                        qt[buf].ap()[0:71, qa:qb], start=True, stop=True),
                        inc="PT")
                    tok_scores_last[h] = tok_sc
                    tok_mask = None
                    if si == 0:
                        P.wait("v", "DG", tok_consts)
                        P.wait("v", "PT", tok_sc)
                        tok_mask = P.emit("v", lambda v, sb=sb: v.tensor_add(
                            sb[:, 0:128], sb[:, 0:128], mask_sb.ap()), inc="PV")
                    ei = st2["expn"] % 3
                    eb = expb[ei]
                    st2["expn"] += 1
                    if tok_mask is not None:
                        P.wait("s", "PV", tok_mask)
                    P.wait("s", "DG", tok_consts)
                    P.wait("s", "PT", tok_sc)
                    P.wait("s", "PT", st2["av_by_expb"][ei])
                    bias = float(np.float32(sl * 128.0 * j))
                    tok_exp = P.emit("s", lambda s, sb=sb, eb=eb, qa=qa, qb=qb,
                                     bias=bias: s.activation(
                        eb.ap()[:, 0:qb - qa], sb[:, 0:qb - qa], AF.Exp,
                        bias=bias), inc="PS")
                    st2["exp_by_bank"][sbk] = tok_exp
                    st2["sbank"] ^= 1
                    if not do_av:
                        continue
                    P.wait("t", "PS", tok_exp)
                    if first_of_batch and j == 0 and si == 0:
                        P.wait("t", "PV", tok_v_evac)
                    if j == 0 and si == 0:
                        P.wait("t", "PS", st2["ye"])
                        P.wait("t", "PV", st2["rell"])
                    bk0 = qa // 512
                    bk1 = (qb - 1) // 512
                    for bank in range(bk0, bk1 + 1):
                        ca = max(qa, bank * 512)
                        cb = min(qb, (bank + 1) * 512)
                        stf = (firstj[bank] == j)
                        spf = (lastj[bank] == j)
                        tok_av_last = P.emit("t", lambda t, h=h, j=j, ca=ca,
                                             cb=cb, qa=qa, eb=eb, stf=stf,
                                             spf=spf: t.matmul(
                            Yps[:, ca:cb],
                            VS.ap()[:, j * 1040 + h * 65:j * 1040 + h * 65 + 65],
                            eb.ap()[:, ca - qa:cb - qa], start=stf, stop=spf),
                            inc="PT")
                    st2["av_by_expb"][ei] = tok_av_last
            return tok_av_last

        def p2_fin(h, tok_av_last, first_of_batch):
            import os as _os
            KFIN = int(_os.environ.get("KFIN", "31"))
            pair = h // 2
            odd = h % 2
            P.wait("v", "PT", tok_av_last)
            tok_rell = None
            tok_rell_done = None
            if KFIN & 1:
                tok_rell = P.emit("v", lambda v: v.tensor_copy(
                    sqb.ap()[64:65, 0:2048], Yl), inc="PV")
                P.wait("s", "PV", tok_rell)
                t1 = P.emit("s", lambda s, h=h: s.dma_start(
                    ElRaw.ap()[h:h + 1, :], sqb.ap()[64:65, 0:2048]),
                    inc="DSE", dr=False)
                P.wait("y", "DSE", t1)
                P.wait("y", "PV", st2.get("ellp_free", 0))
                t2 = P.emit("y", lambda y, h=h: y.dma_start(
                    ellp.ap(), ElRaw.ap()[h].rearrange("(p c) -> p c", p=128)),
                    inc="DYE")
                P.wait("v", "DYE", t2)
                tok_rec = P.emit("v", lambda v: v.reciprocal(
                    ellr.ap(), ellp.ap()), inc="PV", dr=True)
                st2["ellp_free"] = tok_rec
                P.wait("g", "PV", tok_rec)
                t3 = P.emit("g", lambda g, h=h: g.dma_start(
                    ElRec.ap()[h].rearrange("(p c) -> p c", p=128), ellr.ap()),
                    inc="DGR", dr=False)
                P.wait("y", "DGR", t3)
                tok_rell_done = P.emit("y", lambda y, h=h: y.dma_start(
                    sqb.ap()[64:65, 0:2048], ElRec.ap()[h:h + 1, :]), inc="DYR")
            tok_ye = None
            if KFIN & 2:
                P.wait("s", "PT", tok_av_last)
                P.wait("s", "PV", st2["norm"])
                P.wait("s", "DSY", st2.get("yun_free_ds", 0))
                tok_ye = P.emit("s", lambda s: s.activation(
                    yun.ap()[0:64, :], PS.ap()[0:64, 0:2048], AF.Copy), inc="PS")
            if KDUMP == 2 and b == 0 and h == int(_os.environ.get("KH", "1")):
                d_yun = nc.dram_tensor("d_yun", [64, T], BF16, kind="ExternalOutput")
                P.wait("s", "PS", tok_ye)
                P.emit("s", lambda s: s.dma_start(d_yun.ap(), yun.ap()[0:64, :]),
                       inc="DSO", dr=False)
                d_rell = nc.dram_tensor("d_rell", [1, T], BF16, kind="ExternalOutput")
                P.wait("s", "PV", tok_rell)
                P.emit("s", lambda s: s.dma_start(d_rell.ap(), sqb.ap()[64:65, :]),
                       inc="DSO", dr=False)
            if tok_ye:
                st2["ye"] = tok_ye
            if tok_rell:
                st2["rell"] = tok_rell
            tok_norm = 0
            for sgi in range(4):
                if not (KFIN & 4):
                    break
                yield
                P.wait("t", "DYR", tok_rell_done)
                if sgi == 0:
                    P.wait("t", "PV", st2["norm"])
                    if first_of_batch:
                        P.wait("t", "PV", tok_p1_evac_last)
                else:
                    P.wait("t", "PV", tok_norm)
                tbc = P.emit("t", lambda t, sgi=sgi: t.matmul(
                    bcb, ones_sb.ap()[64:65, 0:64],
                    sqb.ap()[64:65, sgi * 512:(sgi + 1) * 512],
                    start=True, stop=True), inc="PT")
                if not (KFIN & 8):
                    continue
                P.wait("v", "PT", tbc)
                P.wait("v", "PS", tok_ye)
                if odd:
                    tok_norm = P.emit("v", lambda v, sgi=sgi: v.tensor_mul(
                        yun.ap()[0:64, sgi * 512:(sgi + 1) * 512],
                        yun.ap()[0:64, sgi * 512:(sgi + 1) * 512], bcb),
                        inc="PV")
                else:
                    tok_norm = P.emit("v", lambda v, pair=pair, sgi=sgi:
                                      v.tensor_mul(
                        yT(pair)[0:64, sgi * 512:(sgi + 1) * 512],
                        yun.ap()[0:64, sgi * 512:(sgi + 1) * 512], bcb),
                        inc="PV")
            if (KFIN & 16) and odd:
                slot = pair
                P.wait("s", "PV", tok_norm)
                tw = P.emit("s", lambda s, slot=slot: s.dma_start(
                    Ysh.ap()[slot], yun.ap()[0:64, :]), inc="DSY", dr=False)
                P.wait("y", "DSY", tw)
                tr = P.emit("y", lambda y, pair=pair, slot=slot: y.dma_start(
                    XY.ap()[64:128, pair * T:(pair + 1) * T], Ysh.ap()[slot]),
                    inc="DYS")
                st2["ydma"] = tr
                tok_norm = None
                st2["yun_free_ds"] = tw
            if tok_norm:
                st2["norm"] = tok_norm

        p2_load(0)
        if phases >= 1.5:
            for _ in p2_prep(0):
                pass
        fin_prev = None
        for h in range(H):
            if h + 1 < H:
                p2_load(h + 1)
            if phases < 1.5:
                continue
            gen_next = p2_prep(h + 1) if h + 1 < H else None
            p2_dump_qt(h)
            if phases < 1.7:
                continue
            tok_avl = p2_head(h, first_of_batch=(h == 0), prep_gen=gen_next,
                              fin_gen=fin_prev, do_av=(phases >= 1.8),
                              do_fin=(phases >= 1.9))
            if gen_next is not None:
                for _ in gen_next:
                    pass
            if fin_prev is not None:
                for _ in fin_prev:
                    pass
            fin_prev = None
            if phases >= 1.9 and tok_avl is not None:
                fin_prev = p2_fin(h, tok_avl, first_of_batch=(h == 0))
                next(fin_prev, None)
        if fin_prev is not None:
            for _ in fin_prev:
                pass
        tok_norm_last = st2["norm"]
        tok_ydma_last = st2.get("ydma", 0)
        tok_ye_last = st2["ye"]

        if phases < 3:
            tok_p3_done_t = ch["PT"].n
            tok_p3_done_v = ch["PV"].n
            tok_p3_done_s = ch["PS"].n
            tok_p3_out_d = ch["DSO"].n
            continue
        P1p = PS.ap()[:, 0:1024]
        R1 = PS.ap()[:, 1024:2048]
        K2 = PS.ap()[:, 2048:3072]
        P3p = PS.ap()[:, 3072:4096]
        TpA = PS.ap()[:, 0:512].bitcast(BF16)

        tok_x3 = [None] * NTT
        P.wait("y", "PT", ch["PT"].n)

        def load_x3(tt):
            if tt >= 2:
                P.wait("y", "PV", tok_s_tile[tt - 2])
            tok_x3[tt] = P.emit("y", lambda y, b=b, tt=tt: y.dma_start(
                x3f[tt % 2], x_in.ap()[b, tt * 128:(tt + 1) * 128, :]),
                inc="DY3")

        load_x3(0)
        tok_lc = P.emit("v", lambda v: v.memset(lastc.ap(), 0.0), inc="PV")

        for tt in range(NTT):
            par = tt % 2
            if tt + 1 < NTT:
                load_x3(tt + 1)
            P.wait("t", "DYW", tok_bw)
            P.wait("t", "PV", tok_norm_last)
            P.wait("t", "DYS", tok_ydma_last)
            if tt == 0:
                P.wait("t", "PS", ch["PS"].n)
            else:
                P.wait("t", "PV", tok_tp_evac)
            tok_proj = None
            for dc in range(NCC):
                for sgi in range(2):
                    tok_proj = P.emit("t", lambda t, dc=dc, tt=tt, sgi=sgi:
                                      t.matmul(
                        P1p[:, sgi * 512:(sgi + 1) * 512],
                        yT(dc)[:, tt * 128:(tt + 1) * 128],
                        bw(0, dc)[:, sgi * 512:(sgi + 1) * 512],
                        start=(dc == 0), stop=(dc == NCC - 1)),
                        inc="PT" if (dc == NCC - 1 and sgi == 1) else None)
            P.wait("v", "PT", tok_proj)
            P.wait("v", "DY3", tok_x3[tt])
            tok_s = P.emit("v", lambda v, tt=tt: v.scalar_tensor_tensor(
                sx1, P1p, 1.0, x3f[tt % 2], ALU.mult, ALU.add, accum_out=ssum),
                inc="PV")
            tok_s_tile[tt] = tok_s
            P.wait("s", "PV", tok_s)
            P.wait("s", "DSO", tok_out_dma[par])
            if KDUMP == 3 and b == 0 and tt == 0:
                d_s = nc.dram_tensor("d_s", [128, C], F32, kind="ExternalOutput")
                tds = P.emit("s", lambda s: s.dma_start(d_s.ap(), sx1),
                             inc="DSO", dr=False)
                P.wait("s", "DSO", tds)
            tok_ssq = P.emit("s", lambda s, tt=tt: s.activation(
                outb[tt % 2], sx1, AF.Square, accum_out=ssq), inc="PS")

            def newton(tok_sq):
                P.wait("v", "PS", tok_sq)
                P.emit("v", lambda v: v.tensor_scalar(
                    negm, ssum, -1.0 / C, None, ALU.mult))
                P.emit("v", lambda v: v.tensor_mul(msq, negm, negm))
                P.emit("v", lambda v: v.tensor_scalar(
                    nt1, ssq, 1.0 / C, EPS, ALU.mult, ALU.add))
                P.emit("v", lambda v: v.tensor_sub(var, nt1, msq))
                P.emit("v", lambda v: v.tensor_scalar(
                    SMI.ap()[:, 1:2], var.bitcast(I32), 1, None,
                    ALU.arith_shift_right))
                P.emit("v", lambda v: v.scalar_tensor_tensor(
                    rstd.bitcast(I32), SMI.ap()[:, 0:1], 0, SMI.ap()[:, 1:2],
                    ALU.add, ALU.subtract))
                P.emit("v", lambda v: v.tensor_scalar(
                    hv, var, -0.5, None, ALU.mult))
                for _ in range(3):
                    P.emit("v", lambda v: v.tensor_mul(nt1, rstd, rstd))
                    P.emit("v", lambda v: v.tensor_scalar(
                        nt2, nt1, hv, 1.5, ALU.mult, ALU.add))
                    P.emit("v", lambda v: v.tensor_mul(rstd, rstd, nt2))

            newton(tok_ssq)
            tok_nmr = P.emit("v", lambda v: v.tensor_mul(nt1, negm, rstd),
                             inc="PV", dr=True)
            P.wait("s", "PV", tok_nmr)
            tok_x1 = P.emit("s", lambda s: s.activation(
                sx1, sx1, AF.Identity, bias=nt1, scale=rstd), inc="PS", dr=True)
            if KDUMP == 3 and b == 0 and tt == 0:
                d_x1 = nc.dram_tensor("d_x1", [128, C], F32, kind="ExternalOutput")
                P.wait("s", "PV", tok_x1)
                P.emit("s", lambda s: s.dma_start(d_x1.ap(), sx1), inc="DSO", dr=False)
            P.wait("s", "PV", tok_bl_prev)
            tok_x1b = P.emit("s", lambda s: s.activation(
                x1b.ap(), sx1, AF.Copy), inc="PS")
            P.wait("t", "PS", tok_x1b)
            P.wait("t", "PV", tok_s)
            P.wait("t", "DG", tok_consts)
            tok_tp = None
            for cc in range(NCC):
                tok_tp = P.emit("t", lambda t, cc=cc: t.transpose(
                    TpA[:, cc * 128:(cc + 1) * 128],
                    x1b.ap()[:, cc * 128:(cc + 1) * 128], id_sb.ap()),
                    inc="PT" if cc == NCC - 1 else None)
            P.wait("v", "PT", tok_tp)
            tok_tp_evac = P.emit("v", lambda v: v.tensor_copy(
                x1T.ap(), TpA), inc="PV")
            P.wait("s", "PV", tok_tp_evac)
            P.wait("s", "PV", tok_lc)
            src_sh = x1T.ap().rearrange("p (c x) -> p c x", x=128)[:, :, 0:127]
            dst_sh = xpT.ap().rearrange("p (c x) -> p c x", x=128)[:, :, 1:128]
            P.emit("s", lambda s: s.copy(dst_sh, src_sh))
            tok_xp = P.emit("s", lambda s: s.copy(
                xpT.ap().rearrange("p (c x) -> p c x", x=128)[:, :, 0:1],
                lastc.ap().rearrange("p (c o) -> p c o", o=1)), inc="PS")
            P.wait("v", "PS", tok_xp)
            tok_lc = P.emit("v", lambda v: v.tensor_copy(
                lastc.ap(),
                x1T.ap().rearrange("p (c x) -> p c x", x=128)[:, :, 127:128]
                .rearrange("p c x -> p (c x)")), inc="PV")
            tok_d = P.emit("v", lambda v: v.tensor_sub(
                x1b.ap(), x1T.ap(), xpT.ap()), inc="PV", dr=True)
            P.wait("v", "PT", tok_mm2_prev)
            P.wait("v", "DG", tok_consts)
            tok_bl = None
            for cc in range(NCC):
                P.emit("v", lambda v, cc=cc: v.scalar_tensor_tensor(
                    xrT.ap()[:, cc * 128:(cc + 1) * 128],
                    x1b.ap()[:, cc * 128:(cc + 1) * 128],
                    mur_sb.ap()[:, cc:cc + 1],
                    xpT.ap()[:, cc * 128:(cc + 1) * 128], ALU.mult, ALU.add),
                    dr=False)
                tok_bl = P.emit("v", lambda v, cc=cc: v.scalar_tensor_tensor(
                    xkT.ap()[:, cc * 128:(cc + 1) * 128],
                    x1b.ap()[:, cc * 128:(cc + 1) * 128],
                    muk_sb.ap()[:, cc:cc + 1],
                    xpT.ap()[:, cc * 128:(cc + 1) * 128], ALU.mult, ALU.add),
                    inc="PV" if cc == NCC - 1 else None,
                    dr=False if cc < NCC - 1 else None)
            tok_bl_prev = tok_bl
            if KDUMP == 3 and b == 0 and tt == 0:
                d_xr = nc.dram_tensor("d_xr", [128, C], BF16, kind="ExternalOutput")
                d_xk = nc.dram_tensor("d_xk", [128, C], BF16, kind="ExternalOutput")
                P.wait("s", "PV", tok_bl)
                P.emit("s", lambda s: s.dma_start(d_xr.ap(), xrT.ap()), inc="DSO", dr=False)
                P.emit("s", lambda s: s.dma_start(d_xk.ap(), xkT.ap()), inc="DSO", dr=False)
            P.wait("t", "PV", tok_bl)
            if tt > 0:
                P.wait("t", "PS", tok_sq2)
                P.wait("t", "PV", tok_ffn[1 - par])
            tok_mm1 = None
            for cc in range(NCC):
                for sgi in range(2):
                    tok_mm1 = P.emit("t", lambda t, cc=cc, sgi=sgi: t.matmul(
                        R1[:, sgi * 512:(sgi + 1) * 512],
                        xrT.ap()[:, cc * 128:(cc + 1) * 128],
                        bw(1, cc)[:, sgi * 512:(sgi + 1) * 512],
                        start=(cc == 0), stop=(cc == NCC - 1)),
                        inc="PT" if (cc == NCC - 1 and sgi == 1) else None)
            tok_mm2 = None
            for dc in range(NCC):
                for cc in range(NCC):
                    tok_mm2 = P.emit("t", lambda t, dc=dc, cc=cc: t.matmul(
                        K2[:, dc * 128:(dc + 1) * 128],
                        bw(2, cc)[:, dc * 128:(dc + 1) * 128],
                        xkT.ap()[:, cc * 128:(cc + 1) * 128],
                        start=(cc == 0), stop=(cc == NCC - 1)),
                        inc="PT" if (dc == NCC - 1 and cc == NCC - 1) else None)
            tok_mm2_prev = tok_mm2
            P.wait("s", "PT", tok_mm1)
            P.wait("s", "PV", tok_ffn[par])
            tok_th = P.emit("s", lambda s, tt=tt: s.activation(
                rtf[tt % 2], R1, AF.Tanh, scale=0.5), inc="PS")
            P.wait("s", "PT", tok_mm2)
            tok_rel = P.emit("s", lambda s: s.activation(
                ktmp, K2, AF.Relu), inc="PS", dr=True)
            P.wait("s", "PT", tok_mm3_prev)
            tok_ka = P.emit("s", lambda s: s.activation(
                kaT.ap(), ktmp, AF.Square, scale=float(np.float32(0.5 ** 0.5))),
                inc="PS")
            if KDUMP == 3 and b == 0 and tt == 0:
                d_ka = nc.dram_tensor("d_ka", [128, C], BF16, kind="ExternalOutput")
                P.wait("s", "PV", tok_ka)
                P.emit("s", lambda s: s.dma_start(d_ka.ap(), kaT.ap()), inc="DSO", dr=False)
            P.wait("t", "PS", tok_ka)
            tok_mm3 = None
            for dc in range(NCC):
                for sgi in range(2):
                    tok_mm3 = P.emit("t", lambda t, dc=dc, sgi=sgi: t.matmul(
                        P3p[:, sgi * 512:(sgi + 1) * 512],
                        kaT.ap()[:, dc * 128:(dc + 1) * 128],
                        bw(3, dc)[:, sgi * 512:(sgi + 1) * 512],
                        start=(dc == 0), stop=(dc == NCC - 1)),
                        inc="PT" if (dc == NCC - 1 and sgi == 1) else None)
            tok_mm3_prev = tok_mm3
            P.wait("v", "PT", tok_mm3)
            P.wait("v", "PS", tok_th)
            tok_ffn[par] = P.emit("v", lambda v, tt=tt: v.scalar_tensor_tensor(
                rtf[tt % 2], rtf[tt % 2], 1.0, P3p, ALU.add, ALU.mult),
                inc="PV", dr=True)
            tok_s2 = P.emit("v", lambda v, tt=tt: v.scalar_tensor_tensor(
                ktmp, rtf[tt % 2], 1.0, sx1, ALU.mult, ALU.add, accum_out=ssum),
                inc="PV")
            P.wait("s", "PV", tok_s2)
            tok_sq2 = P.emit("s", lambda s, tt=tt: s.activation(
                outb[tt % 2], ktmp, AF.Square, accum_out=ssq), inc="PS")
            newton(tok_sq2)
            tok_out[par] = P.emit("v", lambda v, tt=tt: v.tensor_scalar(
                outb[tt % 2], ktmp, negm, rstd, ALU.add, ALU.mult), inc="PV")
            if KDUMP == 3 and b == 0 and tt == 0:
                d_rt = nc.dram_tensor("d_rt", [128, C], F32, kind="ExternalOutput")
                P.wait("s", "PV", tok_ffn[par])
                P.emit("s", lambda s, tt=tt: s.dma_start(d_rt.ap(), rtf[tt % 2]),
                       inc="DSO", dr=False)
            P.wait("s", "PV", tok_out[par])
            tok_out_dma[par] = P.emit("s", lambda s, b=b, tt=tt: s.dma_start(
                y_out.ap()[b, tt * 128:(tt + 1) * 128, :], outb[tt % 2]),
                inc="DSO", dr=False)

        tok_p3_done_t = ch["PT"].n
        tok_p3_done_v = ch["PV"].n
        tok_p3_done_s = ch["PS"].n
        tok_p3_out_d = ch["DSO"].n

    if KDUMP:
        d_q = nc.dram_tensor("d_q", [H, 71, T], BF16, kind="ExternalOutput")
        d_k = nc.dram_tensor("d_k", [H, 71, T], BF16, kind="ExternalOutput")
        d_xb = nc.dram_tensor("d_xb", [T, C], BF16, kind="ExternalOutput")
        d_xy = nc.dram_tensor("d_xy", [128, 16384], BF16, kind="ExternalOutput")
        d_vs = nc.dram_tensor("d_vs", [128, 16 * 1040], BF16, kind="ExternalOutput")
        d_ysh = nc.dram_tensor("d_ysh", [8, 64, T], BF16, kind="ExternalOutput")
        for _cn in ("DSQ", "DSE", "DSY", "DSO", "DYW", "DYQ", "DYE", "DYR",
                    "DYS", "DY3", "PV", "PS", "PT"):
            P.wait("g", _cn, ch[_cn].n)
        P.emit("g", lambda g: g.dma_start(d_q.ap(), Qs.ap()[0]), inc="DG")
        P.emit("g", lambda g: g.dma_start(d_k.ap(), Ks.ap()[0]), inc="DG")
        P.emit("g", lambda g: g.dma_start(d_xb.ap(), xb16.ap()[0]), inc="DG")
        P.emit("g", lambda g: g.dma_start(d_xy.ap(), XY.ap()), inc="DG")
        P.emit("g", lambda g: g.dma_start(d_vs.ap(), VS.ap()), inc="DG")
        P.emit("g", lambda g: g.dma_start(d_ysh.ap(), Ysh.ap()), inc="DG")
        P.q["y"].append(lambda eng, v=ch["DG"].n: eng.wait_ge(sems["DG"], v))

    final_ds = ch["DSO"].n
    P.q["y"].append(lambda eng: eng.wait_ge(sems["DSO"], final_ds))

    with nc.Block() as blk:
        P.run(blk)
    es.close()
    return nc


_cache = {}


def kernel(L0_attn_x, L0_attn_w_attn, L0_attn_w_proj, L0_ffn_W_r, L0_ffn_W_k,
           L0_ffn_W_v, L0_ffn_mu_r, L0_ffn_mu_k):
    x = np.ascontiguousarray(np.asarray(L0_attn_x, dtype=np.float32))
    wa = np.ascontiguousarray(np.asarray(L0_attn_w_attn, dtype=np.float32))
    wp = np.ascontiguousarray(np.asarray(L0_attn_w_proj, dtype=np.float32))
    wr = np.ascontiguousarray(np.asarray(L0_ffn_W_r, dtype=np.float32))
    wk = np.ascontiguousarray(np.asarray(L0_ffn_W_k, dtype=np.float32))
    wv = np.ascontiguousarray(np.asarray(L0_ffn_W_v, dtype=np.float32))
    mur = np.ascontiguousarray(
        np.asarray(L0_ffn_mu_r, dtype=np.float32).reshape(8, 128).T)
    muk = np.ascontiguousarray(
        np.asarray(L0_ffn_mu_k, dtype=np.float32).reshape(8, 128).T)
    qaug, kaug, maskc, identc, biasc, _ = host_consts()

    import os
    _ph = float(os.environ.get("KPHASES", "3"))
    if _ph not in _cache:
        _cache[_ph] = build_nc(_ph)
    nc = _cache[_ph]

    in_maps = []
    for core in range(8):
        in_maps.append({
            "x_in": np.ascontiguousarray(x[2 * core:2 * core + 2]),
            "w_attn": wa, "w_proj": wp, "w_r": wr, "w_k": wk, "w_v": wv,
            "mu_r": mur, "mu_k": muk,
            "qaug": qaug, "kaug": kaug, "maskc": maskc, "identc": identc,
            "biascc": biasc,
        })
    res = run_bass_kernel_spmd(nc, in_maps, core_ids=list(range(8)))
    return np.concatenate([r["y_out"] for r in res.results], axis=0)


# version 09
# speedup vs baseline: 1.2355x; 1.2058x over previous
import math
import numpy as np
import ml_dtypes

import concourse.bass as bass
import concourse.mybir as mybir
from concourse.bass_utils import run_bass_kernel_spmd

F32 = mybir.dt.float32
BF16 = mybir.dt.bfloat16
I32 = mybir.dt.int32
AF = mybir.ActivationFunctionType
ALU = mybir.AluOpType
AX = mybir.AxisListType

B = 2
T = 2048
C = 1024
H = 16
D = 64
NCC = 8
NTT = 16
SEG = 512
ISQRT8 = float(2.0 ** -1.5)
EPS = 1e-5
NEG = -1.0e30
MAGIC = 0x5F3759DF

def _cut_table():
    cuts = []
    for h in range(H):
        sl = 2.0 ** (-(h + 1) / 2.0)
        c = 1 + math.ceil(110.0 / (128.0 * sl))
        cuts.append(min(16, c))
    return cuts

CUT = _cut_table()


def _slopes_f32():
    start = 2.0 ** (-(2.0 ** (-(math.log2(H) - 3))))
    return [np.float32(start * (start ** i)) for i in range(H)]


def host_consts():
    bf = ml_dtypes.bfloat16
    slopes = _slopes_f32()
    pos = np.arange(T)
    qf = (pos % 128).astype(np.float32)
    qh = ((pos // 128) * 128).astype(np.float32)
    kp = qf
    qaug = np.zeros((H, 7, T), dtype=bf)
    kaug = np.zeros((H, 7, T), dtype=bf)
    for h in range(H):
        s = slopes[h]
        s_hi = np.float32(bf(s))
        s_lo = np.float32(bf(np.float32(s - s_hi)))
        qaug[h, 0, :] = bf(0.0)
        qaug[h, 1, :] = bf(s_hi)
        qaug[h, 2, :] = bf(s_lo)
        qaug[h, 3, :] = qf.astype(bf)
        qaug[h, 4, :] = qf.astype(bf)
        qaug[h, 5, :] = qh.astype(bf)
        qaug[h, 6, :] = qh.astype(bf)
        kaug[h, 0, :] = bf(1.0)
        kaug[h, 1, :] = kp.astype(bf)
        kaug[h, 2, :] = kp.astype(bf)
        kaug[h, 3, :] = bf(-s_hi)
        kaug[h, 4, :] = bf(-s_lo)
        kaug[h, 5, :] = bf(-s_hi)
        kaug[h, 6, :] = bf(-s_lo)
    kk = np.arange(128)[:, None]
    qq = np.arange(128)[None, :]
    mask = np.where(kk > qq, np.float32(NEG), np.float32(0.0)).astype(np.float32)
    ident = np.eye(128, dtype=np.float32).astype(bf)
    biasc = np.zeros((128, 272), dtype=np.float32)
    for h in range(H):
        se = eff_slope(h)
        for j in range(16):
            biasc[:, h * 16 + j] = np.float32(se * 128.0 * j)
    biasc[:, 256] = np.float32(0.5)
    biasc[:, 257] = np.float32(np.float32(0.5) ** np.float32(0.5))
    return qaug, kaug, mask, ident, biasc, slopes


def eff_slope(h):
    bf = ml_dtypes.bfloat16
    s = _slopes_f32()[h]
    s_hi = np.float32(bf(s))
    s_lo = np.float32(bf(np.float32(s - s_hi)))
    return float(np.float32(s_hi + s_lo))


class Chan:
    def __init__(self, sem, step):
        self.sem = sem
        self.n = 0
        self.step = step

    def bump(self):
        self.n += self.step
        return self.n


class Plan:

    ENGS = ("t", "v", "s", "y", "g")

    def __init__(self, nc, sems):
        self.nc = nc
        self.q = {e: [] for e in self.ENGS}
        self.ch = {}
        for nm in sems:
            self.ch[nm] = Chan(sems[nm], 1 if nm.startswith("P") else 16)
        self._waited = {e: {} for e in self.ENGS}

    def wait(self, e, cname, tok):
        if tok is None or tok <= 0:
            return
        w = self._waited[e]
        if w.get(cname, 0) >= tok:
            return
        w[cname] = tok
        sem = self.ch[cname].sem
        self.q[e].append(lambda eng: eng.wait_ge(sem, tok))

    def emit(self, e, fn, inc=None, dr=None):
        if dr is None:
            dr = e in ("v", "s") and inc is None
        if inc is None:
            if dr:
                def thunk0(eng):
                    fn(eng)
                    eng.drain()
                self.q[e].append(thunk0)
            else:
                self.q[e].append(lambda eng: fn(eng))
            return None
        ch = self.ch[inc]
        tok = ch.bump()
        sem, step = ch.sem, ch.step

        def thunk(eng):
            fn(eng).then_inc(sem, step)
            if dr:
                eng.drain()

        self.q[e].append(thunk)
        return tok

    def run(self, block):
        block.tensor(lambda eng: [f(eng) for f in self.q["t"]])
        block.vector(lambda eng: [f(eng) for f in self.q["v"]])
        block.scalar(lambda eng: [f(eng) for f in self.q["s"]])
        block.sync(lambda eng: [f(eng) for f in self.q["y"]])
        block.gpsimd(lambda eng: [f(eng) for f in self.q["g"]])


def build_nc(phases=3):
    import os as _os
    KDUMP = int(_os.environ.get("KDUMP", "0"))
    KB = int(_os.environ.get("KB", str(B)))
    nc = bass.Bass("TRN2", target_bir_lowering=False,
                   dynamic_dma_scratch_size=12288)

    x_in = nc.dram_tensor("x_in", [B, T, C], F32, kind="ExternalInput")
    w_attn = nc.dram_tensor("w_attn", [3 * C, C], F32, kind="ExternalInput")
    w_proj = nc.dram_tensor("w_proj", [C, C], F32, kind="ExternalInput")
    w_r = nc.dram_tensor("w_r", [C, C], F32, kind="ExternalInput")
    w_k = nc.dram_tensor("w_k", [C, C], F32, kind="ExternalInput")
    w_v = nc.dram_tensor("w_v", [C, C], F32, kind="ExternalInput")
    mu_r = nc.dram_tensor("mu_r", [128, 8], F32, kind="ExternalInput")
    mu_k = nc.dram_tensor("mu_k", [128, 8], F32, kind="ExternalInput")
    qaug = nc.dram_tensor("qaug", [H, 7, T], BF16, kind="ExternalInput")
    kaug = nc.dram_tensor("kaug", [H, 7, T], BF16, kind="ExternalInput")
    maskc = nc.dram_tensor("maskc", [128, 128], F32, kind="ExternalInput")
    identc = nc.dram_tensor("identc", [128, 128], BF16, kind="ExternalInput")
    biascc = nc.dram_tensor("biascc", [128, 272], F32, kind="ExternalInput")
    y_out = nc.dram_tensor("y_out", [B, T, C], F32, kind="ExternalOutput")

    wattn_b = nc.dram_tensor("wattn_b", [3 * C, C], BF16, kind="Internal")
    wproj_b = nc.dram_tensor("wproj_b", [C, C], BF16, kind="Internal")
    wr_b = nc.dram_tensor("wr_b", [C, C], BF16, kind="Internal")
    wk_b = nc.dram_tensor("wk_b", [C, C], BF16, kind="Internal")
    wv_b = nc.dram_tensor("wv_b", [C, C], BF16, kind="Internal")
    xb16 = nc.dram_tensor("xb16", [B, T, C], BF16, kind="Internal")
    Qs = nc.dram_tensor("Qs", [B, H, 71, T], BF16, kind="Internal")
    Ks = nc.dram_tensor("Ks", [B, H, 71, T], BF16, kind="Internal")
    Ysh = nc.dram_tensor("Ysh", [8, 64, T], BF16, kind="Internal")
    ElRaw = nc.dram_tensor("ElRaw", [H, T], BF16, kind="Internal")
    ElRec = nc.dram_tensor("ElRec", [H, T], BF16, kind="Internal")

    AW = nc.alloc_sbuf_tensor("AW", [128, 16384], BF16)
    BW = nc.alloc_sbuf_tensor("BW", [128, 32768], BF16)
    XY = nc.alloc_sbuf_tensor("XY", [128, 16384], BF16)
    VS = nc.alloc_sbuf_tensor("VS", [128, 16 * 1040], BF16)
    qt = [nc.alloc_sbuf_tensor(f"qt{i}", [128, T], BF16) for i in range(2)]
    kt = [nc.alloc_sbuf_tensor(f"kt{i}", [128, T], BF16) for i in range(2)]
    qkev = [nc.alloc_sbuf_tensor(f"qkev{i}", [128, T], BF16) for i in range(2)]
    expb = [nc.alloc_sbuf_tensor(f"expb{i}", [128, SEG], BF16) for i in range(3)]
    yun = nc.alloc_sbuf_tensor("yun", [128, T], BF16)
    sqb = nc.alloc_sbuf_tensor("sqb", [128, T], BF16)
    x1b = nc.alloc_sbuf_tensor("x1b", [128, C], BF16)
    x1T = nc.alloc_sbuf_tensor("x1T", [128, C], BF16)
    xpT = nc.alloc_sbuf_tensor("xpT", [128, C], BF16)
    xrT = nc.alloc_sbuf_tensor("xrT", [128, C], BF16)
    xkT = nc.alloc_sbuf_tensor("xkT", [128, C], BF16)
    kaT = nc.alloc_sbuf_tensor("kaT", [128, C], BF16)
    mask_sb = nc.alloc_sbuf_tensor("mask_sb", [128, 128], F32)
    biasc_sb = nc.alloc_sbuf_tensor("biasc_sb", [128, 272], F32)
    id_sb = nc.alloc_sbuf_tensor("id_sb", [128, 128], BF16)
    ones_sb = nc.alloc_sbuf_tensor("ones_sb", [128, 64], BF16)
    mur_sb = nc.alloc_sbuf_tensor("mur_sb", [128, 8], F32)
    muk_sb = nc.alloc_sbuf_tensor("muk_sb", [128, 8], F32)
    SM2 = nc.alloc_sbuf_tensor("SM2", [128, 32], F32)
    SMI = nc.alloc_sbuf_tensor("SMI", [128, 4], I32)
    lastc = nc.alloc_sbuf_tensor("lastc", [128, 8], BF16)
    ellp = nc.alloc_sbuf_tensor("ellp", [128, 16], BF16)
    ellr = nc.alloc_sbuf_tensor("ellr", [128, 16], F32)
    ellrb = nc.alloc_sbuf_tensor("ellrb", [128, 16], BF16)

    PS = nc.alloc_psum_tensor("PS", [128, 4096], F32)

    def awf(lo, hi):
        return AW.ap()[:, lo:hi].bitcast(F32)

    x3f = [awf(0, 2048), awf(2048, 4096)]
    sx1 = awf(4096, 6144)
    rtf = [awf(6144, 8192), awf(8192, 10240)]
    outb = [awf(10240, 12288), awf(12288, 14336)]
    ktmp = awf(14336, 16384)

    ssum = SM2.ap()[:, 0:1]
    ssq = SM2.ap()[:, 1:2]
    negm = SM2.ap()[:, 2:3]
    msq = SM2.ap()[:, 3:4]
    var = SM2.ap()[:, 4:5]
    rstd = SM2.ap()[:, 5:6]
    hv = SM2.ap()[:, 6:7]
    nt1 = SM2.ap()[:, 7:8]
    nt2 = SM2.ap()[:, 8:9]
    mkseg = lambda i: SM2.ap()[64:65, 16 + i:17 + i]
    mk2 = SM2.ap()[64:65, 20:21]
    ngmk = SM2.ap()[64:65, 21:22]

    sems = {}
    import contextlib
    es = contextlib.ExitStack()
    for nm in ("PT", "PV", "PS", "DYW", "DYQ", "DYE", "DYR", "DYS", "DY3",
               "DSQ", "DSE", "DSF", "DSY", "DSO", "DG", "DGX", "DGR"):
        sems[nm] = es.enter_context(nc.semaphore(nm))

    P = Plan(nc, sems)
    ch = P.ch

    P.emit("g", lambda g: g.dma_start(wattn_b.ap(), w_attn.ap()), inc="DG")
    P.emit("g", lambda g: g.dma_start(wproj_b.ap(), w_proj.ap()), inc="DG")
    P.emit("g", lambda g: g.dma_start(wr_b.ap(), w_r.ap()), inc="DG")
    P.emit("g", lambda g: g.dma_start(wk_b.ap(), w_k.ap()), inc="DG")
    P.emit("g", lambda g: g.dma_start(wv_b.ap(), w_v.ap()), inc="DG")
    tok_wcast = ch["DG"].n
    for b in range(B):
        for h in range(H):
            P.emit("g", lambda g, b=b, h=h: g.dma_start(
                Qs.ap()[b, h, 64:71, :], qaug.ap()[h]), inc="DG")
            P.emit("g", lambda g, b=b, h=h: g.dma_start(
                Ks.ap()[b, h, 64:71, :], kaug.ap()[h]), inc="DG")
    tok_aug = ch["DG"].n
    P.emit("g", lambda g: g.dma_start(mask_sb.ap(), maskc.ap()), inc="DG")
    P.emit("g", lambda g: g.dma_start(id_sb.ap(), identc.ap()), inc="DG")
    P.emit("g", lambda g: g.dma_start(mur_sb.ap(), mu_r.ap()), inc="DG")
    P.emit("g", lambda g: g.dma_start(muk_sb.ap(), mu_k.ap()), inc="DG")
    P.emit("g", lambda g: g.dma_start(biasc_sb.ap(), biascc.ap()), inc="DG")
    tok_consts = ch["DG"].n
    _reg = set()

    def _regconst(val, col):
        if val in _reg or val == 0.0:
            return
        _reg.add(val)
        nc.const_aps.register(F32, val, biasc_sb.ap()[:, col:col + 1])

    for _h in range(H):
        _se = eff_slope(_h)
        for _j in range(16):
            _regconst(float(np.float32(_se * 128.0 * _j)), _h * 16 + _j)
    _regconst(0.5, 256)
    _regconst(float(np.float32(0.5 ** 0.5)), 257)
    tok_xb = [None] * B
    for b in range(B):
        tok_xb[b] = P.emit("g", lambda g, b=b: g.dma_start(
            xb16.ap()[b], x_in.ap()[b]), inc="DGX")

    P.emit("v", lambda v: v.memset(ones_sb.ap(), 1.0))
    P.emit("v", lambda v: v.memset(SMI.ap()[:, 0:1], MAGIC))
    vs_ones = VS.ap().rearrange("p (g x) -> p g x", x=65)[:, :, 64:65]
    tok_vinit = P.emit("v", lambda v: v.memset(vs_ones, 1.0), inc="PV")

    def wslab(dst_ap, src, r0, c0):
        return lambda y: y.dma_start(dst_ap, src.ap()[r0:r0 + 1024, c0:c0 + 128],
                                     transpose=True)

    def load_wq():
        P.wait("y", "DG", tok_wcast)
        for c in range(NCC):
            P.emit("y", wslab(AW.ap()[:, c * 1024:(c + 1) * 1024],
                              wattn_b, 0, c * 128), inc="DYW")
        return ch["DYW"].n

    def load_wk_slot1():
        for c in range(NCC):
            P.emit("y", wslab(AW.ap()[:, 8192 + c * 1024:8192 + (c + 1) * 1024],
                              wattn_b, 1024, c * 128), inc="DYW")
        return ch["DYW"].n

    def load_wv_slot0():
        for c in range(NCC):
            P.emit("y", wslab(AW.ap()[:, c * 1024:(c + 1) * 1024],
                              wattn_b, 2048, c * 128), inc="DYW")
        return ch["DYW"].n

    def load_bw():
        for i, wsrc in enumerate((wproj_b, wr_b, wk_b, wv_b)):
            for c in range(NCC):
                P.emit("y", wslab(
                    BW.ap()[:, i * 8192 + c * 1024:i * 8192 + (c + 1) * 1024],
                    wsrc, 0, c * 128), inc="DYW")
        return ch["DYW"].n

    def load_xT(b):
        P.wait("y", "DGX", tok_xb[b])
        for c in range(NCC):
            P.emit("y", lambda y, b=b, c=c: y.dma_start(
                XY.ap()[:, c * T:(c + 1) * T],
                xb16.ap()[b][:, c * 128:(c + 1) * 128], transpose=True),
                inc="DYW")
        return ch["DYW"].n

    def aw_q(c):
        return AW.ap()[:, c * 1024:(c + 1) * 1024]

    def aw_k(c):
        return AW.ap()[:, 8192 + c * 1024:8192 + (c + 1) * 1024]

    def bw(i, c):
        return BW.ap()[:, i * 8192 + c * 1024:i * 8192 + (c + 1) * 1024]

    def xT(c):
        return XY.ap()[:, c * T:(c + 1) * T]

    yT = xT

    tok_wq = load_wq()
    tok_wk = load_wk_slot1()
    tok_xT = [None] * B
    tok_xT[0] = load_xT(0)
    tok_bw = load_bw()

    tok_p3_done_t = 0
    tok_p3_done_v = 0
    tok_p3_done_s = 0
    tok_p3_out_d = 0
    slopes_eff = [eff_slope(h) for h in range(H)]
    tok_s_tile = [0] * NTT
    tok_tp_evac = 0
    tok_ffn = {0: 0, 1: 0}
    tok_out = {0: 0, 1: 0}
    tok_out_dma = {0: 0, 1: 0}
    tok_sq2 = 0
    tok_bl_prev = 0
    tok_mm2_prev = 0
    tok_mm3_prev = 0
    st2 = {
        "exp_by_bank": {0: 0, 1: 0},
        "sbank": 0,
        "expn": 0,
        "av_by_expb": {0: 0, 1: 0, 2: 0},
        "ye": 0,
        "rell": 0,
        "norm": 0,
        "yun_free_ds": 0,
    }

    for b in range(KB):
        tok_qs_w = {}
        grp_evac = {0: 0, 1: 0}
        if b > 0:
            P.wait("y", "PT", tok_p3_done_t)
            P.wait("y", "PV", tok_p3_done_v)
            P.wait("y", "PS", tok_p3_done_s)
            P.wait("y", "DSO", tok_p3_out_d)
            tok_wq = load_wq()
            tok_wk = load_wk_slot1()
            tok_xT[b] = load_xT(b)

        last_qk_mm = 0
        tok_p1_evac_last = 0
        _wl = []
        if phases >= 0.5:
            _wl.append(("q", aw_q, Qs))
        if phases >= 0.7:
            _wl.append(("k", aw_k, Ks))
        for which, wsel, scratch in _wl:
            P.wait("t", "DYW", tok_wq if which == "q" else tok_wk)
            P.wait("t", "DYW", tok_xT[b])
            if b > 0:
                P.wait("t", "PV", tok_p3_done_v)
                P.wait("t", "PS", tok_p3_done_s)
            for dt in range(8):
                gi = dt % 2
                grp = gi * 2048
                P.wait("t", "PV", grp_evac[gi])
                tok_mm = None
                for cc in range(NCC):
                    for sgi in range(4):
                        tok_mm = P.emit("t", lambda t, w=wsel(cc), cc=cc, dt=dt,
                                        sgi=sgi, grp=grp: t.matmul(
                            PS.ap()[:, grp + sgi * 512:grp + (sgi + 1) * 512],
                            w[:, dt * 128:(dt + 1) * 128],
                            xT(cc)[:, sgi * 512:(sgi + 1) * 512],
                            start=(cc == 0), stop=(cc == NCC - 1)),
                            inc="PT" if (cc == NCC - 1 and sgi == 3) else None)
                last_qk_mm = tok_mm
                ev = qkev[gi]
                prev = tok_qs_w.get((which, dt - 2)) or (
                    tok_qs_w.get(("q", dt + 6)) if which == "k" else None)
                P.wait("v", "DSQ", prev)
                P.wait("v", "PT", tok_mm)
                tok_ev = None
                for sgi in range(4):
                    tok_ev = P.emit("v", lambda v, ev=ev, grp=grp, sgi=sgi:
                                    v.tensor_scalar(
                        ev.ap()[:, sgi * 512:(sgi + 1) * 512],
                        PS.ap()[:, grp + sgi * 512:grp + (sgi + 1) * 512],
                        ISQRT8, None, ALU.mult),
                        inc="PV" if sgi == 3 else None,
                        dr=False if sgi < 3 else None)
                grp_evac[gi] = tok_ev
                tok_p1_evac_last = tok_ev
                P.wait("s", "PV", tok_ev)
                P.wait("s", "DG", tok_aug)
                tk = None
                for half in range(2):
                    tk = P.emit("s", lambda s, ev=ev, b=b, dt=dt, half=half,
                                scratch=scratch: s.dma_start(
                        scratch.ap()[b, 2 * dt + half, 0:64, :],
                        ev.ap()[64 * half:64 * half + 64, :]), inc="DSQ",
                        dr=False)
                tok_qs_w[(which, dt)] = tk

        if phases < 1:
            continue
        P.wait("y", "PT", last_qk_mm)
        tok_wv = load_wv_slot0()
        P.wait("t", "DYW", tok_wv)
        P.wait("t", "PV", grp_evac[0])
        vgrp_evac = {0: 0, 1: 0}
        tok_v_evac = 0
        for tt in range(NTT):
            gi = tt % 2
            grp = gi * 1024
            P.wait("t", "PV", vgrp_evac[gi])
            tok_mm = None
            for cc in range(NCC):
                for sgi in range(2):
                    tok_mm = P.emit("t", lambda t, cc=cc, tt=tt, sgi=sgi, grp=grp:
                                    t.matmul(
                        PS.ap()[:, grp + sgi * 512:grp + (sgi + 1) * 512],
                        xT(cc)[:, tt * 128:(tt + 1) * 128],
                        aw_q(cc)[:, sgi * 512:(sgi + 1) * 512],
                        start=(cc == 0), stop=(cc == NCC - 1)),
                        inc="PT" if (cc == NCC - 1 and sgi == 1) else None)
            P.wait("v", "PT", tok_mm)
            P.wait("v", "PV", tok_vinit)
            src = PS.ap()[:, grp:grp + 1024].rearrange("p (h e) -> p h e", e=64)
            dst = VS.ap()[:, tt * 1040:(tt + 1) * 1040].rearrange(
                "p (h e) -> p h e", e=65)[:, :, 0:64]
            tok_v_evac = P.emit("v", lambda v, src=src, dst=dst:
                                v.tensor_copy(dst, src), inc="PV")
            vgrp_evac[gi] = tok_v_evac

        if phases < 2:
            tok_p3_done_t = ch["PT"].n
            tok_p3_done_v = ch["PV"].n
            tok_p3_done_s = ch["PS"].n
            tok_p3_out_d = ch["DSO"].n
            continue
        Yps = PS.ap()[0:65, 0:2048]
        Yl = PS.ap()[64:65, 0:2048]
        Sb = [PS.ap()[:, 2048 + 512 * i:2048 + 512 * (i + 1)] for i in range(2)]
        nrm = PS.ap()[64:65, 3072:3584]
        bcb = PS.ap()[0:64, 3584:4096]

        tok_qk_load = {}
        tok_ub = {}
        tok_scores_last = {}

        def p2_load(h):
            buf = h % 2
            P.wait("y", "DSQ", ch["DSQ"].n if h == 0 else
                   max(tok_qs_w[("q", h // 2)], tok_qs_w[("k", h // 2)]))
            P.wait("y", "PT", tok_scores_last.get(h - 2))
            P.wait("y", "PV", tok_ub.get(h - 2))
            P.emit("y", lambda y, b=b, h=h, buf=buf: y.dma_start(
                kt[buf].ap()[0:71, :], Ks.ap()[b, h]), inc="DYQ")
            tok_qk_load[h] = P.emit("y", lambda y, b=b, h=h, buf=buf: y.dma_start(
                qt[buf].ap()[0:71, :], Qs.ap()[b, h]), inc="DYQ")

        def p2_prep(h):
            buf = h % 2
            P.wait("v", "DYQ", tok_qk_load[h])
            tk = P.emit("v", lambda v, buf=buf: v.tensor_mul(
                sqb.ap()[0:64, :], kt[buf].ap()[0:64, :], kt[buf].ap()[0:64, :]),
                inc="PV")
            P.wait("t", "PV", tok_vinit)
            if h == 0:
                P.wait("t", "PV", tok_p1_evac_last)
            last_red = 0
            for sgi in range(4):
                P.wait("t", "PV", tk)
                P.wait("t", "PV", last_red)
                tm = P.emit("t", lambda t, sgi=sgi: t.matmul(
                    nrm, ones_sb.ap()[0:64, 0:1],
                    sqb.ap()[0:64, sgi * 512:(sgi + 1) * 512],
                    start=True, stop=True), inc="PT")
                yield
                P.wait("v", "PT", tm)
                last_red = P.emit("v", lambda v, sgi=sgi: v.reduce_max(
                    mkseg(sgi), nrm, AX.X), inc="PV", dr=True)
            P.wait("v", "PT", ch["PT"].n)
            tkq = P.emit("v", lambda v, buf=buf: v.tensor_mul(
                sqb.ap()[0:64, :], qt[buf].ap()[0:64, :], qt[buf].ap()[0:64, :]),
                inc="PV")
            P.emit("v", lambda v: v.reduce_max(mk2, SM2.ap()[64:65, 16:20], AX.X))
            tmk = P.emit("v", lambda v: v.tensor_scalar(
                ngmk, mk2, -0.5, None, ALU.mult), inc="PV")
            lastu = 0
            for sgi in range(4):
                P.wait("t", "PV", tkq)
                P.wait("t", "PV", lastu)
                tm = P.emit("t", lambda t, sgi=sgi: t.matmul(
                    nrm, ones_sb.ap()[0:64, 0:1],
                    sqb.ap()[0:64, sgi * 512:(sgi + 1) * 512],
                    start=True, stop=True), inc="PT")
                yield
                P.wait("v", "PT", tm)
                P.wait("v", "PV", tmk)
                lastu = P.emit("v", lambda v, buf=buf, sgi=sgi: v.tensor_scalar(
                    qt[buf].ap()[64:65, sgi * 512:(sgi + 1) * 512],
                    nrm, -0.5, ngmk, ALU.mult, ALU.add), inc="PV")
            tok_ub[h] = lastu

        def p2_dump_qt(h):
            if not (KDUMP == 2 and b == 0 and h == int(_os.environ.get("KH", "1"))):
                return
            buf = h % 2
            d_qt = nc.dram_tensor("d_qt", [128, T], BF16, kind="ExternalOutput")
            d_kt = nc.dram_tensor("d_kt", [128, T], BF16, kind="ExternalOutput")
            P.wait("s", "PV", tok_ub[h])
            P.emit("s", lambda s, buf=buf: s.dma_start(d_qt.ap(), qt[buf].ap()),
                   inc="DSO", dr=False)
            P.emit("s", lambda s, buf=buf: s.dma_start(d_kt.ap(), kt[buf].ap()),
                   inc="DSO", dr=False)

        def p2_head(h, first_of_batch, prep_gen=None, fin_gen=None,
                    do_av=True, do_fin=True):
            buf = h % 2
            cut = CUT[h]
            sl = slopes_eff[h]
            firstj = {}
            lastj = {}
            for j in range(NTT):
                qmax = min(T, 128 * (j + cut))
                for bank in range((128 * j) // 512, (qmax + 511) // 512):
                    firstj.setdefault(bank, j)
                    lastj[bank] = j
            tok_av_last = 0
            pend_av = []

            def emit_av(j, qa, qb, eb, ei, tok_exp):
                nonlocal tok_av_last
                P.wait("t", "PS", tok_exp)
                if first_of_batch and j == 0 and qa == 128 * j:
                    P.wait("t", "PV", tok_v_evac)
                if j == 0 and qa == 0:
                    P.wait("t", "PS", st2["ye"])
                    P.wait("t", "PV", st2["rell"])
                bk0 = qa // 512
                bk1 = (qb - 1) // 512
                for bank in range(bk0, bk1 + 1):
                    ca = max(qa, bank * 512)
                    cb = min(qb, (bank + 1) * 512)
                    stf = (firstj[bank] == j)
                    spf = (lastj[bank] == j)
                    tok_av_last = P.emit("t", lambda t, h=h, j=j, ca=ca,
                                         cb=cb, qa=qa, eb=eb, stf=stf,
                                         spf=spf: t.matmul(
                        Yps[:, ca:cb],
                        VS.ap()[:, j * 1040 + h * 65:j * 1040 + h * 65 + 65],
                        eb.ap()[:, ca - qa:cb - qa], start=stf, stop=spf),
                        inc="PT")
                st2["av_by_expb"][ei] = tok_av_last

            for j in range(NTT):
                if prep_gen is not None:
                    next(prep_gen, None)
                if fin_gen is not None and j in (8, 10, 12, 14):
                    next(fin_gen, None)
                qmax = min(T, 128 * (j + cut))
                q0 = 128 * j
                bounds = [q0]
                nb = (q0 // 512 + 1) * 512
                while nb < qmax:
                    bounds.append(nb)
                    nb += 512
                bounds.append(qmax)
                for si in range(len(bounds) - 1):
                    qa, qb = bounds[si], bounds[si + 1]
                    sbk = st2["sbank"]
                    sb = Sb[sbk]
                    P.wait("t", "PV", tok_ub[h])
                    P.wait("t", "PS", st2["exp_by_bank"][sbk])
                    if first_of_batch and j == 0 and si == 0:
                        P.wait("t", "PV", tok_p1_evac_last)
                    tok_sc = P.emit("t", lambda t, buf=buf, j=j, qa=qa, qb=qb,
                                    sb=sb: t.matmul(
                        sb[:, 0:qb - qa], kt[buf].ap()[0:71, j * 128:(j + 1) * 128],
                        qt[buf].ap()[0:71, qa:qb], start=True, stop=True),
                        inc="PT")
                    tok_scores_last[h] = tok_sc
                    tok_mask = None
                    if si == 0:
                        P.wait("v", "DG", tok_consts)
                        P.wait("v", "PT", tok_sc)
                        tok_mask = P.emit("v", lambda v, sb=sb: v.tensor_add(
                            sb[:, 0:128], sb[:, 0:128], mask_sb.ap()), inc="PV")
                    ei = st2["expn"] % 3
                    eb = expb[ei]
                    st2["expn"] += 1
                    if tok_mask is not None:
                        P.wait("s", "PV", tok_mask)
                    P.wait("s", "DG", tok_consts)
                    P.wait("s", "PT", tok_sc)
                    P.wait("s", "PT", st2["av_by_expb"][ei])
                    bias = float(np.float32(sl * 128.0 * j))
                    tok_exp = P.emit("s", lambda s, sb=sb, eb=eb, qa=qa, qb=qb,
                                     bias=bias: s.activation(
                        eb.ap()[:, 0:qb - qa], sb[:, 0:qb - qa], AF.Exp,
                        bias=bias), inc="PS")
                    st2["exp_by_bank"][sbk] = tok_exp
                    st2["sbank"] ^= 1
                    if not do_av:
                        continue
                    pend_av.append((j, qa, qb, eb, ei, tok_exp))
                    if len(pend_av) > 1:
                        emit_av(*pend_av.pop(0))
            while pend_av:
                emit_av(*pend_av.pop(0))
            return tok_av_last

        def p2_fin(h, tok_av_last, first_of_batch):
            import os as _os
            KFIN = int(_os.environ.get("KFIN", "31"))
            pair = h // 2
            odd = h % 2
            P.wait("v", "PT", tok_av_last)
            tok_rell = None
            tok_rell_done = None
            if KFIN & 1:
                tok_rell = P.emit("v", lambda v: v.tensor_copy(
                    sqb.ap()[64:65, 0:2048], Yl), inc="PV")
                P.wait("s", "PV", tok_rell)
                t1 = P.emit("s", lambda s, h=h: s.dma_start(
                    ElRaw.ap()[h:h + 1, :], sqb.ap()[64:65, 0:2048]),
                    inc="DSE", dr=False)
                P.wait("y", "DSE", t1)
                P.wait("y", "PV", st2.get("ellp_free", 0))
                t2 = P.emit("y", lambda y, h=h: y.dma_start(
                    ellp.ap(), ElRaw.ap()[h].rearrange("(p c) -> p c", p=128)),
                    inc="DYE")
                P.wait("v", "DYE", t2)
                tok_rec = P.emit("v", lambda v: v.reciprocal(
                    ellr.ap(), ellp.ap()), inc="PV", dr=True)
                st2["ellp_free"] = tok_rec
                tok_recb = P.emit("v", lambda v: v.tensor_copy(
                    ellrb.ap(), ellr.ap()), inc="PV", dr=False)
                P.wait("s", "PV", tok_recb)
                t3 = P.emit("s", lambda s, h=h: s.dma_start(
                    ElRec.ap()[h].rearrange("(p c) -> p c", p=128), ellrb.ap()),
                    inc="DSF", dr=False)
                P.wait("y", "DSF", t3)
                tok_rell_done = P.emit("y", lambda y, h=h: y.dma_start(
                    sqb.ap()[64:65, 0:2048], ElRec.ap()[h:h + 1, :]), inc="DYR")
            tok_ye = None
            if KFIN & 2:
                P.wait("s", "PT", tok_av_last)
                P.wait("s", "PV", st2["norm"])
                P.wait("s", "DSY", st2.get("yun_free_ds", 0))
                tok_ye = P.emit("s", lambda s: s.activation(
                    yun.ap()[0:64, :], PS.ap()[0:64, 0:2048], AF.Copy), inc="PS")
            if KDUMP == 2 and b == 0 and h == int(_os.environ.get("KH", "1")):
                d_yun = nc.dram_tensor("d_yun", [64, T], BF16, kind="ExternalOutput")
                P.wait("s", "PS", tok_ye)
                P.emit("s", lambda s: s.dma_start(d_yun.ap(), yun.ap()[0:64, :]),
                       inc="DSO", dr=False)
                d_rell = nc.dram_tensor("d_rell", [1, T], BF16, kind="ExternalOutput")
                P.wait("s", "PV", tok_rell)
                P.emit("s", lambda s: s.dma_start(d_rell.ap(), sqb.ap()[64:65, :]),
                       inc="DSO", dr=False)
            if tok_ye:
                st2["ye"] = tok_ye
            if tok_rell:
                st2["rell"] = tok_rell
            tok_norm = 0
            for sgi in range(4):
                if not (KFIN & 4):
                    break
                yield
                P.wait("t", "DYR", tok_rell_done)
                if sgi == 0:
                    P.wait("t", "PV", st2["norm"])
                    if first_of_batch:
                        P.wait("t", "PV", tok_p1_evac_last)
                else:
                    P.wait("t", "PV", tok_norm)
                tbc = P.emit("t", lambda t, sgi=sgi: t.matmul(
                    bcb, ones_sb.ap()[64:65, 0:64],
                    sqb.ap()[64:65, sgi * 512:(sgi + 1) * 512],
                    start=True, stop=True), inc="PT")
                if not (KFIN & 8):
                    continue
                P.wait("v", "PT", tbc)
                P.wait("v", "PS", tok_ye)
                if odd:
                    tok_norm = P.emit("v", lambda v, sgi=sgi: v.tensor_mul(
                        yun.ap()[0:64, sgi * 512:(sgi + 1) * 512],
                        yun.ap()[0:64, sgi * 512:(sgi + 1) * 512], bcb),
                        inc="PV")
                else:
                    tok_norm = P.emit("v", lambda v, pair=pair, sgi=sgi:
                                      v.tensor_mul(
                        yT(pair)[0:64, sgi * 512:(sgi + 1) * 512],
                        yun.ap()[0:64, sgi * 512:(sgi + 1) * 512], bcb),
                        inc="PV")
            if (KFIN & 16) and odd:
                slot = pair
                P.wait("s", "PV", tok_norm)
                tw = P.emit("s", lambda s, slot=slot: s.dma_start(
                    Ysh.ap()[slot], yun.ap()[0:64, :]), inc="DSY", dr=False)
                P.wait("y", "DSY", tw)
                tr = P.emit("y", lambda y, pair=pair, slot=slot: y.dma_start(
                    XY.ap()[64:128, pair * T:(pair + 1) * T], Ysh.ap()[slot]),
                    inc="DYS")
                st2["ydma"] = tr
                tok_norm = None
                st2["yun_free_ds"] = tw
            if tok_norm:
                st2["norm"] = tok_norm

        p2_load(0)
        if phases >= 1.5:
            for _ in p2_prep(0):
                pass
        fin_prev = None
        for h in range(H):
            if h + 1 < H:
                p2_load(h + 1)
            if phases < 1.5:
                continue
            gen_next = p2_prep(h + 1) if h + 1 < H else None
            p2_dump_qt(h)
            if phases < 1.7:
                continue
            tok_avl = p2_head(h, first_of_batch=(h == 0), prep_gen=gen_next,
                              fin_gen=fin_prev, do_av=(phases >= 1.8),
                              do_fin=(phases >= 1.9))
            if gen_next is not None:
                for _ in gen_next:
                    pass
            if fin_prev is not None:
                for _ in fin_prev:
                    pass
            fin_prev = None
            if phases >= 1.9 and tok_avl is not None:
                fin_prev = p2_fin(h, tok_avl, first_of_batch=(h == 0))
                next(fin_prev, None)
        if fin_prev is not None:
            for _ in fin_prev:
                pass
        tok_norm_last = st2["norm"]
        tok_ydma_last = st2.get("ydma", 0)
        tok_ye_last = st2["ye"]

        if phases < 3:
            tok_p3_done_t = ch["PT"].n
            tok_p3_done_v = ch["PV"].n
            tok_p3_done_s = ch["PS"].n
            tok_p3_out_d = ch["DSO"].n
            continue
        P1p = PS.ap()[:, 0:1024]
        R1 = PS.ap()[:, 1024:2048]
        K2 = PS.ap()[:, 2048:3072]
        P3p = PS.ap()[:, 3072:4096]
        TpA = PS.ap()[:, 0:512].bitcast(BF16)

        tok_x3 = [None] * NTT
        P.wait("y", "PT", ch["PT"].n)

        def load_x3(tt):
            if tt >= 2:
                P.wait("y", "PV", tok_s_tile[tt - 2])
            tok_x3[tt] = P.emit("y", lambda y, b=b, tt=tt: y.dma_start(
                x3f[tt % 2], x_in.ap()[b, tt * 128:(tt + 1) * 128, :]),
                inc="DY3")

        load_x3(0)
        tok_lc_t = {-1: P.emit("v", lambda v: v.memset(lastc.ap(), 0.0),
                               inc="PV")}
        t3 = {}

        def p3_proj(tt):
            P.wait("t", "DYW", tok_bw)
            P.wait("t", "PV", tok_norm_last)
            P.wait("t", "DYS", tok_ydma_last)
            if tt == 0:
                P.wait("t", "PS", ch["PS"].n)
            else:
                P.wait("t", "PV", t3[tt - 1]["tp_evac"])
            tok = None
            for dc in range(NCC):
                for sgi in range(2):
                    tok = P.emit("t", lambda t, dc=dc, tt=tt, sgi=sgi:
                                 t.matmul(
                        P1p[:, sgi * 512:(sgi + 1) * 512],
                        yT(dc)[:, tt * 128:(tt + 1) * 128],
                        bw(0, dc)[:, sgi * 512:(sgi + 1) * 512],
                        start=(dc == 0), stop=(dc == NCC - 1)),
                        inc="PT" if (dc == NCC - 1 and sgi == 1) else None)
            t3.setdefault(tt, {})["proj"] = tok

        def p3_mm12(u):
            P.wait("t", "PV", t3[u]["bl"])
            if u > 0:
                P.wait("t", "PS", t3[u - 1]["sq2"])
                P.wait("t", "PV", t3[u - 1]["ffn"])
            tok = None
            for cc in range(NCC):
                for sgi in range(2):
                    tok = P.emit("t", lambda t, cc=cc, sgi=sgi: t.matmul(
                        R1[:, sgi * 512:(sgi + 1) * 512],
                        xrT.ap()[:, cc * 128:(cc + 1) * 128],
                        bw(1, cc)[:, sgi * 512:(sgi + 1) * 512],
                        start=(cc == 0), stop=(cc == NCC - 1)),
                        inc="PT" if (cc == NCC - 1 and sgi == 1) else None)
            t3[u]["mm1"] = tok
            tok = None
            for dc in range(NCC):
                for cc in range(NCC):
                    tok = P.emit("t", lambda t, dc=dc, cc=cc: t.matmul(
                        K2[:, dc * 128:(dc + 1) * 128],
                        bw(2, cc)[:, dc * 128:(dc + 1) * 128],
                        xkT.ap()[:, cc * 128:(cc + 1) * 128],
                        start=(cc == 0), stop=(cc == NCC - 1)),
                        inc="PT" if (dc == NCC - 1 and cc == NCC - 1) else None)
            t3[u]["mm2"] = tok

        def p3_act(u):
            par = u % 2
            P.wait("s", "PT", t3[u]["mm1"])
            if u > 1:
                P.wait("s", "PV", t3[u - 2]["s2"])
            t3[u]["th"] = P.emit("s", lambda s, u=u: s.activation(
                rtf[u % 2], R1, AF.Tanh, scale=0.5), inc="PS")
            P.wait("s", "PT", t3[u]["mm2"])
            t3[u]["rel"] = P.emit("s", lambda s: s.activation(
                ktmp, K2, AF.Relu), inc="PS", dr=True)
            if u > 0:
                P.wait("s", "PT", t3[u - 1]["mm3"])
            t3[u]["ka"] = P.emit("s", lambda s: s.activation(
                kaT.ap(), ktmp, AF.Square, scale=float(np.float32(0.5 ** 0.5))),
                inc="PS")

        def p3_mm3(u):
            P.wait("t", "PS", t3[u]["ka"])
            tok = None
            for dc in range(NCC):
                for sgi in range(2):
                    tok = P.emit("t", lambda t, dc=dc, sgi=sgi: t.matmul(
                        P3p[:, sgi * 512:(sgi + 1) * 512],
                        kaT.ap()[:, dc * 128:(dc + 1) * 128],
                        bw(3, dc)[:, sgi * 512:(sgi + 1) * 512],
                        start=(dc == 0), stop=(dc == NCC - 1)),
                        inc="PT" if (dc == NCC - 1 and sgi == 1) else None)
            t3[u]["mm3"] = tok

        def newton(tok_sq, c):
            P.wait("v", "PS", tok_sq)
            P.emit("v", lambda v: v.tensor_scalar(
                c["negm"], c["ssum"], -1.0 / C, None, ALU.mult))
            P.emit("v", lambda v: v.tensor_mul(c["msq"], c["negm"], c["negm"]))
            P.emit("v", lambda v: v.tensor_scalar(
                c["nt1"], c["ssq"], 1.0 / C, EPS, ALU.mult, ALU.add))
            P.emit("v", lambda v: v.tensor_sub(c["var"], c["nt1"], c["msq"]))
            P.emit("v", lambda v: v.tensor_scalar(
                c["smi"], c["var"].bitcast(I32), 1, None,
                ALU.arith_shift_right))
            P.emit("v", lambda v: v.scalar_tensor_tensor(
                c["rstd"].bitcast(I32), SMI.ap()[:, 0:1], 0, c["smi"],
                ALU.add, ALU.subtract))
            P.emit("v", lambda v: v.tensor_scalar(
                c["hv"], c["var"], -0.5, None, ALU.mult))
            for _ in range(3):
                P.emit("v", lambda v: v.tensor_mul(c["nt1"], c["rstd"], c["rstd"]))
                P.emit("v", lambda v: v.tensor_scalar(
                    c["nt2"], c["nt1"], c["hv"], 1.5, ALU.mult, ALU.add))
                P.emit("v", lambda v: v.tensor_mul(c["rstd"], c["rstd"], c["nt2"]))
            return P.emit("v", lambda v: v.tensor_mul(
                c["nt1"], c["negm"], c["rstd"]), inc="PV", dr=True)

        LN1C = {"ssum": ssum, "ssq": ssq, "negm": negm, "msq": msq, "var": var,
                "rstd": rstd, "hv": hv, "nt1": nt1, "nt2": nt2,
                "smi": SMI.ap()[:, 1:2]}
        LN2C = {"ssum": SM2.ap()[:, 9:10], "ssq": SM2.ap()[:, 10:11],
                "negm": SM2.ap()[:, 11:12], "msq": SM2.ap()[:, 12:13],
                "var": SM2.ap()[:, 13:14], "rstd": SM2.ap()[:, 14:15],
                "hv": SM2.ap()[:, 15:16], "nt1": SM2.ap()[:, 24:25],
                "nt2": SM2.ap()[:, 25:26], "smi": SMI.ap()[:, 2:3]}

        def p3_ln2(u):
            par = u % 2
            P.wait("v", "PT", t3[u]["mm3"])
            P.wait("v", "PS", t3[u]["th"])
            t3[u]["ffn"] = P.emit("v", lambda v, u=u: v.scalar_tensor_tensor(
                rtf[u % 2], rtf[u % 2], 1.0, P3p, ALU.add, ALU.mult),
                inc="PV", dr=True)
            t3[u]["s2"] = P.emit("v", lambda v, u=u: v.scalar_tensor_tensor(
                rtf[u % 2], rtf[u % 2], 1.0, sx1, ALU.mult, ALU.add,
                accum_out=LN2C["ssum"]), inc="PV")
            P.wait("s", "PV", t3[u]["s2"])
            P.wait("s", "DSO", tok_out_dma[par])
            t3[u]["sq2"] = P.emit("s", lambda s, u=u: s.activation(
                outb[u % 2], rtf[u % 2], AF.Square, accum_out=LN2C["ssq"]),
                inc="PS")
            tok_nmr2 = newton(t3[u]["sq2"], LN2C)
            P.wait("s", "PV", tok_nmr2)
            tok_out[par] = P.emit("s", lambda s, u=u: s.activation(
                outb[u % 2], rtf[u % 2], AF.Identity, bias=LN2C["nt1"],
                scale=LN2C["rstd"]), inc="PS", dr=True)
            P.wait("s", "PS", tok_out[par])
            tok_out_dma[par] = P.emit("s", lambda s, b=b, u=u: s.dma_start(
                y_out.ap()[b, u * 128:(u + 1) * 128, :], outb[u % 2]),
                inc="DSO", dr=False)

        def p3_ln1(tt):
            par = tt % 2
            P.wait("v", "PT", t3[tt]["proj"])
            P.wait("v", "DY3", tok_x3[tt])
            tok_s = P.emit("v", lambda v, tt=tt: v.scalar_tensor_tensor(
                sx1, P1p, 1.0, x3f[tt % 2], ALU.mult, ALU.add,
                accum_out=ssum), inc="PV")
            t3[tt]["s"] = tok_s
            tok_s_tile[tt] = tok_s
            P.wait("s", "PV", tok_s)
            P.wait("s", "DSO", tok_out_dma[par])
            tok_ssq = P.emit("s", lambda s, tt=tt: s.activation(
                outb[tt % 2], sx1, AF.Square, accum_out=ssq), inc="PS")
            tok_nmr = newton(tok_ssq, LN1C)
            P.wait("s", "PV", tok_nmr)
            if tt > 0:
                P.wait("s", "PV", t3[tt - 1]["bl"])
            tok_x1 = P.emit("s", lambda s: s.activation(
                sx1, sx1, AF.Identity, bias=nt1, scale=rstd), inc="PS",
                dr=True)
            t3[tt]["x1b"] = P.emit("s", lambda s: s.activation(
                x1b.ap(), sx1, AF.Copy), inc="PS")

        def p3_blend(tt):
            P.wait("t", "PS", t3[tt]["x1b"])
            P.wait("t", "PV", t3[tt]["s"])
            P.wait("t", "DG", tok_consts)
            tok_tp = None
            for cc in range(NCC):
                tok_tp = P.emit("t", lambda t, cc=cc: t.transpose(
                    TpA[:, cc * 128:(cc + 1) * 128],
                    x1b.ap()[:, cc * 128:(cc + 1) * 128], id_sb.ap()),
                    inc="PT" if cc == NCC - 1 else None)
            P.wait("v", "PT", tok_tp)
            t3[tt]["tp_evac"] = P.emit("v", lambda v: v.tensor_copy(
                x1T.ap(), TpA), inc="PV")
            P.wait("s", "PV", t3[tt]["tp_evac"])
            P.wait("s", "PV", tok_lc_t[tt - 1])
            src_sh = x1T.ap().rearrange("p (c x) -> p c x", x=128)[:, :, 0:127]
            dst_sh = xpT.ap().rearrange("p (c x) -> p c x", x=128)[:, :, 1:128]
            P.emit("s", lambda s: s.copy(dst_sh, src_sh))
            tok_xp = P.emit("s", lambda s: s.copy(
                xpT.ap().rearrange("p (c x) -> p c x", x=128)[:, :, 0:1],
                lastc.ap().rearrange("p (c o) -> p c o", o=1)), inc="PS")
            P.wait("v", "PS", tok_xp)
            tok_lc_t[tt] = P.emit("v", lambda v: v.tensor_copy(
                lastc.ap(),
                x1T.ap().rearrange("p (c x) -> p c x", x=128)[:, :, 127:128]
                .rearrange("p c x -> p (c x)")), inc="PV")
            tok_d = P.emit("v", lambda v: v.tensor_sub(
                x1b.ap(), x1T.ap(), xpT.ap()), inc="PV", dr=True)
            if tt > 0:
                P.wait("v", "PT", t3[tt - 1]["mm2"])
            P.wait("v", "DG", tok_consts)
            tok_bl = None
            for cc in range(NCC):
                P.emit("v", lambda v, cc=cc: v.scalar_tensor_tensor(
                    xrT.ap()[:, cc * 128:(cc + 1) * 128],
                    x1b.ap()[:, cc * 128:(cc + 1) * 128],
                    mur_sb.ap()[:, cc:cc + 1],
                    xpT.ap()[:, cc * 128:(cc + 1) * 128], ALU.mult, ALU.add),
                    dr=False)
                tok_bl = P.emit("v", lambda v, cc=cc: v.scalar_tensor_tensor(
                    xkT.ap()[:, cc * 128:(cc + 1) * 128],
                    x1b.ap()[:, cc * 128:(cc + 1) * 128],
                    muk_sb.ap()[:, cc:cc + 1],
                    xpT.ap()[:, cc * 128:(cc + 1) * 128], ALU.mult, ALU.add),
                    inc="PV" if cc == NCC - 1 else None,
                    dr=False if cc < NCC - 1 else None)
            t3[tt]["bl"] = tok_bl

        for tt in range(NTT):
            if tt + 1 < NTT:
                load_x3(tt + 1)
            p3_proj(tt)
            if tt > 0:
                p3_mm12(tt - 1)
                p3_act(tt - 1)
                p3_mm3(tt - 1)
                p3_ln2(tt - 1)
            p3_ln1(tt)
            p3_blend(tt)
        p3_mm12(NTT - 1)
        p3_act(NTT - 1)
        p3_mm3(NTT - 1)
        p3_ln2(NTT - 1)

        tok_p3_done_t = ch["PT"].n
        tok_p3_done_v = ch["PV"].n
        tok_p3_done_s = ch["PS"].n
        tok_p3_out_d = ch["DSO"].n

    if KDUMP:
        d_q = nc.dram_tensor("d_q", [H, 71, T], BF16, kind="ExternalOutput")
        d_k = nc.dram_tensor("d_k", [H, 71, T], BF16, kind="ExternalOutput")
        d_xb = nc.dram_tensor("d_xb", [T, C], BF16, kind="ExternalOutput")
        d_xy = nc.dram_tensor("d_xy", [128, 16384], BF16, kind="ExternalOutput")
        d_vs = nc.dram_tensor("d_vs", [128, 16 * 1040], BF16, kind="ExternalOutput")
        d_ysh = nc.dram_tensor("d_ysh", [8, 64, T], BF16, kind="ExternalOutput")
        for _cn in ("DSQ", "DSE", "DSY", "DSO", "DYW", "DYQ", "DYE", "DYR",
                    "DYS", "DY3", "PV", "PS", "PT"):
            P.wait("g", _cn, ch[_cn].n)
        P.emit("g", lambda g: g.dma_start(d_q.ap(), Qs.ap()[0]), inc="DG")
        P.emit("g", lambda g: g.dma_start(d_k.ap(), Ks.ap()[0]), inc="DG")
        P.emit("g", lambda g: g.dma_start(d_xb.ap(), xb16.ap()[0]), inc="DG")
        P.emit("g", lambda g: g.dma_start(d_xy.ap(), XY.ap()), inc="DG")
        P.emit("g", lambda g: g.dma_start(d_vs.ap(), VS.ap()), inc="DG")
        P.emit("g", lambda g: g.dma_start(d_ysh.ap(), Ysh.ap()), inc="DG")
        P.q["y"].append(lambda eng, v=ch["DG"].n: eng.wait_ge(sems["DG"], v))

    final_ds = ch["DSO"].n
    P.q["y"].append(lambda eng: eng.wait_ge(sems["DSO"], final_ds))

    with nc.Block() as blk:
        P.run(blk)
    es.close()
    return nc


_cache = {}


def kernel(L0_attn_x, L0_attn_w_attn, L0_attn_w_proj, L0_ffn_W_r, L0_ffn_W_k,
           L0_ffn_W_v, L0_ffn_mu_r, L0_ffn_mu_k):
    x = np.ascontiguousarray(np.asarray(L0_attn_x, dtype=np.float32))
    wa = np.ascontiguousarray(np.asarray(L0_attn_w_attn, dtype=np.float32))
    wp = np.ascontiguousarray(np.asarray(L0_attn_w_proj, dtype=np.float32))
    wr = np.ascontiguousarray(np.asarray(L0_ffn_W_r, dtype=np.float32))
    wk = np.ascontiguousarray(np.asarray(L0_ffn_W_k, dtype=np.float32))
    wv = np.ascontiguousarray(np.asarray(L0_ffn_W_v, dtype=np.float32))
    mur = np.ascontiguousarray(
        np.asarray(L0_ffn_mu_r, dtype=np.float32).reshape(8, 128).T)
    muk = np.ascontiguousarray(
        np.asarray(L0_ffn_mu_k, dtype=np.float32).reshape(8, 128).T)
    qaug, kaug, maskc, identc, biasc, _ = host_consts()

    import os
    _ph = float(os.environ.get("KPHASES", "3"))
    if _ph not in _cache:
        _cache[_ph] = build_nc(_ph)
    nc = _cache[_ph]

    in_maps = []
    for core in range(8):
        in_maps.append({
            "x_in": np.ascontiguousarray(x[2 * core:2 * core + 2]),
            "w_attn": wa, "w_proj": wp, "w_r": wr, "w_k": wk, "w_v": wv,
            "mu_r": mur, "mu_k": muk,
            "qaug": qaug, "kaug": kaug, "maskc": maskc, "identc": identc,
            "biascc": biasc,
        })
    res = run_bass_kernel_spmd(nc, in_maps, core_ids=list(range(8)))
    return np.concatenate([r["y_out"] for r in res.results], axis=0)


# version 10
# speedup vs baseline: 1.2669x; 1.0254x over previous
import math
import numpy as np
import ml_dtypes

import concourse.bass as bass
import concourse.mybir as mybir
from concourse.bass_utils import run_bass_kernel_spmd

F32 = mybir.dt.float32
BF16 = mybir.dt.bfloat16
I32 = mybir.dt.int32
AF = mybir.ActivationFunctionType
ALU = mybir.AluOpType
AX = mybir.AxisListType

B = 2
T = 2048
C = 1024
H = 16
D = 64
NCC = 8
NTT = 16
SEG = 512
ISQRT8 = float(2.0 ** -1.5)
EPS = 1e-5
NEG = -1.0e30
MAGIC = 0x5F3759DF

def _cut_table():
    cuts = []
    for h in range(H):
        sl = 2.0 ** (-(h + 1) / 2.0)
        c = 1 + math.ceil(110.0 / (128.0 * sl))
        cuts.append(min(16, c))
    return cuts

CUT = _cut_table()


def _slopes_f32():
    start = 2.0 ** (-(2.0 ** (-(math.log2(H) - 3))))
    return [np.float32(start * (start ** i)) for i in range(H)]


def host_consts():
    bf = ml_dtypes.bfloat16
    slopes = _slopes_f32()
    pos = np.arange(T)
    qf = (pos % 128).astype(np.float32)
    qh = ((pos // 128) * 128).astype(np.float32)
    kp = qf
    qaug = np.zeros((H, 7, T), dtype=bf)
    kaug = np.zeros((H, 7, T), dtype=bf)
    for h in range(H):
        s = slopes[h]
        s_hi = np.float32(bf(s))
        s_lo = np.float32(bf(np.float32(s - s_hi)))
        qaug[h, 0, :] = bf(0.0)
        qaug[h, 1, :] = bf(s_hi)
        qaug[h, 2, :] = bf(s_lo)
        qaug[h, 3, :] = qf.astype(bf)
        qaug[h, 4, :] = qf.astype(bf)
        qaug[h, 5, :] = qh.astype(bf)
        qaug[h, 6, :] = qh.astype(bf)
        kaug[h, 0, :] = bf(1.0)
        kaug[h, 1, :] = kp.astype(bf)
        kaug[h, 2, :] = kp.astype(bf)
        kaug[h, 3, :] = bf(-s_hi)
        kaug[h, 4, :] = bf(-s_lo)
        kaug[h, 5, :] = bf(-s_hi)
        kaug[h, 6, :] = bf(-s_lo)
    kk = np.arange(128)[:, None]
    qq = np.arange(128)[None, :]
    mask = np.where(kk > qq, np.float32(NEG), np.float32(0.0)).astype(np.float32)
    ident = np.eye(128, dtype=np.float32).astype(bf)
    biasc = np.zeros((128, 272), dtype=np.float32)
    for h in range(H):
        se = eff_slope(h)
        for j in range(16):
            biasc[:, h * 16 + j] = np.float32(se * 128.0 * j)
    biasc[:, 256] = np.float32(0.5)
    biasc[:, 257] = np.float32(np.float32(0.5) ** np.float32(0.5))
    return qaug, kaug, mask, ident, biasc, slopes


def eff_slope(h):
    bf = ml_dtypes.bfloat16
    s = _slopes_f32()[h]
    s_hi = np.float32(bf(s))
    s_lo = np.float32(bf(np.float32(s - s_hi)))
    return float(np.float32(s_hi + s_lo))


class Chan:
    def __init__(self, sem, step):
        self.sem = sem
        self.n = 0
        self.step = step

    def bump(self):
        self.n += self.step
        return self.n


class Plan:

    ENGS = ("t", "v", "s", "y", "g")

    def __init__(self, nc, sems):
        self.nc = nc
        self.q = {e: [] for e in self.ENGS}
        self.ch = {}
        for nm in sems:
            self.ch[nm] = Chan(sems[nm], 1 if nm.startswith("P") else 16)
        self._waited = {e: {} for e in self.ENGS}

    def wait(self, e, cname, tok):
        if tok is None or tok <= 0:
            return
        w = self._waited[e]
        if w.get(cname, 0) >= tok:
            return
        w[cname] = tok
        sem = self.ch[cname].sem
        self.q[e].append(lambda eng: eng.wait_ge(sem, tok))

    def emit(self, e, fn, inc=None, dr=None):
        if dr is None:
            dr = e in ("v", "s") and inc is None
        if inc is None:
            if dr:
                def thunk0(eng):
                    fn(eng)
                    eng.drain()
                self.q[e].append(thunk0)
            else:
                self.q[e].append(lambda eng: fn(eng))
            return None
        ch = self.ch[inc]
        tok = ch.bump()
        sem, step = ch.sem, ch.step

        def thunk(eng):
            fn(eng).then_inc(sem, step)
            if dr:
                eng.drain()

        self.q[e].append(thunk)
        return tok

    def run(self, block):
        block.tensor(lambda eng: [f(eng) for f in self.q["t"]])
        block.vector(lambda eng: [f(eng) for f in self.q["v"]])
        block.scalar(lambda eng: [f(eng) for f in self.q["s"]])
        block.sync(lambda eng: [f(eng) for f in self.q["y"]])
        block.gpsimd(lambda eng: [f(eng) for f in self.q["g"]])


def build_nc(phases=3):
    import os as _os
    KDUMP = int(_os.environ.get("KDUMP", "0"))
    KB = int(_os.environ.get("KB", str(B)))
    nc = bass.Bass("TRN2", target_bir_lowering=False,
                   dynamic_dma_scratch_size=12288)

    x_in = nc.dram_tensor("x_in", [B, T, C], F32, kind="ExternalInput")
    w_attn = nc.dram_tensor("w_attn", [3 * C, C], F32, kind="ExternalInput")
    w_proj = nc.dram_tensor("w_proj", [C, C], F32, kind="ExternalInput")
    w_r = nc.dram_tensor("w_r", [C, C], F32, kind="ExternalInput")
    w_k = nc.dram_tensor("w_k", [C, C], F32, kind="ExternalInput")
    w_v = nc.dram_tensor("w_v", [C, C], F32, kind="ExternalInput")
    mu_r = nc.dram_tensor("mu_r", [128, 8], F32, kind="ExternalInput")
    mu_k = nc.dram_tensor("mu_k", [128, 8], F32, kind="ExternalInput")
    qaug = nc.dram_tensor("qaug", [H, 7, T], BF16, kind="ExternalInput")
    kaug = nc.dram_tensor("kaug", [H, 7, T], BF16, kind="ExternalInput")
    maskc = nc.dram_tensor("maskc", [128, 128], F32, kind="ExternalInput")
    identc = nc.dram_tensor("identc", [128, 128], BF16, kind="ExternalInput")
    biascc = nc.dram_tensor("biascc", [128, 272], F32, kind="ExternalInput")
    y_out = nc.dram_tensor("y_out", [B, T, C], F32, kind="ExternalOutput")

    wattn_b = nc.dram_tensor("wattn_b", [3 * C, C], BF16, kind="Internal")
    wproj_b = nc.dram_tensor("wproj_b", [C, C], BF16, kind="Internal")
    wr_b = nc.dram_tensor("wr_b", [C, C], BF16, kind="Internal")
    wk_b = nc.dram_tensor("wk_b", [C, C], BF16, kind="Internal")
    wv_b = nc.dram_tensor("wv_b", [C, C], BF16, kind="Internal")
    xb16 = nc.dram_tensor("xb16", [B, T, C], BF16, kind="Internal")
    Qs = nc.dram_tensor("Qs", [B, H, 71, T], BF16, kind="Internal")
    Ks = nc.dram_tensor("Ks", [B, H, 71, T], BF16, kind="Internal")
    Ysh = nc.dram_tensor("Ysh", [8, 64, T], BF16, kind="Internal")
    ElRaw = nc.dram_tensor("ElRaw", [H, T], BF16, kind="Internal")
    ElRec = nc.dram_tensor("ElRec", [H, T], BF16, kind="Internal")

    AW = nc.alloc_sbuf_tensor("AW", [128, 16384], BF16)
    BW = nc.alloc_sbuf_tensor("BW", [128, 32768], BF16)
    XY = nc.alloc_sbuf_tensor("XY", [128, 16384], BF16)
    VS = nc.alloc_sbuf_tensor("VS", [128, 16 * 1040], BF16)
    qt = [nc.alloc_sbuf_tensor(f"qt{i}", [128, T], BF16) for i in range(2)]
    kt = [nc.alloc_sbuf_tensor(f"kt{i}", [128, T], BF16) for i in range(2)]
    qkev = [nc.alloc_sbuf_tensor(f"qkev{i}", [128, T], BF16) for i in range(2)]
    expb = [nc.alloc_sbuf_tensor(f"expb{i}", [128, SEG], BF16) for i in range(3)]
    yun = nc.alloc_sbuf_tensor("yun", [128, T], BF16)
    sqb = nc.alloc_sbuf_tensor("sqb", [128, T], BF16)
    x1b = nc.alloc_sbuf_tensor("x1b", [128, C], BF16)
    x1T = nc.alloc_sbuf_tensor("x1T", [128, C], BF16)
    xpT = nc.alloc_sbuf_tensor("xpT", [128, C], BF16)
    xrT = nc.alloc_sbuf_tensor("xrT", [128, C], BF16)
    xkT = nc.alloc_sbuf_tensor("xkT", [128, C], BF16)
    kaT = nc.alloc_sbuf_tensor("kaT", [128, C], BF16)
    mask_sb = nc.alloc_sbuf_tensor("mask_sb", [128, 128], F32)
    biasc_sb = nc.alloc_sbuf_tensor("biasc_sb", [128, 272], F32)
    id_sb = nc.alloc_sbuf_tensor("id_sb", [128, 128], BF16)
    ones_sb = nc.alloc_sbuf_tensor("ones_sb", [128, 64], BF16)
    mur_sb = nc.alloc_sbuf_tensor("mur_sb", [128, 8], F32)
    muk_sb = nc.alloc_sbuf_tensor("muk_sb", [128, 8], F32)
    SM2 = nc.alloc_sbuf_tensor("SM2", [128, 32], F32)
    SMI = nc.alloc_sbuf_tensor("SMI", [128, 4], I32)
    lastc = nc.alloc_sbuf_tensor("lastc", [128, 8], BF16)
    ellp = nc.alloc_sbuf_tensor("ellp", [128, 16], BF16)
    ellr = nc.alloc_sbuf_tensor("ellr", [128, 16], F32)
    ellrb = nc.alloc_sbuf_tensor("ellrb", [128, 16], BF16)

    PS = nc.alloc_psum_tensor("PS", [128, 4096], F32)

    def awf(lo, hi):
        return AW.ap()[:, lo:hi].bitcast(F32)

    x3f = [awf(0, 2048), awf(2048, 4096)]
    sx1 = awf(4096, 6144)
    rtf = [awf(6144, 8192), awf(8192, 10240)]
    outb = [awf(10240, 12288), awf(12288, 14336)]
    ktmp = awf(14336, 16384)

    ssum = SM2.ap()[:, 0:1]
    ssq = SM2.ap()[:, 1:2]
    negm = SM2.ap()[:, 2:3]
    msq = SM2.ap()[:, 3:4]
    var = SM2.ap()[:, 4:5]
    rstd = SM2.ap()[:, 5:6]
    hv = SM2.ap()[:, 6:7]
    nt1 = SM2.ap()[:, 7:8]
    nt2 = SM2.ap()[:, 8:9]
    mkseg = lambda i: SM2.ap()[64:65, 16 + i:17 + i]
    mk2 = SM2.ap()[64:65, 20:21]
    ngmk = SM2.ap()[64:65, 21:22]

    sems = {}
    import contextlib
    es = contextlib.ExitStack()
    for nm in ("PT", "PV", "PS", "DYW", "DYQ", "DYE", "DYR", "DYS", "DY3",
               "DSQ", "DSE", "DSF", "DSY", "DSO", "DG", "DGX", "DGR"):
        sems[nm] = es.enter_context(nc.semaphore(nm))

    P = Plan(nc, sems)
    ch = P.ch

    P.emit("g", lambda g: g.dma_start(wattn_b.ap(), w_attn.ap()), inc="DG")
    P.emit("g", lambda g: g.dma_start(wproj_b.ap(), w_proj.ap()), inc="DG")
    P.emit("g", lambda g: g.dma_start(wr_b.ap(), w_r.ap()), inc="DG")
    P.emit("g", lambda g: g.dma_start(wk_b.ap(), w_k.ap()), inc="DG")
    P.emit("g", lambda g: g.dma_start(wv_b.ap(), w_v.ap()), inc="DG")
    tok_wcast = ch["DG"].n
    for b in range(B):
        for h in range(H):
            P.emit("g", lambda g, b=b, h=h: g.dma_start(
                Qs.ap()[b, h, 64:71, :], qaug.ap()[h]), inc="DG")
            P.emit("g", lambda g, b=b, h=h: g.dma_start(
                Ks.ap()[b, h, 64:71, :], kaug.ap()[h]), inc="DG")
    tok_aug = ch["DG"].n
    P.emit("g", lambda g: g.dma_start(mask_sb.ap(), maskc.ap()), inc="DG")
    P.emit("g", lambda g: g.dma_start(id_sb.ap(), identc.ap()), inc="DG")
    P.emit("g", lambda g: g.dma_start(mur_sb.ap(), mu_r.ap()), inc="DG")
    P.emit("g", lambda g: g.dma_start(muk_sb.ap(), mu_k.ap()), inc="DG")
    P.emit("g", lambda g: g.dma_start(biasc_sb.ap(), biascc.ap()), inc="DG")
    tok_consts = ch["DG"].n
    _reg = set()

    def _regconst(val, col):
        if val in _reg or val == 0.0:
            return
        _reg.add(val)
        nc.const_aps.register(F32, val, biasc_sb.ap()[:, col:col + 1])

    for _h in range(H):
        _se = eff_slope(_h)
        for _j in range(16):
            _regconst(float(np.float32(_se * 128.0 * _j)), _h * 16 + _j)
    _regconst(0.5, 256)
    _regconst(float(np.float32(0.5 ** 0.5)), 257)
    tok_xb = [None] * B
    for b in range(B):
        tok_xb[b] = P.emit("g", lambda g, b=b: g.dma_start(
            xb16.ap()[b], x_in.ap()[b]), inc="DGX")

    P.emit("v", lambda v: v.memset(ones_sb.ap(), 1.0))
    P.emit("v", lambda v: v.memset(SMI.ap()[:, 0:1], MAGIC))
    vs_ones = VS.ap().rearrange("p (g x) -> p g x", x=65)[:, :, 64:65]
    tok_vinit = P.emit("v", lambda v: v.memset(vs_ones, 1.0), inc="PV")

    def wslab(dst_ap, src, r0, c0):
        return lambda y: y.dma_start(dst_ap, src.ap()[r0:r0 + 1024, c0:c0 + 128],
                                     transpose=True)

    def load_wq():
        P.wait("y", "DG", tok_wcast)
        for c in range(NCC):
            P.emit("y", wslab(AW.ap()[:, c * 1024:(c + 1) * 1024],
                              wattn_b, 0, c * 128), inc="DYW")
        return ch["DYW"].n

    def load_wk_slot1():
        for c in range(NCC):
            P.emit("y", wslab(AW.ap()[:, 8192 + c * 1024:8192 + (c + 1) * 1024],
                              wattn_b, 1024, c * 128), inc="DYW")
        return ch["DYW"].n

    def load_wv_slot0():
        for c in range(NCC):
            P.emit("y", wslab(AW.ap()[:, c * 1024:(c + 1) * 1024],
                              wattn_b, 2048, c * 128), inc="DYW")
        return ch["DYW"].n

    def load_bw():
        for i, wsrc in enumerate((wproj_b, wr_b, wk_b, wv_b)):
            for c in range(NCC):
                P.emit("y", wslab(
                    BW.ap()[:, i * 8192 + c * 1024:i * 8192 + (c + 1) * 1024],
                    wsrc, 0, c * 128), inc="DYW")
        return ch["DYW"].n

    def load_xT(b):
        P.wait("y", "DGX", tok_xb[b])
        for c in range(NCC):
            P.emit("y", lambda y, b=b, c=c: y.dma_start(
                XY.ap()[:, c * T:(c + 1) * T],
                xb16.ap()[b][:, c * 128:(c + 1) * 128], transpose=True),
                inc="DYW")
        return ch["DYW"].n

    def aw_q(c):
        return AW.ap()[:, c * 1024:(c + 1) * 1024]

    def aw_k(c):
        return AW.ap()[:, 8192 + c * 1024:8192 + (c + 1) * 1024]

    def bw(i, c):
        return BW.ap()[:, i * 8192 + c * 1024:i * 8192 + (c + 1) * 1024]

    def xT(c):
        return XY.ap()[:, c * T:(c + 1) * T]

    yT = xT

    tok_wq = load_wq()
    tok_wk = load_wk_slot1()
    tok_xT = [None] * B
    tok_xT[0] = load_xT(0)
    tok_bw = load_bw()

    tok_p3_done_t = 0
    tok_p3_done_v = 0
    tok_p3_done_s = 0
    tok_p3_out_d = 0
    slopes_eff = [eff_slope(h) for h in range(H)]
    tok_s_tile = [0] * NTT
    tok_tp_evac = 0
    tok_ffn = {0: 0, 1: 0}
    tok_out = {0: 0, 1: 0}
    tok_out_dma = {0: 0, 1: 0}
    tok_sq2 = 0
    tok_bl_prev = 0
    tok_mm2_prev = 0
    tok_mm3_prev = 0
    st2 = {
        "exp_by_bank": {0: 0, 1: 0, 2: 0},
        "sbank": 0,
        "expn": 0,
        "av_by_expb": {0: 0, 1: 0, 2: 0},
        "ye": 0,
        "rell": 0,
        "norm": 0,
        "yun_free_ds": 0,
    }

    for b in range(KB):
        tok_qs_w = {}
        grp_evac = {0: 0, 1: 0}
        if b > 0:
            P.wait("y", "PT", tok_p3_done_t)
            P.wait("y", "PV", tok_p3_done_v)
            P.wait("y", "PS", tok_p3_done_s)
            P.wait("y", "DSO", tok_p3_out_d)
            tok_wq = load_wq()
            tok_wk = load_wk_slot1()
            tok_xT[b] = load_xT(b)

        last_qk_mm = 0
        tok_p1_evac_last = 0
        _wl = []
        if phases >= 0.5:
            _wl.append(("q", aw_q, Qs))
        if phases >= 0.7:
            _wl.append(("k", aw_k, Ks))
        for which, wsel, scratch in _wl:
            P.wait("t", "DYW", tok_wq if which == "q" else tok_wk)
            P.wait("t", "DYW", tok_xT[b])
            if b > 0:
                P.wait("t", "PV", tok_p3_done_v)
                P.wait("t", "PS", tok_p3_done_s)
            for dt in range(8):
                gi = dt % 2
                grp = gi * 2048
                P.wait("t", "PV", grp_evac[gi])
                tok_mm = None
                for cc in range(NCC):
                    for sgi in range(4):
                        tok_mm = P.emit("t", lambda t, w=wsel(cc), cc=cc, dt=dt,
                                        sgi=sgi, grp=grp: t.matmul(
                            PS.ap()[:, grp + sgi * 512:grp + (sgi + 1) * 512],
                            w[:, dt * 128:(dt + 1) * 128],
                            xT(cc)[:, sgi * 512:(sgi + 1) * 512],
                            start=(cc == 0), stop=(cc == NCC - 1)),
                            inc="PT" if (cc == NCC - 1 and sgi == 3) else None)
                last_qk_mm = tok_mm
                ev = qkev[gi]
                prev = tok_qs_w.get((which, dt - 2)) or (
                    tok_qs_w.get(("q", dt + 6)) if which == "k" else None)
                P.wait("v", "DSQ", prev)
                P.wait("v", "PT", tok_mm)
                tok_ev = None
                for sgi in range(4):
                    tok_ev = P.emit("v", lambda v, ev=ev, grp=grp, sgi=sgi:
                                    v.tensor_scalar(
                        ev.ap()[:, sgi * 512:(sgi + 1) * 512],
                        PS.ap()[:, grp + sgi * 512:grp + (sgi + 1) * 512],
                        ISQRT8, None, ALU.mult),
                        inc="PV" if sgi == 3 else None,
                        dr=False if sgi < 3 else None)
                grp_evac[gi] = tok_ev
                tok_p1_evac_last = tok_ev
                P.wait("s", "PV", tok_ev)
                P.wait("s", "DG", tok_aug)
                tk = None
                for half in range(2):
                    tk = P.emit("s", lambda s, ev=ev, b=b, dt=dt, half=half,
                                scratch=scratch: s.dma_start(
                        scratch.ap()[b, 2 * dt + half, 0:64, :],
                        ev.ap()[64 * half:64 * half + 64, :]), inc="DSQ",
                        dr=False)
                tok_qs_w[(which, dt)] = tk

        if phases < 1:
            continue
        P.wait("y", "PT", last_qk_mm)
        tok_wv = load_wv_slot0()
        P.wait("t", "DYW", tok_wv)
        P.wait("t", "PV", grp_evac[0])
        vgrp_evac = {0: 0, 1: 0}
        tok_v_evac = 0
        for tt in range(NTT):
            gi = tt % 2
            grp = gi * 1024
            P.wait("t", "PV", vgrp_evac[gi])
            tok_mm = None
            for cc in range(NCC):
                for sgi in range(2):
                    tok_mm = P.emit("t", lambda t, cc=cc, tt=tt, sgi=sgi, grp=grp:
                                    t.matmul(
                        PS.ap()[:, grp + sgi * 512:grp + (sgi + 1) * 512],
                        xT(cc)[:, tt * 128:(tt + 1) * 128],
                        aw_q(cc)[:, sgi * 512:(sgi + 1) * 512],
                        start=(cc == 0), stop=(cc == NCC - 1)),
                        inc="PT" if (cc == NCC - 1 and sgi == 1) else None)
            P.wait("v", "PT", tok_mm)
            P.wait("v", "PV", tok_vinit)
            src = PS.ap()[:, grp:grp + 1024].rearrange("p (h e) -> p h e", e=64)
            dst = VS.ap()[:, tt * 1040:(tt + 1) * 1040].rearrange(
                "p (h e) -> p h e", e=65)[:, :, 0:64]
            tok_v_evac = P.emit("v", lambda v, src=src, dst=dst:
                                v.tensor_copy(dst, src), inc="PV")
            vgrp_evac[gi] = tok_v_evac

        if phases < 2:
            tok_p3_done_t = ch["PT"].n
            tok_p3_done_v = ch["PV"].n
            tok_p3_done_s = ch["PS"].n
            tok_p3_out_d = ch["DSO"].n
            continue
        Yps = PS.ap()[0:65, 0:2048]
        Yl = PS.ap()[64:65, 0:2048]
        Sb = [PS.ap()[:, 2048 + 512 * i:2048 + 512 * (i + 1)] for i in (0, 1, 3)]
        nrm = PS.ap()[64:65, 3072:3584]

        tok_qk_load = {}
        tok_ub = {}
        tok_scores_last = {}

        def p2_load(h):
            buf = h % 2
            P.wait("y", "DSQ", ch["DSQ"].n if h == 0 else
                   max(tok_qs_w[("q", h // 2)], tok_qs_w[("k", h // 2)]))
            P.wait("y", "PT", tok_scores_last.get(h - 2))
            P.wait("y", "PV", tok_ub.get(h - 2))
            P.emit("y", lambda y, b=b, h=h, buf=buf: y.dma_start(
                kt[buf].ap()[0:71, :], Ks.ap()[b, h]), inc="DYQ")
            tok_qk_load[h] = P.emit("y", lambda y, b=b, h=h, buf=buf: y.dma_start(
                qt[buf].ap()[0:71, :], Qs.ap()[b, h]), inc="DYQ")

        def p2_prep(h):
            buf = h % 2
            P.wait("v", "DYQ", tok_qk_load[h])
            tk = P.emit("v", lambda v, buf=buf: v.tensor_mul(
                sqb.ap()[0:64, :], kt[buf].ap()[0:64, :], kt[buf].ap()[0:64, :]),
                inc="PV")
            P.wait("t", "PV", tok_vinit)
            if h == 0:
                P.wait("t", "PV", tok_p1_evac_last)
            last_red = 0
            for sgi in range(4):
                P.wait("t", "PV", tk)
                P.wait("t", "PV", last_red)
                tm = P.emit("t", lambda t, sgi=sgi: t.matmul(
                    nrm, ones_sb.ap()[0:64, 0:1],
                    sqb.ap()[0:64, sgi * 512:(sgi + 1) * 512],
                    start=True, stop=True), inc="PT")
                yield
                P.wait("v", "PT", tm)
                last_red = P.emit("v", lambda v, sgi=sgi: v.reduce_max(
                    mkseg(sgi), nrm, AX.X), inc="PV", dr=True)
            P.wait("v", "PT", ch["PT"].n)
            tkq = P.emit("v", lambda v, buf=buf: v.tensor_mul(
                sqb.ap()[0:64, :], qt[buf].ap()[0:64, :], qt[buf].ap()[0:64, :]),
                inc="PV")
            P.emit("v", lambda v: v.reduce_max(mk2, SM2.ap()[64:65, 16:20], AX.X))
            tmk = P.emit("v", lambda v: v.tensor_scalar(
                ngmk, mk2, -0.5, None, ALU.mult), inc="PV")
            lastu = 0
            for sgi in range(4):
                P.wait("t", "PV", tkq)
                P.wait("t", "PV", lastu)
                tm = P.emit("t", lambda t, sgi=sgi: t.matmul(
                    nrm, ones_sb.ap()[0:64, 0:1],
                    sqb.ap()[0:64, sgi * 512:(sgi + 1) * 512],
                    start=True, stop=True), inc="PT")
                yield
                P.wait("v", "PT", tm)
                P.wait("v", "PV", tmk)
                lastu = P.emit("v", lambda v, buf=buf, sgi=sgi: v.tensor_scalar(
                    qt[buf].ap()[64:65, sgi * 512:(sgi + 1) * 512],
                    nrm, -0.5, ngmk, ALU.mult, ALU.add), inc="PV")
            tok_ub[h] = lastu

        def p2_dump_qt(h):
            if not (KDUMP == 2 and b == 0 and h == int(_os.environ.get("KH", "1"))):
                return
            buf = h % 2
            d_qt = nc.dram_tensor("d_qt", [128, T], BF16, kind="ExternalOutput")
            d_kt = nc.dram_tensor("d_kt", [128, T], BF16, kind="ExternalOutput")
            P.wait("s", "PV", tok_ub[h])
            P.emit("s", lambda s, buf=buf: s.dma_start(d_qt.ap(), qt[buf].ap()),
                   inc="DSO", dr=False)
            P.emit("s", lambda s, buf=buf: s.dma_start(d_kt.ap(), kt[buf].ap()),
                   inc="DSO", dr=False)

        def p2_head(h, first_of_batch, prep_gen=None, fin_gen=None,
                    do_av=True, do_fin=True):
            buf = h % 2
            cut = CUT[h]
            sl = slopes_eff[h]
            firstj = {}
            lastj = {}
            for j in range(NTT):
                qmax = min(T, 128 * (j + cut))
                for bank in range((128 * j) // 512, (qmax + 511) // 512):
                    firstj.setdefault(bank, j)
                    lastj[bank] = j
            tok_av_last = 0
            pend_av = []

            def emit_av(j, qa, qb, eb, ei, tok_exp):
                nonlocal tok_av_last
                P.wait("t", "PS", tok_exp)
                if first_of_batch and j == 0 and qa == 128 * j:
                    P.wait("t", "PV", tok_v_evac)
                if j == 0 and qa == 0:
                    P.wait("t", "PS", st2["ye"])
                    P.wait("t", "PV", st2["rell"])
                bk0 = qa // 512
                bk1 = (qb - 1) // 512
                for bank in range(bk0, bk1 + 1):
                    ca = max(qa, bank * 512)
                    cb = min(qb, (bank + 1) * 512)
                    stf = (firstj[bank] == j)
                    spf = (lastj[bank] == j)
                    tok_av_last = P.emit("t", lambda t, h=h, j=j, ca=ca,
                                         cb=cb, qa=qa, eb=eb, stf=stf,
                                         spf=spf: t.matmul(
                        Yps[:, ca:cb],
                        VS.ap()[:, j * 1040 + h * 65:j * 1040 + h * 65 + 65],
                        eb.ap()[:, ca - qa:cb - qa], start=stf, stop=spf),
                        inc="PT")
                st2["av_by_expb"][ei] = tok_av_last

            for j in range(NTT):
                if prep_gen is not None:
                    next(prep_gen, None)
                if fin_gen is not None and j in (8, 10, 12, 14):
                    next(fin_gen, None)
                qmax = min(T, 128 * (j + cut))
                q0 = 128 * j
                bounds = [q0]
                nb = (q0 // 512 + 1) * 512
                while nb < qmax:
                    bounds.append(nb)
                    nb += 512
                bounds.append(qmax)
                for si in range(len(bounds) - 1):
                    qa, qb = bounds[si], bounds[si + 1]
                    sbk = st2["sbank"]
                    sb = Sb[sbk]
                    P.wait("t", "PV", tok_ub[h])
                    P.wait("t", "PS", st2["exp_by_bank"][sbk])
                    if first_of_batch and j == 0 and si == 0:
                        P.wait("t", "PV", tok_p1_evac_last)
                    tok_sc = P.emit("t", lambda t, buf=buf, j=j, qa=qa, qb=qb,
                                    sb=sb: t.matmul(
                        sb[:, 0:qb - qa], kt[buf].ap()[0:71, j * 128:(j + 1) * 128],
                        qt[buf].ap()[0:71, qa:qb], start=True, stop=True),
                        inc="PT")
                    tok_scores_last[h] = tok_sc
                    tok_mask = None
                    if si == 0:
                        P.wait("v", "DG", tok_consts)
                        P.wait("v", "PT", tok_sc)
                        tok_mask = P.emit("v", lambda v, sb=sb: v.tensor_add(
                            sb[:, 0:128], sb[:, 0:128], mask_sb.ap()), inc="PV")
                    ei = st2["expn"] % 3
                    eb = expb[ei]
                    st2["expn"] += 1
                    if tok_mask is not None:
                        P.wait("s", "PV", tok_mask)
                    P.wait("s", "DG", tok_consts)
                    P.wait("s", "PT", tok_sc)
                    P.wait("s", "PT", st2["av_by_expb"][ei])
                    bias = float(np.float32(sl * 128.0 * j))
                    tok_exp = P.emit("s", lambda s, sb=sb, eb=eb, qa=qa, qb=qb,
                                     bias=bias: s.activation(
                        eb.ap()[:, 0:qb - qa], sb[:, 0:qb - qa], AF.Exp,
                        bias=bias), inc="PS")
                    st2["exp_by_bank"][sbk] = tok_exp
                    st2["sbank"] = (sbk + 1) % 3
                    if not do_av:
                        continue
                    pend_av.append((j, qa, qb, eb, ei, tok_exp))
                    if len(pend_av) > 1:
                        emit_av(*pend_av.pop(0))
            while pend_av:
                emit_av(*pend_av.pop(0))
            return tok_av_last

        def p2_fin(h, tok_av_last, first_of_batch):
            import os as _os
            KFIN = int(_os.environ.get("KFIN", "31"))
            pair = h // 2
            odd = h % 2
            P.wait("v", "PT", tok_av_last)
            tok_rell = None
            tok_rell_done = None
            if KFIN & 1:
                tok_rell = P.emit("v", lambda v: v.tensor_copy(
                    sqb.ap()[64:65, 0:2048], Yl), inc="PV")
                P.wait("s", "PV", tok_rell)
                t1 = P.emit("s", lambda s, h=h: s.dma_start(
                    ElRaw.ap()[h:h + 1, :], sqb.ap()[64:65, 0:2048]),
                    inc="DSE", dr=False)
                P.wait("y", "DSE", t1)
                P.wait("y", "PV", st2.get("ellp_free", 0))
                t2 = P.emit("y", lambda y, h=h: y.dma_start(
                    ellp.ap(), ElRaw.ap()[h].rearrange("(p c) -> p c", p=128)),
                    inc="DYE")
                P.wait("v", "DYE", t2)
                tok_rec = P.emit("v", lambda v: v.reciprocal(
                    ellr.ap(), ellp.ap()), inc="PV", dr=True)
                st2["ellp_free"] = tok_rec
                tok_recb = P.emit("v", lambda v: v.tensor_copy(
                    ellrb.ap(), ellr.ap()), inc="PV", dr=False)
                P.wait("s", "PV", tok_recb)
                t3 = P.emit("s", lambda s, h=h: s.dma_start(
                    ElRec.ap()[h].rearrange("(p c) -> p c", p=128), ellrb.ap()),
                    inc="DSF", dr=False)
                P.wait("y", "DSF", t3)
                P.wait("y", "PV", st2.get("bcb_free", 0))
                tok_rell_done = P.emit("y", lambda y, h=h: y.dma_start(
                    qkev[0].ap()[0:64, :].rearrange("p (o c) -> p o c", o=1),
                    ElRec.ap()[h:h + 1, :].partition_broadcast(64)), inc="DYR")
            tok_ye = None
            if KFIN & 2:
                P.wait("s", "PT", tok_av_last)
                P.wait("s", "PV", st2["norm"])
                P.wait("s", "DSY", st2.get("yun_free_ds", 0))
                tok_ye = P.emit("s", lambda s: s.activation(
                    yun.ap()[0:64, :], PS.ap()[0:64, 0:2048], AF.Copy), inc="PS")
            if KDUMP == 2 and b == 0 and h == int(_os.environ.get("KH", "1")):
                d_yun = nc.dram_tensor("d_yun", [64, T], BF16, kind="ExternalOutput")
                P.wait("s", "PS", tok_ye)
                P.emit("s", lambda s: s.dma_start(d_yun.ap(), yun.ap()[0:64, :]),
                       inc="DSO", dr=False)
                d_rell = nc.dram_tensor("d_rell", [1, T], BF16, kind="ExternalOutput")
                P.wait("s", "PV", tok_rell)
                P.emit("s", lambda s: s.dma_start(d_rell.ap(), sqb.ap()[64:65, :]),
                       inc="DSO", dr=False)
            if tok_ye:
                st2["ye"] = tok_ye
            if tok_rell:
                st2["rell"] = tok_rell
            tok_norm = 0
            bcbS = qkev[0].ap()[0:64, :]
            yield
            yield
            if KFIN & 8:
                P.wait("v", "DYR", tok_rell_done)
                P.wait("v", "PS", tok_ye)
                if odd:
                    tok_norm = P.emit("v", lambda v: v.tensor_mul(
                        yun.ap()[0:64, :], yun.ap()[0:64, :], bcbS),
                        inc="PV")
                else:
                    tok_norm = P.emit("v", lambda v, pair=pair: v.tensor_mul(
                        yT(pair)[0:64, :], yun.ap()[0:64, :], bcbS),
                        inc="PV")
                st2["bcb_free"] = tok_norm
            if (KFIN & 16) and odd:
                slot = pair
                P.wait("s", "PV", tok_norm)
                tw = P.emit("s", lambda s, slot=slot: s.dma_start(
                    Ysh.ap()[slot], yun.ap()[0:64, :]), inc="DSY", dr=False)
                P.wait("y", "DSY", tw)
                tr = P.emit("y", lambda y, pair=pair, slot=slot: y.dma_start(
                    XY.ap()[64:128, pair * T:(pair + 1) * T], Ysh.ap()[slot]),
                    inc="DYS")
                st2["ydma"] = tr
                tok_norm = None
                st2["yun_free_ds"] = tw
            if tok_norm:
                st2["norm"] = tok_norm

        p2_load(0)
        if phases >= 1.5:
            for _ in p2_prep(0):
                pass
        fin_prev = None
        for h in range(H):
            if h + 1 < H:
                p2_load(h + 1)
            if phases < 1.5:
                continue
            gen_next = p2_prep(h + 1) if h + 1 < H else None
            p2_dump_qt(h)
            if phases < 1.7:
                continue
            tok_avl = p2_head(h, first_of_batch=(h == 0), prep_gen=gen_next,
                              fin_gen=fin_prev, do_av=(phases >= 1.8),
                              do_fin=(phases >= 1.9))
            if gen_next is not None:
                for _ in gen_next:
                    pass
            if fin_prev is not None:
                for _ in fin_prev:
                    pass
            fin_prev = None
            if phases >= 1.9 and tok_avl is not None:
                fin_prev = p2_fin(h, tok_avl, first_of_batch=(h == 0))
                next(fin_prev, None)
        if fin_prev is not None:
            for _ in fin_prev:
                pass
        tok_norm_last = st2["norm"]
        tok_ydma_last = st2.get("ydma", 0)
        tok_ye_last = st2["ye"]

        if phases < 3:
            tok_p3_done_t = ch["PT"].n
            tok_p3_done_v = ch["PV"].n
            tok_p3_done_s = ch["PS"].n
            tok_p3_out_d = ch["DSO"].n
            continue
        P1p = PS.ap()[:, 0:1024]
        R1 = PS.ap()[:, 1024:2048]
        K2 = PS.ap()[:, 2048:3072]
        P3p = PS.ap()[:, 3072:4096]
        TpA = PS.ap()[:, 0:512].bitcast(BF16)

        tok_x3 = [None] * NTT
        P.wait("y", "PT", ch["PT"].n)

        def load_x3(tt):
            if tt >= 2:
                P.wait("y", "PV", tok_s_tile[tt - 2])
            tok_x3[tt] = P.emit("y", lambda y, b=b, tt=tt: y.dma_start(
                x3f[tt % 2], x_in.ap()[b, tt * 128:(tt + 1) * 128, :]),
                inc="DY3")

        load_x3(0)
        tok_lc_t = {-1: P.emit("v", lambda v: v.memset(lastc.ap(), 0.0),
                               inc="PV")}
        t3 = {}

        def p3_proj(tt):
            P.wait("t", "DYW", tok_bw)
            P.wait("t", "PV", tok_norm_last)
            P.wait("t", "DYS", tok_ydma_last)
            if tt == 0:
                P.wait("t", "PS", ch["PS"].n)
            else:
                P.wait("t", "PV", t3[tt - 1]["tp_evac"])
            tok = None
            for dc in range(NCC):
                for sgi in range(2):
                    tok = P.emit("t", lambda t, dc=dc, tt=tt, sgi=sgi:
                                 t.matmul(
                        P1p[:, sgi * 512:(sgi + 1) * 512],
                        yT(dc)[:, tt * 128:(tt + 1) * 128],
                        bw(0, dc)[:, sgi * 512:(sgi + 1) * 512],
                        start=(dc == 0), stop=(dc == NCC - 1)),
                        inc="PT" if (dc == NCC - 1 and sgi == 1) else None)
            t3.setdefault(tt, {})["proj"] = tok

        def p3_mm12(u):
            P.wait("t", "PV", t3[u]["bl"])
            if u > 0:
                P.wait("t", "PS", t3[u - 1]["sq2"])
                P.wait("t", "PV", t3[u - 1]["ffn"])
            tok = None
            for cc in range(NCC):
                for sgi in range(2):
                    tok = P.emit("t", lambda t, cc=cc, sgi=sgi: t.matmul(
                        R1[:, sgi * 512:(sgi + 1) * 512],
                        xrT.ap()[:, cc * 128:(cc + 1) * 128],
                        bw(1, cc)[:, sgi * 512:(sgi + 1) * 512],
                        start=(cc == 0), stop=(cc == NCC - 1)),
                        inc="PT" if (cc == NCC - 1 and sgi == 1) else None)
            t3[u]["mm1"] = tok
            tok = None
            for dc in range(NCC):
                for cc in range(NCC):
                    tok = P.emit("t", lambda t, dc=dc, cc=cc: t.matmul(
                        K2[:, dc * 128:(dc + 1) * 128],
                        bw(2, cc)[:, dc * 128:(dc + 1) * 128],
                        xkT.ap()[:, cc * 128:(cc + 1) * 128],
                        start=(cc == 0), stop=(cc == NCC - 1)),
                        inc="PT" if (dc == NCC - 1 and cc == NCC - 1) else None)
            t3[u]["mm2"] = tok

        def p3_act(u):
            par = u % 2
            P.wait("s", "PT", t3[u]["mm1"])
            if u > 1:
                P.wait("s", "PV", t3[u - 2]["s2"])
            t3[u]["th"] = P.emit("s", lambda s, u=u: s.activation(
                rtf[u % 2], R1, AF.Tanh, scale=0.5), inc="PS")
            P.wait("s", "PT", t3[u]["mm2"])
            t3[u]["rel"] = P.emit("s", lambda s: s.activation(
                ktmp, K2, AF.Relu), inc="PS", dr=True)
            if u > 0:
                P.wait("s", "PT", t3[u - 1]["mm3"])
            t3[u]["ka"] = P.emit("s", lambda s: s.activation(
                kaT.ap(), ktmp, AF.Square, scale=float(np.float32(0.5 ** 0.5))),
                inc="PS")

        def p3_mm3(u):
            P.wait("t", "PS", t3[u]["ka"])
            tok = None
            for dc in range(NCC):
                for sgi in range(2):
                    tok = P.emit("t", lambda t, dc=dc, sgi=sgi: t.matmul(
                        P3p[:, sgi * 512:(sgi + 1) * 512],
                        kaT.ap()[:, dc * 128:(dc + 1) * 128],
                        bw(3, dc)[:, sgi * 512:(sgi + 1) * 512],
                        start=(dc == 0), stop=(dc == NCC - 1)),
                        inc="PT" if (dc == NCC - 1 and sgi == 1) else None)
            t3[u]["mm3"] = tok

        def newton(tok_sq, c):
            P.wait("v", "PS", tok_sq)
            P.emit("v", lambda v: v.tensor_scalar(
                c["negm"], c["ssum"], -1.0 / C, None, ALU.mult))
            P.emit("v", lambda v: v.tensor_mul(c["msq"], c["negm"], c["negm"]))
            P.emit("v", lambda v: v.tensor_scalar(
                c["nt1"], c["ssq"], 1.0 / C, EPS, ALU.mult, ALU.add))
            P.emit("v", lambda v: v.tensor_sub(c["var"], c["nt1"], c["msq"]))
            P.emit("v", lambda v: v.tensor_scalar(
                c["smi"], c["var"].bitcast(I32), 1, None,
                ALU.arith_shift_right))
            P.emit("v", lambda v: v.scalar_tensor_tensor(
                c["rstd"].bitcast(I32), SMI.ap()[:, 0:1], 0, c["smi"],
                ALU.add, ALU.subtract))
            P.emit("v", lambda v: v.tensor_scalar(
                c["hv"], c["var"], -0.5, None, ALU.mult))
            for _ in range(3):
                P.emit("v", lambda v: v.tensor_mul(c["nt1"], c["rstd"], c["rstd"]))
                P.emit("v", lambda v: v.tensor_scalar(
                    c["nt2"], c["nt1"], c["hv"], 1.5, ALU.mult, ALU.add))
                P.emit("v", lambda v: v.tensor_mul(c["rstd"], c["rstd"], c["nt2"]))
            return P.emit("v", lambda v: v.tensor_mul(
                c["nt1"], c["negm"], c["rstd"]), inc="PV", dr=True)

        LN1C = {"ssum": ssum, "ssq": ssq, "negm": negm, "msq": msq, "var": var,
                "rstd": rstd, "hv": hv, "nt1": nt1, "nt2": nt2,
                "smi": SMI.ap()[:, 1:2]}
        LN2C = {"ssum": SM2.ap()[:, 9:10], "ssq": SM2.ap()[:, 10:11],
                "negm": SM2.ap()[:, 11:12], "msq": SM2.ap()[:, 12:13],
                "var": SM2.ap()[:, 13:14], "rstd": SM2.ap()[:, 14:15],
                "hv": SM2.ap()[:, 15:16], "nt1": SM2.ap()[:, 24:25],
                "nt2": SM2.ap()[:, 25:26], "smi": SMI.ap()[:, 2:3]}

        def p3_ln2(u):
            par = u % 2
            P.wait("v", "PT", t3[u]["mm3"])
            P.wait("v", "PS", t3[u]["th"])
            t3[u]["ffn"] = P.emit("v", lambda v, u=u: v.scalar_tensor_tensor(
                rtf[u % 2], rtf[u % 2], 1.0, P3p, ALU.add, ALU.mult),
                inc="PV", dr=True)
            t3[u]["s2"] = P.emit("v", lambda v, u=u: v.scalar_tensor_tensor(
                rtf[u % 2], rtf[u % 2], 1.0, sx1, ALU.mult, ALU.add,
                accum_out=LN2C["ssum"]), inc="PV")
            P.wait("s", "PV", t3[u]["s2"])
            P.wait("s", "DSO", tok_out_dma[par])
            t3[u]["sq2"] = P.emit("s", lambda s, u=u: s.activation(
                outb[u % 2], rtf[u % 2], AF.Square, accum_out=LN2C["ssq"]),
                inc="PS")
            tok_nmr2 = newton(t3[u]["sq2"], LN2C)
            P.wait("s", "PV", tok_nmr2)
            tok_out[par] = P.emit("s", lambda s, u=u: s.activation(
                outb[u % 2], rtf[u % 2], AF.Identity, bias=LN2C["nt1"],
                scale=LN2C["rstd"]), inc="PS", dr=True)
            P.wait("s", "PS", tok_out[par])
            tok_out_dma[par] = P.emit("s", lambda s, b=b, u=u: s.dma_start(
                y_out.ap()[b, u * 128:(u + 1) * 128, :], outb[u % 2]),
                inc="DSO", dr=False)

        def p3_ln1(tt):
            par = tt % 2
            P.wait("v", "PT", t3[tt]["proj"])
            P.wait("v", "DY3", tok_x3[tt])
            tok_s = P.emit("v", lambda v, tt=tt: v.scalar_tensor_tensor(
                sx1, P1p, 1.0, x3f[tt % 2], ALU.mult, ALU.add,
                accum_out=ssum), inc="PV")
            t3[tt]["s"] = tok_s
            tok_s_tile[tt] = tok_s
            P.wait("s", "PV", tok_s)
            P.wait("s", "DSO", tok_out_dma[par])
            tok_ssq = P.emit("s", lambda s, tt=tt: s.activation(
                outb[tt % 2], sx1, AF.Square, accum_out=ssq), inc="PS")
            tok_nmr = newton(tok_ssq, LN1C)
            P.wait("s", "PV", tok_nmr)
            if tt > 0:
                P.wait("s", "PV", t3[tt - 1]["bl"])
            tok_x1 = P.emit("s", lambda s: s.activation(
                sx1, sx1, AF.Identity, bias=nt1, scale=rstd), inc="PS",
                dr=True)
            t3[tt]["x1b"] = P.emit("s", lambda s: s.activation(
                x1b.ap(), sx1, AF.Copy), inc="PS")

        def p3_blend(tt):
            P.wait("t", "PS", t3[tt]["x1b"])
            P.wait("t", "PV", t3[tt]["s"])
            P.wait("t", "DG", tok_consts)
            tok_tp = None
            for cc in range(NCC):
                tok_tp = P.emit("t", lambda t, cc=cc: t.transpose(
                    TpA[:, cc * 128:(cc + 1) * 128],
                    x1b.ap()[:, cc * 128:(cc + 1) * 128], id_sb.ap()),
                    inc="PT" if cc == NCC - 1 else None)
            P.wait("v", "PT", tok_tp)
            t3[tt]["tp_evac"] = P.emit("v", lambda v: v.tensor_copy(
                x1T.ap(), TpA), inc="PV")
            P.wait("s", "PV", t3[tt]["tp_evac"])
            P.wait("s", "PV", tok_lc_t[tt - 1])
            src_sh = x1T.ap().rearrange("p (c x) -> p c x", x=128)[:, :, 0:127]
            dst_sh = xpT.ap().rearrange("p (c x) -> p c x", x=128)[:, :, 1:128]
            P.emit("s", lambda s: s.copy(dst_sh, src_sh))
            tok_xp = P.emit("s", lambda s: s.copy(
                xpT.ap().rearrange("p (c x) -> p c x", x=128)[:, :, 0:1],
                lastc.ap().rearrange("p (c o) -> p c o", o=1)), inc="PS")
            P.wait("v", "PS", tok_xp)
            tok_lc_t[tt] = P.emit("v", lambda v: v.tensor_copy(
                lastc.ap(),
                x1T.ap().rearrange("p (c x) -> p c x", x=128)[:, :, 127:128]
                .rearrange("p c x -> p (c x)")), inc="PV")
            tok_d = P.emit("v", lambda v: v.tensor_sub(
                x1b.ap(), x1T.ap(), xpT.ap()), inc="PV", dr=True)
            if tt > 0:
                P.wait("v", "PT", t3[tt - 1]["mm2"])
            P.wait("v", "DG", tok_consts)
            tok_bl = None
            for cc in range(NCC):
                P.emit("v", lambda v, cc=cc: v.scalar_tensor_tensor(
                    xrT.ap()[:, cc * 128:(cc + 1) * 128],
                    x1b.ap()[:, cc * 128:(cc + 1) * 128],
                    mur_sb.ap()[:, cc:cc + 1],
                    xpT.ap()[:, cc * 128:(cc + 1) * 128], ALU.mult, ALU.add),
                    dr=False)
                tok_bl = P.emit("v", lambda v, cc=cc: v.scalar_tensor_tensor(
                    xkT.ap()[:, cc * 128:(cc + 1) * 128],
                    x1b.ap()[:, cc * 128:(cc + 1) * 128],
                    muk_sb.ap()[:, cc:cc + 1],
                    xpT.ap()[:, cc * 128:(cc + 1) * 128], ALU.mult, ALU.add),
                    inc="PV" if cc == NCC - 1 else None,
                    dr=False if cc < NCC - 1 else None)
            t3[tt]["bl"] = tok_bl

        for tt in range(NTT):
            if tt + 1 < NTT:
                load_x3(tt + 1)
            p3_proj(tt)
            if tt > 0:
                p3_mm12(tt - 1)
                p3_act(tt - 1)
                p3_mm3(tt - 1)
                p3_ln2(tt - 1)
            p3_ln1(tt)
            p3_blend(tt)
        p3_mm12(NTT - 1)
        p3_act(NTT - 1)
        p3_mm3(NTT - 1)
        p3_ln2(NTT - 1)

        tok_p3_done_t = ch["PT"].n
        tok_p3_done_v = ch["PV"].n
        tok_p3_done_s = ch["PS"].n
        tok_p3_out_d = ch["DSO"].n

    if KDUMP:
        d_q = nc.dram_tensor("d_q", [H, 71, T], BF16, kind="ExternalOutput")
        d_k = nc.dram_tensor("d_k", [H, 71, T], BF16, kind="ExternalOutput")
        d_xb = nc.dram_tensor("d_xb", [T, C], BF16, kind="ExternalOutput")
        d_xy = nc.dram_tensor("d_xy", [128, 16384], BF16, kind="ExternalOutput")
        d_vs = nc.dram_tensor("d_vs", [128, 16 * 1040], BF16, kind="ExternalOutput")
        d_ysh = nc.dram_tensor("d_ysh", [8, 64, T], BF16, kind="ExternalOutput")
        for _cn in ("DSQ", "DSE", "DSY", "DSO", "DYW", "DYQ", "DYE", "DYR",
                    "DYS", "DY3", "PV", "PS", "PT"):
            P.wait("g", _cn, ch[_cn].n)
        P.emit("g", lambda g: g.dma_start(d_q.ap(), Qs.ap()[0]), inc="DG")
        P.emit("g", lambda g: g.dma_start(d_k.ap(), Ks.ap()[0]), inc="DG")
        P.emit("g", lambda g: g.dma_start(d_xb.ap(), xb16.ap()[0]), inc="DG")
        P.emit("g", lambda g: g.dma_start(d_xy.ap(), XY.ap()), inc="DG")
        P.emit("g", lambda g: g.dma_start(d_vs.ap(), VS.ap()), inc="DG")
        P.emit("g", lambda g: g.dma_start(d_ysh.ap(), Ysh.ap()), inc="DG")
        P.q["y"].append(lambda eng, v=ch["DG"].n: eng.wait_ge(sems["DG"], v))

    final_ds = ch["DSO"].n
    P.q["y"].append(lambda eng: eng.wait_ge(sems["DSO"], final_ds))

    with nc.Block() as blk:
        P.run(blk)
    es.close()
    return nc


_cache = {}


def kernel(L0_attn_x, L0_attn_w_attn, L0_attn_w_proj, L0_ffn_W_r, L0_ffn_W_k,
           L0_ffn_W_v, L0_ffn_mu_r, L0_ffn_mu_k):
    x = np.ascontiguousarray(np.asarray(L0_attn_x, dtype=np.float32))
    wa = np.ascontiguousarray(np.asarray(L0_attn_w_attn, dtype=np.float32))
    wp = np.ascontiguousarray(np.asarray(L0_attn_w_proj, dtype=np.float32))
    wr = np.ascontiguousarray(np.asarray(L0_ffn_W_r, dtype=np.float32))
    wk = np.ascontiguousarray(np.asarray(L0_ffn_W_k, dtype=np.float32))
    wv = np.ascontiguousarray(np.asarray(L0_ffn_W_v, dtype=np.float32))
    mur = np.ascontiguousarray(
        np.asarray(L0_ffn_mu_r, dtype=np.float32).reshape(8, 128).T)
    muk = np.ascontiguousarray(
        np.asarray(L0_ffn_mu_k, dtype=np.float32).reshape(8, 128).T)
    qaug, kaug, maskc, identc, biasc, _ = host_consts()

    import os
    _ph = float(os.environ.get("KPHASES", "3"))
    if _ph not in _cache:
        _cache[_ph] = build_nc(_ph)
    nc = _cache[_ph]

    in_maps = []
    for core in range(8):
        in_maps.append({
            "x_in": np.ascontiguousarray(x[2 * core:2 * core + 2]),
            "w_attn": wa, "w_proj": wp, "w_r": wr, "w_k": wk, "w_v": wv,
            "mu_r": mur, "mu_k": muk,
            "qaug": qaug, "kaug": kaug, "maskc": maskc, "identc": identc,
            "biascc": biasc,
        })
    res = run_bass_kernel_spmd(nc, in_maps, core_ids=list(range(8)))
    return np.concatenate([r["y_out"] for r in res.results], axis=0)
